# version 12
# baseline (speedup 1.0000x reference)
import contextlib
import numpy as np
import ml_dtypes
import concourse.bass as bass
import concourse.mybir as mybir

from concourse.bass_utils import run_bass_kernel_spmd

F32 = mybir.dt.float32
BF16 = mybir.dt.bfloat16
AF = mybir.ActivationFunctionType
ALU = mybir.AluOpType

P = 128
D = 1024
NCH = D // P
T = 8192
F = 256
NT = T // F
B = 8
LN_EPS_A = 1e-5
LN_EPS_2 = 1e-3
QB = 128.0
CLIP = QB - 1e-5


def _ap(t_ap, offset, dims):
    return bass.AP(t_ap.tensor, offset, dims)


def build_nc():
    nc = bass.Bass("TRN2", target_bir_lowering=False, detect_race_conditions=False)

    x_hbm = nc.dram_tensor("x", [D, T], F32, kind="ExternalInput").ap()
    wk_hbm = nc.dram_tensor("wkT", [D, D], F32, kind="ExternalInput").ap()
    wv_hbm = nc.dram_tensor("wvT", [D, D], F32, kind="ExternalInput").ap()
    wr_hbm = nc.dram_tensor("wrT", [D, D], F32, kind="ExternalInput").ap()
    wo_hbm = nc.dram_tensor("woT", [D, D], F32, kind="ExternalInput").ap()
    ffnw_hbm = nc.dram_tensor("ffnw", [D, D], F32, kind="ExternalInput").ap()
    wmod_hbm = nc.dram_tensor("wmod", [D, 6 * D], F32, kind="ExternalInput").ap()
    vecs_hbm = nc.dram_tensor("vecs", [P, 64], F32, kind="ExternalInput").ap()
    ident_hbm = nc.dram_tensor("ident", [P, P], F32, kind="ExternalInput").ap()
    out_hbm = nc.dram_tensor("out", [D, T], F32, kind="ExternalOutput").ap()

    xn_hbm = nc.dram_tensor("xn_scr", [D, T], BF16).ap()
    mx_hbm = nc.dram_tensor("mx_scr", [1, P], F32).ap()
    cc_in_t = nc.dram_tensor("cc_in", [1, 64], F32)
    cc_out_t = nc.dram_tensor("cc_out", [1, 64], F32)
    cc_in = cc_in_t.ap().opt()
    cc_out = cc_out_t.ap().opt()

    es = contextlib.ExitStack()
    with es:
        sb = lambda name, shape, dt: es.enter_context(nc.sbuf_tensor(name, shape, dt))
        wk = sb("wk", [P, NCH * D], BF16)
        wv = sb("wv", [P, NCH * D], BF16)
        wr = sb("wr", [P, NCH * D], BF16)
        wo = sb("wo", [P, NCH * D], BF16)
        xB = [sb(f"x{i}", [P, NCH * F], F32) for i in range(2)]
        xbb = sb("xbb", [P, NCH * F], BF16)
        sqb = sb("sqb", [P, NCH * F], BF16)
        sqb2 = sb("sqb2", [P, NCH * F], BF16)
        h = sb("h", [P, NCH * (F + 1)], BF16)
        hd = sb("hd", [P, NCH * F], BF16)
        xk = sb("xk", [P, NCH * F], BF16)
        xv = sb("xv", [P, NCH * F], BF16)
        xr = sb("xr", [P, NCH * F], BF16)
        a = sb("a", [P, NCH * F], F32)
        cbuf = sb("cbuf", [P, NCH * F], F32)
        Nb = sb("Nb", [P, NCH * (F + 1)], F32)
        Eb = sb("Eb", [P, NCH * (F + 1)], F32)
        rb = sb("rb", [P, NCH * F], F32)
        t1 = sb("t1", [P, NCH * F], F32)
        t2 = sb("t2", [P, NCH * F], F32)
        aaB = [sb(f"aa{i}", [P, NCH * F], F32) for i in range(2)]
        xnB = [sb(f"xn{i}", [P, NCH * F], BF16) for i in range(2)]
        mu_b = sb("mu_b", [P, F], F32)
        iv_b = sb("iv_b", [P, F], F32)
        va_b = sb("va_b", [P, F], F32)
        ones_t = sb("ones_t", [P, P], BF16)
        identb = sb("identb", [P, P], BF16)
        mxb = sb("mxb", [P, P], BF16)
        bacc = sb("bacc", [P, 512], F32)
        vecs = sb("vecs_sb", [P, 64], F32)
        scol = sb("scol", [P, 8], BF16)
        lam = sb("lam", [P, 8], F32)
        eu = sb("eu", [P, 8], F32)
        modv = sb("modv", [P, 48], F32)
        Aa = sb("Aa", [P, 8], F32)
        Bb = sb("Bb", [P, 8], F32)
        gav = sb("gav", [P, 8], F32)
        gbv = sb("gbv", [P, 8], F32)
        gbs = sb("gbs", [P, 8], F32)
        cpool = sb("cpool", [P, 2], F32)
        mxrow = sb("mxrow", [1, P], F32)
        gsc = sb("gsc", [1, 64], F32)
        gcol = sb("gcol", [P, 1], F32)
        sgam = sb("sgam", [P, 1], F32)
        bcol = sb("bcol", [P, 1], F32)
        rbcol = sb("rbcol", [P, 1], F32)
        rbcol2 = sb("rbcol2", [P, 1], F32)
        bsum = sb("bsum", [1, 2], F32)
        maxacc = sb("maxacc", [P, NT], F32)
        mxc = sb("mxc", [P, 1], F32)
        tiny = sb("tinyv", [P, 8], F32)
        ps = [es.enter_context(nc.psum_tensor(f"ps{i}", [P, 512], F32)) for i in range(8)]

        nc.const_aps.register(F32, -1.0, cpool[:, 0:1])
        nc.const_aps.register(F32, -0.5, cpool[:, 1:2])

        sems = {}
        for name in ["s_x", "s_w", "s_S", "s_V", "s_T", "s_G", "s_ao", "s_no",
                     "s_cc", "s_fo", "s_Gc"]:
            sems[name] = es.enter_context(nc.semaphore(name))
        cnt = {k: 0 for k in sems}
        block = es.enter_context(nc.Block())

        sched = {k: [] for k in ["sync", "scalar", "vector", "tensor", "gpsimd"]}

        def emit(eng, fn, waits=(), inc=None, incn=1, aux=False):
            waits = list(waits)
            if inc is not None and incn == 16:
                waits.insert(0, (sems[inc], cnt[inc]))
            sched[eng].append((fn, tuple(waits), inc, incn, aux))
            if inc is not None:
                cnt[inc] += incn
            return cnt[inc] if inc else None

        def W(name, n):
            return (sems[name], n)

        def Wc(name):
            return (sems[name], cnt[name])

        def ch(buf, c, width=F):
            return buf[:, c * width:(c + 1) * width]

        def chp(buf, c):
            base = c * (F + 1)
            return buf[:, base:base + F + 1]

        marks = {}

        def full3(buf):
            return _ap(buf[:, 0:1], 0, [[NCH * F, P], [F, NCH], [1, F]])

        emit("gpsimd", lambda g: g.memset(cpool[:, 0:1], -1.0))
        emit("gpsimd", lambda g: g.memset(cpool[:, 1:2], -0.5))
        emit("gpsimd", lambda g: g.memset(ones_t[:], 1.0 / 1024.0))
        emit("gpsimd", lambda g: g.memset(h[:], 0.0))
        emit("gpsimd", lambda g: g.memset(Nb[:], 0.0))
        emit("gpsimd", lambda g: g.memset(Eb[:], 0.0))
        emit("gpsimd", lambda g: g.memset(maxacc[:], 0.0))
        emit("gpsimd", lambda g: g.memset(gsc[:], 0.0), inc="s_Gc")
        m_const = cnt["s_Gc"]
        emit("gpsimd", lambda g: g.dma_start(out=vecs[:], in_=vecs_hbm),
             inc="s_w", incn=16)
        m_vecs = cnt["s_w"]

        emit("scalar", lambda s: s.activation(scol[:], vecs[:, 0:8], AF.Silu),
             waits=[Wc("s_w"), W("s_Gc", m_const)], inc="s_S")
        emit("scalar", lambda s: s.activation(lam[:], vecs[:, 32:40], AF.Exp),
             inc="s_S")
        emit("scalar", lambda s: s.activation(eu[:], vecs[:, 40:48], AF.Exp),
             inc="s_S")
        m_scol = cnt["s_S"]

        def wload(dst, src):
            def f(g):
                return g.dma_start(
                    out=dst[:], in_=_ap(src, 0, [[D, P], [P * D, NCH], [1, D]]))
            return f
        emit("gpsimd", lambda g: g.dma_start(out=identb[:], in_=ident_hbm),
             inc="s_w", incn=16)
        emit("gpsimd", wload(wk, wk_hbm), inc="s_w", incn=16)
        emit("gpsimd", wload(wv, wv_hbm), inc="s_w", incn=16)
        emit("gpsimd", wload(wr, wr_hbm), inc="s_w", incn=16)
        emit("gpsimd", wload(wo, wo_hbm), inc="s_w", incn=16)
        m_wall = cnt["s_w"]

        stageA = (xk, xv, xr)
        stageB = (xbb, sqb, xnB[0])

        def wmod_load1(bufp, c, i):
            def f(g):
                return g.dma_start(
                    out=bufp[:, 0:2 * D],
                    in_=_ap(wmod_hbm, c * P * 6 * D + i * 2 * D,
                            [[6 * D, P], [1, 2 * D]]))
            return f

        def mod_mm(stage, c, j):
            def f(t):
                which = (j * P) // (2 * D)
                off = (j * P) % (2 * D)
                stat = stage[which][:, off:off + P]
                return t.matmul(ps[4][:, j:j + 1], stat, scol[:, c:c + 1],
                                start=(c == 0 and j == 0),
                                stop=(c == NCH - 1 and j == 47),
                                skip_group_check=True)
            return f

        for c in range(NCH):
            stg = stageA if c % 2 == 0 else stageB
            wts = []
            if c >= 2:
                wts = [W("s_T", marks[("modmm", c - 2)])]
            emit("gpsimd", wmod_load1(stg[0], c, 0), waits=wts, inc="s_w", incn=16)
            emit("gpsimd", wmod_load1(stg[1], c, 1), inc="s_w", incn=16)
            emit("gpsimd", wmod_load1(stg[2], c, 2), inc="s_w", incn=16)
            m_wmod_c = cnt["s_w"]
            for j in range(48):
                wts = [W("s_w", m_wmod_c)] if j == 0 else ()
                if c == 0 and j == 0:
                    wts = [W("s_w", m_wmod_c), W("s_S", m_scol)]
                emit("tensor", mod_mm(stg, c, j), waits=wts,
                     inc="s_T" if j == 47 else None)
            marks[("modmm", c)] = cnt["s_T"]
        m_modmm = cnt["s_T"]

        emit("scalar", lambda s: s.activation(modv[:], ps[4][:, 0:48], AF.Copy),
             waits=[W("s_T", m_modmm)], inc="s_S")
        m_modv = cnt["s_S"]
        sa_v = modv[:, 0:8]
        ca_v = modv[:, 8:16]
        ga_v = modv[:, 16:24]
        sbv_v = modv[:, 24:32]
        cb_v = modv[:, 32:40]
        gb_v = modv[:, 40:48]

        GW = 256
        emit("scalar", lambda s: s.activation(t1[:, 0:48], modv[:], AF.Copy),
             waits=[W("s_S", m_modv)], inc="s_S")
        m_gcp = cnt["s_S"]

        def gatewide(v):
            g0 = t1[:, 0:GW]
            g1 = t1[:, GW:2 * GW]
            g2 = t1[:, 2 * GW:3 * GW]
            v.tensor_scalar(g1, g0, 0.0, None, ALU.is_ge)
            v.tensor_scalar(g1, g1, 2.0, -1.0, ALU.mult, ALU.add)
            v.tensor_tensor(g2, g0, g1, ALU.mult)
            v.tensor_scalar(g2, g2, 0.25, None, ALU.max)
            return v.tensor_tensor(g2, g1, g2, ALU.mult)
        emit("vector", gatewide, waits=[W("s_S", m_gcp)], inc="s_V")
        m_gw = cnt["s_V"]
        emit("scalar", lambda s: s.activation(
            gav[:], t1[:, 2 * GW + 16:2 * GW + 24], AF.Copy),
            waits=[W("s_V", m_gw)])
        emit("scalar", lambda s: s.activation(
            gbv[:], t1[:, 2 * GW + 40:2 * GW + 48], AF.Copy), inc="s_S")
        m_gates = cnt["s_S"]
        emit("vector", lambda v: v.tensor_scalar(Aa[:], ca_v, 1.0, None, ALU.add),
             waits=[W("s_S", m_gates)], aux=True)
        emit("vector", lambda v: v.tensor_scalar(Bb[:], cb_v, 1.0, None, ALU.add),
             inc="s_V", aux=True)
        m_mod_ready = cnt["s_V"]

        def xload(ti, extra=()):
            def f(e):
                return e.dma_start(
                    out=xB[ti % 2][:],
                    in_=_ap(x_hbm, ti * F, [[T, P], [P * T, NCH], [1, F]]))
            return f
        emit("sync", xload(0), inc="s_x", incn=16)
        emit("sync", xload(1), inc="s_x", incn=16)

        lamc = [_ap(lam[:, 0:1], c, [[8, P], [0, F]]) for c in range(NCH)]
        x_rel = {}
        aa_st = {}
        xn_st = {}
        st_aa = 0
        st_xn = 0

        def stat_mms(bank, src):
            def f(t):
                last = None
                for c in range(NCH):
                    last = t.matmul(ps[bank][:, 0:F], ones_t[:], ch(src, c),
                                    start=(c == 0), stop=(c == NCH - 1))
                return last
            return f

        def var_chain(eps):
            def f(v):
                v.tensor_tensor(va_b[:], ps[1][:, 0:F], va_b[:], ALU.subtract)
                return v.tensor_scalar(va_b[:], va_b[:], eps, None, ALU.add)
            return f

        mu_full = _ap(mu_b[:, 0:1], 0, [[F, P], [0, NCH], [1, F]])
        iv_full = _ap(iv_b[:, 0:1], 0, [[F, P], [0, NCH], [1, F]])
        assert mu_full is not None

        def proj_mms(wsb, src, e, bank):
            def f(t):
                last = None
                for c in range(NCH):
                    last = t.matmul(ps[bank][:, 0:F],
                                    wsb[:, c * D + e * P:c * D + (e + 1) * P],
                                    ch(src, c), start=(c == 0),
                                    stop=(c == NCH - 1))
                return last
            return f

        for ti in range(NT):
            xb_ = xB[ti % 2]
            aab = aaB[ti % 2]
            xnb = xnB[ti % 2]

            wts = [Wc("s_x")]
            if ti == 0:
                wts.append(W("s_V", m_mod_ready))
            else:
                wts.append(W("s_T", marks[("ssq3", ti - 1)]))
                wts.append(W("s_V", marks[("vend", ti - 1)]))
                wts.append(W("s_S", marks[("send", ti - 1)]))
            emit("scalar", lambda s, xb_=xb_: s.activation(xbb[:], xb_[:], AF.Copy),
                 waits=wts, inc="s_S")
            emit("scalar", lambda s, xb_=xb_: s.activation(sqb[:], xb_[:], AF.Square),
                 inc="s_S")
            m_casts = cnt["s_S"]

            wts = [W("s_S", m_casts)]
            if ti > 0:
                wts.append(W("s_V", marks[("z3", ti - 1)]))
            emit("tensor", stat_mms(0, xbb), waits=wts, inc="s_T")
            m_sx = cnt["s_T"]
            emit("tensor", stat_mms(1, sqb), inc="s_T")
            m_ssq = cnt["s_T"]

            emit("scalar", lambda s: s.activation(mu_b[:], ps[0][:, 0:F], AF.Copy),
                 waits=[W("s_T", m_sx)])
            def musq_f(s):
                s.nop()
                return s.activation(va_b[:], mu_b[:], AF.Square)
            emit("scalar", musq_f, inc="s_S")
            m_mu = cnt["s_S"]
            emit("vector", var_chain(LN_EPS_A),
                 waits=[W("s_S", m_mu), W("s_T", m_ssq)], inc="s_V")
            m_var = cnt["s_V"]
            emit("scalar", lambda s: s.activation(va_b[:], va_b[:], AF.Ln),
                 waits=[W("s_V", m_var)])
            def iv_f(s):
                s.nop()
                return s.activation(iv_b[:], va_b[:], AF.Exp, scale=-0.5)
            emit("scalar", iv_f, inc="s_S")
            m_iv = cnt["s_S"]

            emit("vector", lambda v, xb_=xb_: v.tensor_tensor(
                full3(t1), full3(xb_), mu_full, ALU.subtract),
                waits=[W("s_S", m_mu)])
            emit("vector", lambda v: v.tensor_tensor(
                full3(t1), full3(t1), iv_full, ALU.mult),
                waits=[W("s_S", m_iv)], inc="s_V")
            m_z = cnt["s_V"]

            emit("vector", lambda v: v.tensor_copy(
                _ap(h[:, 0:1], 0, [[NCH * (F + 1), P], [F + 1, NCH], [1, 1]]),
                _ap(h[:, 0:1], F, [[NCH * (F + 1), P], [F + 1, NCH], [1, 1]])),
                inc="s_V")
            m_hcarry = cnt["s_V"]

            def h_aff(c):
                def f(s):
                    s.nop()
                    return s.activation(chp(h, c)[:, 1:F + 1], ch(t1, c),
                                        AF.Identity, scale=Aa[:, c:c + 1],
                                        bias=sa_v[:, c:c + 1])
                return f
            for c in range(NCH):
                emit("scalar", h_aff(c),
                     waits=[W("s_V", m_z), W("s_V", m_hcarry)] if c == 0 else (),
                     inc="s_S" if c == NCH - 1 else None)
            m_h = cnt["s_S"]

            for c in range(NCH):
                emit("vector", lambda v, c=c: v.tensor_tensor(
                    ch(hd, c), chp(h, c)[:, 1:F + 1], chp(h, c)[:, 0:F],
                    ALU.subtract), waits=[W("s_S", m_h)] if c == 0 else ())

            def lerp(dst, tmoff, c):
                def f(v):
                    return v.scalar_tensor_tensor(
                        ch(dst, c), ch(hd, c), vecs[:, tmoff + c:tmoff + c + 1],
                        chp(h, c)[:, 0:F],
                        ALU.mult, ALU.add)
                return f
            for c in range(NCH):
                emit("vector", lerp(xk, 8, c), aux=True,
                     inc="s_V" if c == NCH - 1 else None)
            m_xk = cnt["s_V"]
            for c in range(NCH):
                emit("vector", lerp(xv, 16, c), aux=True,
                     inc="s_V" if c == NCH - 1 else None)
            m_xv = cnt["s_V"]
            for c in range(NCH):
                emit("vector", lerp(xr, 24, c), aux=True,
                     inc="s_V" if c == NCH - 1 else None)
            m_xr = cnt["s_V"]

            for e in range(NCH):
                bank = 2 + (e % 2)
                wts = []
                if e == 0:
                    wts = [W("s_V", m_xk), Wc("s_w")]
                if e >= 2:
                    wts.append(W("s_S", marks[("kev", ti, e - 2)]))
                emit("tensor", proj_mms(wk, xk, e, bank), waits=wts, inc="s_T")
                mme = cnt["s_T"]
                emit("scalar", lambda s, e=e, bank=bank: s.activation(
                    ch(a, e), ps[bank][:, 0:F], AF.Exp),
                    waits=[W("s_T", mme)], inc="s_S")
                marks[("kev", ti, e)] = cnt["s_S"]
            m_kev = marks[("kev", ti, NCH - 1)]
            for e in range(NCH):
                bank = 4 + (e % 2)
                wts = []
                if e == 0:
                    wts = [W("s_V", m_xv)]
                if e >= 2:
                    wts.append(W("s_S", marks[("vev", ti, e - 2)]))
                emit("tensor", proj_mms(wv, xv, e, bank), waits=wts, inc="s_T")
                mme = cnt["s_T"]
                emit("scalar", lambda s, e=e, bank=bank: s.activation(
                    ch(t2, e), ps[bank][:, 0:F], AF.Copy),
                    waits=[W("s_T", mme)] + ([W("s_V", m_xr)] if e == 0 else []),
                    inc="s_S")
                marks[("vev", ti, e)] = cnt["s_S"]
            m_vev = marks[("vev", ti, NCH - 1)]
            for e in range(NCH):
                bank = 6 + (e % 2)
                wts = []
                if e == 0:
                    wts = [W("s_V", m_xr), Wc("s_w")]
                if e >= 2:
                    wts.append(W("s_S", marks[("rev", ti, e - 2)]))
                emit("tensor", proj_mms(wr, xr, e, bank), waits=wts, inc="s_T")
                mme = cnt["s_T"]
                emit("scalar", lambda s, e=e, bank=bank: s.activation(
                    ch(rb, e), ps[bank][:, 0:F], AF.Exp, scale=-1.0),
                    waits=[W("s_T", mme)], inc="s_S")
                marks[("rev", ti, e)] = cnt["s_S"]
            m_rev = cnt["s_S"]

            emit("vector", lambda v: v.tensor_tensor(cbuf[:], a[:], t2[:], ALU.mult),
                 waits=[W("s_S", m_vev)], inc="s_V")
            m_c = cnt["s_V"]

            emit("vector", lambda v: v.tensor_copy(
                _ap(Nb[:, 0:1], 0, [[NCH * (F + 1), P], [F + 1, NCH], [1, 1]]),
                _ap(Nb[:, 0:1], F, [[NCH * (F + 1), P], [F + 1, NCH], [1, 1]])))
            emit("vector", lambda v: v.tensor_copy(
                _ap(Eb[:, 0:1], 0, [[NCH * (F + 1), P], [F + 1, NCH], [1, 1]]),
                _ap(Eb[:, 0:1], F, [[NCH * (F + 1), P], [F + 1, NCH], [1, 1]])))
            for c in range(NCH):
                emit("vector", lambda v, c=c: v.tensor_tensor_scan(
                    chp(Nb, c)[:, 1:F + 1], lamc[c], ch(cbuf, c),
                    chp(Nb, c)[:, 0:1], ALU.mult, ALU.add), aux=True)
            for c in range(NCH):
                emit("vector", lambda v, c=c: v.tensor_tensor_scan(
                    chp(Eb, c)[:, 1:F + 1], lamc[c], ch(a, c),
                    chp(Eb, c)[:, 0:1], ALU.mult, ALU.max),
                    waits=[W("s_S", m_kev)] if c == 0 else (), aux=True)
            for c in range(NCH):
                emit("vector", lambda v, c=c: v.scalar_tensor_tensor(
                    ch(t1, c), ch(cbuf, c), eu[:, c:c + 1],
                    chp(Nb, c)[:, 0:F], ALU.mult, ALU.add), aux=True)
            for c in range(NCH):
                emit("vector", lambda v, c=c: v.scalar_tensor_tensor(
                    ch(t2, c), ch(a, c), eu[:, c:c + 1],
                    chp(Eb, c)[:, 0:F], ALU.mult, ALU.add), aux=True,
                    inc="s_V" if c == NCH - 1 else None)
            m_den = cnt["s_V"]

            emit("vector", lambda v: v.scalar_tensor_tensor(
                t2[:], rb[:], 1.0, t2[:], ALU.add, ALU.mult),
                waits=[W("s_S", m_rev)], inc="s_V", aux=True)
            m_den2 = cnt["s_V"]

            emit("scalar", lambda s: s.activation(t2[:], t2[:], AF.Ln),
                 waits=[W("s_V", m_den2)])
            def rec_f(s):
                s.nop()
                return s.activation(t2[:], t2[:], AF.Exp, scale=-1.0)
            emit("scalar", rec_f, inc="s_S")
            m_rec = cnt["s_S"]

            emit("vector", lambda v: v.tensor_tensor(xk[:], t1[:], t2[:], ALU.mult),
                 waits=[W("s_S", m_rec)], inc="s_V")
            m_y = cnt["s_V"]

            for e in range(NCH):
                bank = 2 + (e % 2)
                wts = []
                if e == 0:
                    wts = [W("s_V", m_y)]
                if e >= 2:
                    wts.append(W("s_S", marks[("oev", ti, e - 2)]))
                emit("tensor", proj_mms(wo, xk, e, bank), waits=wts, inc="s_T")
                mme = cnt["s_T"]
                def oev_f(s, e=e, bank=bank):
                    s.nop()
                    return s.activation(ch(t2, e), ps[bank][:, 0:F], AF.Identity,
                                        scale=gav[:, e:e + 1])
                emit("scalar", oev_f, waits=[W("s_T", mme)], inc="s_S")
                marks[("oev", ti, e)] = cnt["s_S"]
            m_oev = cnt["s_S"]
            wts = [W("s_S", m_oev)]
            if ti >= 2:
                wts.append(Wc("s_ao"))
            emit("vector", lambda v, xb_=xb_, aab=aab: v.tensor_tensor(
                aab[:], xb_[:], t2[:], ALU.add), waits=wts, inc="s_V")
            m_aa = cnt["s_V"]
            x_rel[ti] = m_aa

            emit("sync", lambda e, aab=aab, ti=ti: e.dma_start(
                out=_ap(out_hbm, ti * F, [[T, P], [P * T, NCH], [1, F]]),
                in_=aab[:]), waits=[W("s_V", m_aa)], inc="s_ao", incn=16)
            st_aa += 16
            aa_st[ti] = st_aa

            emit("scalar", lambda s, aab=aab: s.activation(xbb[:], aab[:], AF.Copy),
                 waits=[W("s_V", m_aa)], inc="s_S")
            emit("scalar", lambda s, aab=aab: s.activation(sqb[:], aab[:], AF.Square),
                 inc="s_S")
            m_casts2 = cnt["s_S"]
            emit("tensor", stat_mms(0, xbb), waits=[W("s_S", m_casts2)], inc="s_T")
            m_sx2 = cnt["s_T"]
            emit("tensor", stat_mms(1, sqb), inc="s_T")
            m_ssq2 = cnt["s_T"]
            emit("scalar", lambda s: s.activation(mu_b[:], ps[0][:, 0:F], AF.Copy),
                 waits=[W("s_T", m_sx2)], inc="s_S")
            m_mu2 = cnt["s_S"]
            emit("vector", var_chain(LN_EPS_A),
                 waits=[W("s_S", m_mu2), W("s_T", m_ssq2)], inc="s_V")
            m_var2 = cnt["s_V"]
            emit("scalar", lambda s: s.activation(va_b[:], va_b[:], AF.Ln),
                 waits=[W("s_V", m_var2)])
            def iv_f(s):
                s.nop()
                return s.activation(iv_b[:], va_b[:], AF.Exp, scale=-0.5)
            emit("scalar", iv_f, inc="s_S")
            m_iv2 = cnt["s_S"]
            emit("vector", lambda v, aab=aab: v.tensor_tensor(
                full3(t1), full3(aab), mu_full, ALU.subtract),
                waits=[W("s_S", m_mu2)])
            emit("vector", lambda v: v.tensor_tensor(
                full3(t1), full3(t1), iv_full, ALU.mult),
                waits=[W("s_S", m_iv2)], inc="s_V")
            m_z2 = cnt["s_V"]

            def h2_aff(c):
                def f(s):
                    s.nop()
                    return s.activation(ch(t2, c), ch(t1, c), AF.Identity,
                                        scale=Bb[:, c:c + 1],
                                        bias=sbv_v[:, c:c + 1])
                return f
            for c in range(NCH):
                emit("scalar", h2_aff(c), waits=[W("s_V", m_z2)] if c == 0 else (),
                     inc="s_S" if c == NCH - 1 else None)
            m_h2 = cnt["s_S"]

            emit("scalar", lambda s: s.activation(xbb[:], t2[:], AF.Copy), inc="s_S")
            emit("scalar", lambda s: s.activation(sqb[:], t2[:], AF.Square),
                 inc="s_S")
            m_casts3 = cnt["s_S"]
            emit("tensor", stat_mms(0, xbb), waits=[W("s_S", m_casts3)], inc="s_T")
            m_sx3 = cnt["s_T"]
            emit("tensor", stat_mms(1, sqb), inc="s_T")
            m_ssq3 = cnt["s_T"]
            marks[("ssq3", ti)] = m_ssq3
            emit("scalar", lambda s: s.activation(mu_b[:], ps[0][:, 0:F], AF.Copy),
                 waits=[W("s_T", m_sx3)])
            emit("scalar", musq_f, inc="s_S")
            m_mu3 = cnt["s_S"]
            emit("vector", var_chain(LN_EPS_2),
                 waits=[W("s_S", m_mu3), W("s_T", m_ssq3)], inc="s_V")
            m_var3 = cnt["s_V"]
            emit("scalar", lambda s: s.activation(va_b[:], va_b[:], AF.Ln),
                 waits=[W("s_V", m_var3)])
            def iv_f(s):
                s.nop()
                return s.activation(iv_b[:], va_b[:], AF.Exp, scale=-0.5)
            emit("scalar", iv_f, inc="s_S")
            m_iv3 = cnt["s_S"]
            emit("vector", lambda v: v.tensor_tensor(
                full3(t1), full3(t2), mu_full, ALU.subtract),
                waits=[W("s_S", m_mu3)])
            emit("vector", lambda v: v.tensor_tensor(
                full3(t1), full3(t1), iv_full, ALU.mult),
                waits=[W("s_S", m_iv3)], inc="s_V")
            marks[("z3", ti)] = cnt["s_V"]
            m_z3 = cnt["s_V"]

            def xn_aff(c, xnb):
                def f(s):
                    s.nop()
                    return s.activation(ch(xnb, c), ch(t1, c), AF.Identity,
                                        scale=vecs[:, 48 + c:49 + c],
                                        bias=vecs[:, 56 + c:57 + c])
                return f
            for c in range(NCH):
                wts = []
                if c == 0:
                    wts = [W("s_V", m_z3)]
                    if ti >= 2:
                        wts.append(Wc("s_no"))
                emit("scalar", xn_aff(c, xnb), waits=wts,
                     inc="s_S" if c == NCH - 1 else None)
            m_xn = cnt["s_S"]

            emit("vector", lambda v, xnb=xnb, ti=ti: v.tensor_reduce(
                maxacc[:, ti:ti + 1], xnb[:], mybir.AxisListType.X, ALU.max,
                apply_absolute_value=True), waits=[W("s_S", m_xn)], inc="s_V")
            marks[("mx", ti)] = cnt["s_V"]

            emit("sync", lambda e, xnb=xnb, ti=ti: e.dma_start(
                out=_ap(xn_hbm, ti * F, [[T, P], [P * T, NCH], [1, F]]),
                in_=xnb[:]), waits=[W("s_S", m_xn)], inc="s_no", incn=16)
            st_xn += 16
            xn_st[ti] = st_xn
            marks[("vend", ti)] = cnt["s_V"]
            marks[("send", ti)] = cnt["s_S"]

            if ti + 2 < NT:
                emit("sync", xload(ti + 2), waits=[W("s_V", x_rel[ti])],
                     inc="s_x", incn=16)

        m_mx_red = marks[("mx", NT - 1)]
        emit("vector", lambda v: v.tensor_reduce(
            mxc[:], maxacc[:], mybir.AxisListType.X, ALU.max),
            waits=[W("s_V", m_mx_red)], inc="s_V", aux=True)
        m_mx_final = cnt["s_V"]
        emit("sync", lambda e: e.dma_start(
            out=mx_hbm, in_=mxc[:], allow_slow_non_contiguous=True),
            waits=[W("s_V", m_mx_final), W("s_S", cnt["s_S"]),
                   W("s_T", cnt["s_T"])], inc="s_x", incn=16)
        m_gx1 = cnt["s_x"]
        emit("sync", lambda e: e.dma_start(out=mxrow[:], in_=mx_hbm),
             waits=[W("s_x", m_gx1)], inc="s_x", incn=16)
        m_gx2 = cnt["s_x"]
        emit("vector", lambda v: v.tensor_reduce(
            gsc[0:1, 0:1], mxrow[:], mybir.AxisListType.X, ALU.max),
            waits=[W("s_x", m_gx2)], inc="s_V", aux=True)
        m_gmax = cnt["s_V"]
        emit("gpsimd", lambda g: g.dma_start(out=cc_in, in_=gsc[:]),
             waits=[W("s_V", m_gmax)], inc="s_G", incn=16)
        m_g3 = cnt["s_G"]
        emit("gpsimd", lambda g: g.collective_compute(
            "AllReduce", ALU.max, replica_groups=[list(range(B))],
            ins=[cc_in], outs=[cc_out]), waits=[W("s_G", m_g3)], inc="s_cc")
        m_cc = cnt["s_cc"]
        emit("gpsimd", lambda g: g.dma_start(out=gsc[:], in_=cc_out),
             waits=[W("s_cc", m_cc)], inc="s_G", incn=16)
        m_g4 = cnt["s_G"]
        emit("sync", lambda e: e.dma_start(
            out=gcol[:], in_=_ap(gsc[0:1, 0:1], 0, [[64, 1], [0, P]]),
            allow_slow_non_contiguous=True),
            waits=[W("s_G", m_g4)], inc="s_x", incn=16)
        m_g5 = cnt["s_x"]
        emit("vector", lambda v: v.reciprocal(sgam[:], gcol[:]),
             waits=[W("s_x", m_g5)], aux=True)
        emit("vector", lambda v: v.tensor_scalar(sgam[:], sgam[:], QB, None,
                                                 ALU.mult), inc="s_V", aux=True)
        m_sg = cnt["s_V"]

        wq_parts = (xk, xv, xr, xbb)

        def wq_part(c):
            return wq_parts[c // 2], (c % 2) * D

        m_z3_last = marks[("z3", NT - 1)]
        m_xn_last = m_xn
        for q in range(4):
            wts = ([W("s_V", m_z3_last), W("s_S", m_xn_last),
                    W("s_T", marks[("ssq3", NT - 1)])] if q == 0
                   else [W("s_S", marks[("wabs", q - 1)])])
            emit("gpsimd", lambda g, q=q: g.dma_start(
                out=t1[:], in_=_ap(ffnw_hbm, 2 * q * P * D,
                                   [[D, P], [P * D, 2], [1, D]])),
                waits=wts, inc="s_G", incn=16)
            m_ld = cnt["s_G"]
            swts = [W("s_G", m_ld)]
            if q == 0:
                swts.append(W("s_V", m_mx_final))
            else:
                swts.append(W("s_T", marks[("bmm", q - 1)]))
            emit("scalar", lambda s: s.activation(sqb[:], t1[:], AF.Abs),
                 waits=swts, inc="s_S")
            m_abs = cnt["s_S"]
            marks[("wabs", q)] = m_abs

            def beta_mms(q):
                def f(t):
                    last = None
                    bank = (2, 3, 6, 7)[q]
                    for half in range(4):
                        last = t.matmul(ps[bank][:, 0:512], ones_t[:],
                                        sqb[:, half * 512:(half + 1) * 512],
                                        start=(half == 0), stop=(half == 3))
                    return last
                return f
            emit("tensor", beta_mms(q), waits=[W("s_S", m_abs)], inc="s_T")
            marks[("bmm", q)] = cnt["s_T"]
        m_bmm_all = cnt["s_T"]
        emit("vector", lambda v: v.tensor_copy(bacc[:], ps[2][:, 0:512]),
             waits=[W("s_T", m_bmm_all)])
        for bank in (3, 6, 7):
            emit("vector", lambda v, bank=bank: v.tensor_tensor(
                bacc[:], bacc[:], ps[bank][:, 0:512], ALU.add),
                inc="s_V" if bank == 7 else None)
        marks[("bev", 3)] = cnt["s_V"]
        def betasum_f(s):
            s.nop()
            return s.activation(t2[:, 0:512], bacc[:], AF.Copy,
                                accum_out=t2[:, 1024:1025])
        emit("scalar", betasum_f, waits=[W("s_V", marks[("bev", 3)])], inc="s_S")
        m_bsum_s = cnt["s_S"]
        emit("vector", lambda v: v.tensor_scalar(
            bcol[:], t2[:, 1024:1025], 1.0 / 1024.0, None, ALU.mult),
            waits=[W("s_S", m_bsum_s)], aux=True)
        emit("vector", lambda v: v.tensor_scalar(
            rbcol[:], bcol[:], 1e-5, None, ALU.add), inc="s_V", aux=True)
        m_g6 = cnt["s_V"]
        emit("vector", lambda v: v.reciprocal(rbcol[:], rbcol[:]),
             inc="s_V", aux=True)
        m_rb = cnt["s_V"]

        for q in range(4):
            wts = ([W("s_V", m_rb)] if q == 0
                   else [W("s_V", marks[("wqq", q - 1)])])
            emit("gpsimd", lambda g, q=q: g.dma_start(
                out=t1[:], in_=_ap(ffnw_hbm, 2 * q * P * D,
                                   [[D, P], [P * D, 2], [1, D]])),
                waits=wts, inc="s_G", incn=16)
            m_ld2 = cnt["s_G"]

            def quant(q):
                def f(v):
                    v.nop()
                    v.tensor_scalar(t1[:], t1[:], rbcol[:, 0:1], None, ALU.mult)
                    v.nop()
                    v.tensor_scalar(t2[:], t1[:], 0.5, None, ALU.is_gt)
                    v.nop()
                    v.tensor_scalar(t1[:], t1[:], -0.5, None, ALU.is_lt)
                    v.nop()
                    part, _ = wq_part(2 * q)
                    return v.tensor_tensor(part[:, 0:2 * D], t2[:], t1[:],
                                           ALU.subtract)
                return f
            emit("vector", quant(q), waits=[W("s_G", m_ld2)], inc="s_V")
            marks[("wqq", q)] = cnt["s_V"]
        m_wq = cnt["s_V"]

        emit("vector", lambda v: v.tensor_scalar(
            mxc[:], bcol[:], 1.0 / QB, None, ALU.mult),
            waits=[W("s_V", m_g6), W("s_V", m_sg)], inc="s_V", aux=True)
        m_b128 = cnt["s_V"]

        def gbs1_f(s):
            s.nop()
            return s.activation(rbcol2[:], mxc[:], AF.Identity,
                                scale=gcol[:, 0:1])
        emit("scalar", gbs1_f, waits=[W("s_V", m_b128)], inc="s_S")

        def gbs2_f(s):
            s.nop()
            return s.activation(gbs[:], gbv[:], AF.Identity,
                                scale=rbcol2[:, 0:1])
        emit("scalar", gbs2_f, inc="s_S")
        m_gbs = cnt["s_S"]

        def xnload(ti, wts):
            def f(e):
                return e.dma_start(
                    out=xnB[ti % 2][:],
                    in_=_ap(xn_hbm, ti * F, [[T, P], [P * T, NCH], [1, F]]))
            return f, wts
        f0, w0 = xnload(0, [Wc("s_no"), W("s_V", m_wq)])
        emit("sync", f0, waits=w0, inc="s_x", incn=16)
        base_px = cnt["s_x"]
        f1, w1 = xnload(1, [Wc("s_no"), W("s_V", m_wq)])
        emit("sync", f1, waits=w1, inc="s_x", incn=16)

        for ti in range(NT):
            xnb = xnB[ti % 2]
            term = aaB[ti % 2]
            qb = sqb if ti % 2 == 0 else sqb2
            wts = [Wc("s_x"), W("s_S", m_gbs)]
            if ti >= 2:
                wts.append(W("s_T", marks[("fmm", ti - 2)]))
            emit("vector", lambda v, xnb=xnb: v.tensor_scalar(
                t1[:], xnb[:], sgam[:, 0:1], 8388608.0, ALU.mult, ALU.add),
                waits=wts, aux=True)
            emit("vector", lambda v, qb=qb: v.tensor_scalar(
                qb[:], t1[:], -8388608.0, CLIP, ALU.add, ALU.min))
            emit("vector", lambda v, qb=qb: v.tensor_scalar(
                qb[:], qb[:], -CLIP, None, ALU.max), inc="s_V")
            m_xq = cnt["s_V"]
            marks[("xq", ti)] = m_xq

            def fmm(e, bank):
                def f(t, qb=qb):
                    last = None
                    for c in range(NCH):
                        part, off = wq_part(c)
                        last = t.matmul(ps[bank][:, 0:F],
                                        part[:, off + e * P:off + (e + 1) * P],
                                        ch(qb, c), start=(c == 0),
                                        stop=(c == NCH - 1))
                    return last
                return f
            for e in range(NCH):
                bank = 2 + (e % 2)
                wts = [W("s_V", m_xq)] if e == 0 else []
                if e < 2 and ti >= 1:
                    wts.append(W("s_S", marks[("fev", ti - 1, 6 + e)]))
                if e >= 2:
                    wts.append(W("s_S", marks[("fev", ti, e - 2)]))
                emit("tensor", fmm(e, bank), waits=wts, inc="s_T")
                mme = cnt["s_T"]
                swts = [W("s_T", mme)]
                if ti == 0 and e == 0:
                    swts.append(Wc("s_ao"))
                if ti >= 2 and e == 0:
                    swts.append(Wc("s_fo"))
                def fev_f(s, e=e, bank=bank, term=term):
                    s.nop()
                    return s.activation(ch(term, e), ps[bank][:, 0:F], AF.Identity,
                                        scale=gbs[:, e:e + 1])
                emit("scalar", fev_f, waits=swts, inc="s_S")
                marks[("fev", ti, e)] = cnt["s_S"]
            marks[("fmm", ti)] = cnt["s_T"]
            m_term = cnt["s_S"]

            emit("gpsimd", lambda g, term=term, ti=ti: g.dma_start(
                out=_ap(out_hbm, ti * F, [[T, P], [P * T, NCH], [1, F]]),
                in_=term[:], accum_op=ALU.add),
                waits=[W("s_S", m_term)], inc="s_fo", incn=16)
            marks[("acc", ti)] = cnt["s_fo"]

            if ti + 2 < NT:
                fN, wN = xnload(ti + 2, [Wc("s_no"),
                                         W("s_V", marks[("xq", ti)])])
                emit("sync", fN, waits=wN, inc="s_x", incn=16)

        emit("gpsimd", lambda g: g.wait_ge(sems["s_fo"], cnt["s_fo"]))

        def replay(eng_name):
            def runner(eng):
                for fn, waits, inc, incn, aux in sched[eng_name]:
                    for (sem, val) in waits:
                        eng.wait_ge(sem, val)
                    if aux:
                        eng.nop()
                    instr = fn(eng)
                    if inc is not None:
                        assert instr is not None, \
                            f"{eng_name} op returned None with inc"
                        instr.then_inc(sems[inc], incn)
            return runner

        block.sync(replay("sync"))
        block.scalar(replay("scalar"))
        block.vector(replay("vector"))
        block.tensor(replay("tensor"))
        block.gpsimd(replay("gpsimd"))

    return nc


def kernel(**inputs):
    x = np.ascontiguousarray(inputs["L0_attn_x"])
    wkT = np.ascontiguousarray(inputs["L0_attn_W_k"].T)
    wvT = np.ascontiguousarray(inputs["L0_attn_W_v"].T)
    wrT = np.ascontiguousarray(inputs["L0_attn_W_r"].T)
    woT = np.ascontiguousarray(inputs["L0_attn_W_o"].T)
    ffnw = np.ascontiguousarray(inputs["L0_ffn_weight"])
    wmod = np.ascontiguousarray(inputs["L0_wmod"])
    crow = inputs["L0_c"]

    def vec8(v):
        return np.ascontiguousarray(np.reshape(np.asarray(v), (NCH, P)).T)

    nc = build_nc()
    in_maps = []
    for b in range(B):
        vecs = np.zeros((P, 64), np.float32)
        vecs[:, 0:8] = vec8(crow[b])
        vecs[:, 8:16] = vec8(inputs["L0_attn_time_mix_k"])
        vecs[:, 16:24] = vec8(inputs["L0_attn_time_mix_v"])
        vecs[:, 24:32] = vec8(inputs["L0_attn_time_mix_r"])
        vecs[:, 32:40] = vec8(inputs["L0_attn_time_decay"])
        vecs[:, 40:48] = vec8(inputs["L0_attn_time_first"])
        vecs[:, 48:56] = vec8(inputs["L0_ffn_ln_gamma"])
        vecs[:, 56:64] = vec8(inputs["L0_ffn_ln_beta"])
        in_maps.append({
            "x": np.ascontiguousarray(x[b].T),
            "wkT": wkT, "wvT": wvT, "wrT": wrT, "woT": woT,
            "ffnw": ffnw, "wmod": wmod, "vecs": vecs,
            "ident": np.eye(P, dtype=np.float32),
        })
    res = run_bass_kernel_spmd(nc, in_maps, core_ids=list(range(B)))
    out = np.stack([np.ascontiguousarray(r["out"].T) for r in res.results])
    return out


# version 13
# speedup vs baseline: 1.0150x; 1.0150x over previous
import contextlib
import numpy as np
import ml_dtypes
import concourse.bass as bass
import concourse.mybir as mybir

from concourse.bass_utils import run_bass_kernel_spmd

F32 = mybir.dt.float32
BF16 = mybir.dt.bfloat16
AF = mybir.ActivationFunctionType
ALU = mybir.AluOpType

P = 128
D = 1024
NCH = D // P
T = 8192
F = 256
NT = T // F
B = 8
LN_EPS_A = 1e-5
LN_EPS_2 = 1e-3
QB = 128.0
CLIP = QB - 1e-5


def _ap(t_ap, offset, dims):
    return bass.AP(t_ap.tensor, offset, dims)


def build_nc():
    nc = bass.Bass("TRN2", target_bir_lowering=False, detect_race_conditions=False)

    x_hbm = nc.dram_tensor("x", [D, T], F32, kind="ExternalInput").ap()
    wk_hbm = nc.dram_tensor("wkT", [D, D], F32, kind="ExternalInput").ap()
    wv_hbm = nc.dram_tensor("wvT", [D, D], F32, kind="ExternalInput").ap()
    wr_hbm = nc.dram_tensor("wrT", [D, D], F32, kind="ExternalInput").ap()
    wo_hbm = nc.dram_tensor("woT", [D, D], F32, kind="ExternalInput").ap()
    ffnw_hbm = nc.dram_tensor("ffnw", [D, D], F32, kind="ExternalInput").ap()
    wmod_hbm = nc.dram_tensor("wmod", [D, 6 * D], F32, kind="ExternalInput").ap()
    vecs_hbm = nc.dram_tensor("vecs", [P, 64], F32, kind="ExternalInput").ap()
    ident_hbm = nc.dram_tensor("ident", [P, P], F32, kind="ExternalInput").ap()
    out_hbm = nc.dram_tensor("out", [D, T], F32, kind="ExternalOutput").ap()

    xn_hbm = nc.dram_tensor("xn_scr", [D, T], BF16).ap()
    mx_hbm = nc.dram_tensor("mx_scr", [1, P], F32).ap()
    cc_in_t = nc.dram_tensor("cc_in", [1, 64], F32)
    cc_out_t = nc.dram_tensor("cc_out", [1, 64], F32)
    cc_in = cc_in_t.ap().opt()
    cc_out = cc_out_t.ap().opt()

    es = contextlib.ExitStack()
    with es:
        sb = lambda name, shape, dt: es.enter_context(nc.sbuf_tensor(name, shape, dt))
        wk = sb("wk", [P, NCH * D], BF16)
        wv = sb("wv", [P, NCH * D], BF16)
        wr = sb("wr", [P, NCH * D], BF16)
        wo = sb("wo", [P, NCH * D], BF16)
        xB = [sb(f"x{i}", [P, NCH * F], F32) for i in range(2)]
        xbb = sb("xbb", [P, NCH * F], BF16)
        sqb = sb("sqb", [P, NCH * F], BF16)
        sqb2 = sb("sqb2", [P, NCH * F], BF16)
        h = sb("h", [P, NCH * (F + 1)], BF16)
        hd = sb("hd", [P, NCH * F], BF16)
        xk = sb("xk", [P, NCH * F], BF16)
        xv = sb("xv", [P, NCH * F], BF16)
        xr = sb("xr", [P, NCH * F], BF16)
        a = sb("a", [P, NCH * F], F32)
        cbuf = sb("cbuf", [P, NCH * F], F32)
        Nb = sb("Nb", [P, NCH * (F + 1)], F32)
        Eb = sb("Eb", [P, NCH * (F + 1)], F32)
        rb = sb("rb", [P, NCH * F], F32)
        t1 = sb("t1", [P, NCH * F], F32)
        t2 = sb("t2", [P, NCH * F], F32)
        aaB = [sb(f"aa{i}", [P, NCH * F], F32) for i in range(2)]
        xnB = [sb(f"xn{i}", [P, NCH * F], BF16) for i in range(2)]
        mu_b = sb("mu_b", [P, F], F32)
        iv_b = sb("iv_b", [P, F], F32)
        va_b = sb("va_b", [P, F], F32)
        ones_t = sb("ones_t", [P, P], BF16)
        identb = sb("identb", [P, P], BF16)
        mxb = sb("mxb", [P, P], BF16)
        bacc = sb("bacc", [P, 512], F32)
        vecs = sb("vecs_sb", [P, 64], F32)
        scol = sb("scol", [P, 8], BF16)
        lam = sb("lam", [P, 8], F32)
        eu = sb("eu", [P, 8], F32)
        modv = sb("modv", [P, 48], F32)
        Aa = sb("Aa", [P, 8], F32)
        Bb = sb("Bb", [P, 8], F32)
        gav = sb("gav", [P, 8], F32)
        gbv = sb("gbv", [P, 8], F32)
        gbs = sb("gbs", [P, 8], F32)
        cpool = sb("cpool", [P, 2], F32)
        mxrow = sb("mxrow", [1, P], F32)
        gsc = sb("gsc", [1, 64], F32)
        gcol = sb("gcol", [P, 1], F32)
        sgam = sb("sgam", [P, 1], F32)
        bcol = sb("bcol", [P, 1], F32)
        rbcol = sb("rbcol", [P, 1], F32)
        rbcol2 = sb("rbcol2", [P, 1], F32)
        bsum = sb("bsum", [1, 2], F32)
        maxacc = sb("maxacc", [P, NT], F32)
        mxc = sb("mxc", [P, 1], F32)
        tiny = sb("tinyv", [P, 8], F32)
        ps = [es.enter_context(nc.psum_tensor(f"ps{i}", [P, 512], F32)) for i in range(8)]

        nc.const_aps.register(F32, -1.0, cpool[:, 0:1])
        nc.const_aps.register(F32, -0.5, cpool[:, 1:2])

        sems = {}
        for name in ["s_x", "s_w", "s_S", "s_V", "s_T", "s_G", "s_ao", "s_no",
                     "s_cc", "s_fo", "s_Gc"]:
            sems[name] = es.enter_context(nc.semaphore(name))
        cnt = {k: 0 for k in sems}
        block = es.enter_context(nc.Block())

        sched = {k: [] for k in ["sync", "scalar", "vector", "tensor", "gpsimd"]}

        def emit(eng, fn, waits=(), inc=None, incn=1, aux=False):
            waits = list(waits)
            if inc is not None and incn == 16:
                waits.insert(0, (sems[inc], cnt[inc]))
            sched[eng].append((fn, tuple(waits), inc, incn, aux))
            if inc is not None:
                cnt[inc] += incn
            return cnt[inc] if inc else None

        def W(name, n):
            return (sems[name], n)

        def Wc(name):
            return (sems[name], cnt[name])

        def ch(buf, c, width=F):
            return buf[:, c * width:(c + 1) * width]

        def chp(buf, c):
            base = c * (F + 1)
            return buf[:, base:base + F + 1]

        marks = {}

        def full3(buf):
            return _ap(buf[:, 0:1], 0, [[NCH * F, P], [F, NCH], [1, F]])

        emit("gpsimd", lambda g: g.memset(cpool[:, 0:1], -1.0))
        emit("gpsimd", lambda g: g.memset(cpool[:, 1:2], -0.5))
        emit("gpsimd", lambda g: g.memset(ones_t[:], 1.0 / 1024.0))
        emit("gpsimd", lambda g: g.memset(h[:], 0.0))
        emit("gpsimd", lambda g: g.memset(Nb[:], 0.0))
        emit("gpsimd", lambda g: g.memset(Eb[:], 0.0))
        emit("gpsimd", lambda g: g.memset(maxacc[:], 0.0))
        emit("gpsimd", lambda g: g.memset(gsc[:], 0.0), inc="s_Gc")
        m_const = cnt["s_Gc"]
        emit("gpsimd", lambda g: g.dma_start(out=vecs[:], in_=vecs_hbm),
             inc="s_w", incn=16)
        m_vecs = cnt["s_w"]

        emit("scalar", lambda s: s.activation(scol[:], vecs[:, 0:8], AF.Silu),
             waits=[Wc("s_w"), W("s_Gc", m_const)], inc="s_S")
        emit("scalar", lambda s: s.activation(lam[:], vecs[:, 32:40], AF.Exp),
             inc="s_S")
        emit("scalar", lambda s: s.activation(eu[:], vecs[:, 40:48], AF.Exp),
             inc="s_S")
        m_scol = cnt["s_S"]

        def wload(dst, src):
            def f(g):
                return g.dma_start(
                    out=dst[:], in_=_ap(src, 0, [[D, P], [P * D, NCH], [1, D]]))
            return f
        emit("gpsimd", lambda g: g.dma_start(out=identb[:], in_=ident_hbm),
             inc="s_w", incn=16)
        emit("gpsimd", wload(wk, wk_hbm), inc="s_w", incn=16)
        emit("gpsimd", wload(wv, wv_hbm), inc="s_w", incn=16)
        emit("gpsimd", wload(wr, wr_hbm), inc="s_w", incn=16)
        emit("gpsimd", wload(wo, wo_hbm), inc="s_w", incn=16)
        m_wall = cnt["s_w"]

        stageA = (xk, xv, xr)
        stageB = (xbb, sqb, xnB[0])

        def wmod_load1(bufp, c, i):
            def f(g):
                return g.dma_start(
                    out=bufp[:, 0:2 * D],
                    in_=_ap(wmod_hbm, c * P * 6 * D + i * 2 * D,
                            [[6 * D, P], [1, 2 * D]]))
            return f

        def mod_mm(stage, c, j):
            def f(t):
                which = (j * P) // (2 * D)
                off = (j * P) % (2 * D)
                stat = stage[which][:, off:off + P]
                return t.matmul(ps[4][:, j:j + 1], stat, scol[:, c:c + 1],
                                start=(c == 0 and j == 0),
                                stop=(c == NCH - 1 and j == 47),
                                skip_group_check=True)
            return f

        for c in range(NCH):
            stg = stageA if c % 2 == 0 else stageB
            wts = []
            if c >= 2:
                wts = [W("s_T", marks[("modmm", c - 2)])]
            emit("gpsimd", wmod_load1(stg[0], c, 0), waits=wts, inc="s_w", incn=16)
            emit("gpsimd", wmod_load1(stg[1], c, 1), inc="s_w", incn=16)
            emit("gpsimd", wmod_load1(stg[2], c, 2), inc="s_w", incn=16)
            m_wmod_c = cnt["s_w"]
            for j in range(48):
                wts = [W("s_w", m_wmod_c)] if j == 0 else ()
                if c == 0 and j == 0:
                    wts = [W("s_w", m_wmod_c), W("s_S", m_scol)]
                emit("tensor", mod_mm(stg, c, j), waits=wts,
                     inc="s_T" if j == 47 else None)
            marks[("modmm", c)] = cnt["s_T"]
        m_modmm = cnt["s_T"]

        emit("scalar", lambda s: s.activation(modv[:], ps[4][:, 0:48], AF.Copy),
             waits=[W("s_T", m_modmm)], inc="s_S")
        m_modv = cnt["s_S"]
        sa_v = modv[:, 0:8]
        ca_v = modv[:, 8:16]
        ga_v = modv[:, 16:24]
        sbv_v = modv[:, 24:32]
        cb_v = modv[:, 32:40]
        gb_v = modv[:, 40:48]

        GW = 256
        emit("scalar", lambda s: s.activation(t1[:, 0:48], modv[:], AF.Copy),
             waits=[W("s_S", m_modv)], inc="s_S")
        m_gcp = cnt["s_S"]

        def gatewide(v):
            g0 = t1[:, 0:GW]
            g1 = t1[:, GW:2 * GW]
            g2 = t1[:, 2 * GW:3 * GW]
            v.tensor_scalar(g1, g0, 0.0, None, ALU.is_ge)
            v.tensor_scalar(g1, g1, 2.0, -1.0, ALU.mult, ALU.add)
            v.tensor_tensor(g2, g0, g1, ALU.mult)
            v.tensor_scalar(g2, g2, 0.25, None, ALU.max)
            return v.tensor_tensor(g2, g1, g2, ALU.mult)
        emit("vector", gatewide, waits=[W("s_S", m_gcp)], inc="s_V")
        m_gw = cnt["s_V"]
        emit("scalar", lambda s: s.activation(
            gav[:], t1[:, 2 * GW + 16:2 * GW + 24], AF.Copy),
            waits=[W("s_V", m_gw)])
        emit("scalar", lambda s: s.activation(
            gbv[:], t1[:, 2 * GW + 40:2 * GW + 48], AF.Copy), inc="s_S")
        m_gates = cnt["s_S"]
        emit("vector", lambda v: v.tensor_scalar(Aa[:], ca_v, 1.0, None, ALU.add),
             waits=[W("s_S", m_gates)], aux=True)
        emit("vector", lambda v: v.tensor_scalar(Bb[:], cb_v, 1.0, None, ALU.add),
             inc="s_V", aux=True)
        m_mod_ready = cnt["s_V"]

        def xload(ti, extra=()):
            def f(e):
                return e.dma_start(
                    out=xB[ti % 2][:],
                    in_=_ap(x_hbm, ti * F, [[T, P], [P * T, NCH], [1, F]]))
            return f
        emit("sync", xload(0), inc="s_x", incn=16)
        emit("sync", xload(1), inc="s_x", incn=16)

        lamc = [_ap(lam[:, 0:1], c, [[8, P], [0, F]]) for c in range(NCH)]
        x_rel = {}
        aa_st = {}
        xn_st = {}
        st_aa = 0
        st_xn = 0

        def stat_mms(bank, src):
            def f(t):
                last = None
                for c in range(NCH):
                    last = t.matmul(ps[bank][:, 0:F], ones_t[:], ch(src, c),
                                    start=(c == 0), stop=(c == NCH - 1))
                return last
            return f

        def var_chain(eps):
            def f(v):
                v.tensor_tensor(va_b[:], ps[1][:, 0:F], va_b[:], ALU.subtract)
                return v.tensor_scalar(va_b[:], va_b[:], eps, None, ALU.add)
            return f

        mu_full = _ap(mu_b[:, 0:1], 0, [[F, P], [0, NCH], [1, F]])
        iv_full = _ap(iv_b[:, 0:1], 0, [[F, P], [0, NCH], [1, F]])
        assert mu_full is not None

        def proj_mms(wsb, src, e, bank):
            def f(t):
                last = None
                for c in range(NCH):
                    last = t.matmul(ps[bank][:, 0:F],
                                    wsb[:, c * D + e * P:c * D + (e + 1) * P],
                                    ch(src, c), start=(c == 0),
                                    stop=(c == NCH - 1))
                return last
            return f

        def proj_pair(wsb, src, p, bank):
            def f(t):
                last = None
                for half in range(2):
                    e = 2 * p + half
                    for c in range(NCH):
                        last = t.matmul(ps[bank][:, half * F:(half + 1) * F],
                                        wsb[:, c * D + e * P:c * D + (e + 1) * P],
                                        ch(src, c), start=(c == 0),
                                        stop=(c == NCH - 1))
                return last
            return f

        def pr2(buf, p):
            return buf[:, 2 * p * F:(2 * p + 2) * F]

        for ti in range(NT):
            xb_ = xB[ti % 2]
            aab = aaB[ti % 2]
            xnb = xnB[ti % 2]

            wts = [Wc("s_x")]
            if ti == 0:
                wts.append(W("s_V", m_mod_ready))
            else:
                wts.append(W("s_T", marks[("ssq3", ti - 1)]))
                wts.append(W("s_V", marks[("vend", ti - 1)]))
                wts.append(W("s_S", marks[("send", ti - 1)]))
            emit("scalar", lambda s, xb_=xb_: s.activation(xbb[:], xb_[:], AF.Copy),
                 waits=wts, inc="s_S")
            emit("scalar", lambda s, xb_=xb_: s.activation(sqb[:], xb_[:], AF.Square),
                 inc="s_S")
            m_casts = cnt["s_S"]

            wts = [W("s_S", m_casts)]
            if ti > 0:
                wts.append(W("s_V", marks[("z3", ti - 1)]))
            emit("tensor", stat_mms(0, xbb), waits=wts, inc="s_T")
            m_sx = cnt["s_T"]
            emit("tensor", stat_mms(1, sqb), inc="s_T")
            m_ssq = cnt["s_T"]

            emit("scalar", lambda s: s.activation(mu_b[:], ps[0][:, 0:F], AF.Copy),
                 waits=[W("s_T", m_sx)])
            def musq_f(s):
                s.nop()
                return s.activation(va_b[:], mu_b[:], AF.Square)
            emit("scalar", musq_f, inc="s_S")
            m_mu = cnt["s_S"]
            emit("vector", var_chain(LN_EPS_A),
                 waits=[W("s_S", m_mu), W("s_T", m_ssq)], inc="s_V")
            m_var = cnt["s_V"]
            emit("scalar", lambda s: s.activation(va_b[:], va_b[:], AF.Ln),
                 waits=[W("s_V", m_var)])
            def iv_f(s):
                s.nop()
                return s.activation(iv_b[:], va_b[:], AF.Exp, scale=-0.5)
            emit("scalar", iv_f, inc="s_S")
            m_iv = cnt["s_S"]

            emit("vector", lambda v, xb_=xb_: v.tensor_tensor(
                full3(t1), full3(xb_), mu_full, ALU.subtract),
                waits=[W("s_S", m_mu)])
            emit("vector", lambda v: v.tensor_tensor(
                full3(t1), full3(t1), iv_full, ALU.mult),
                waits=[W("s_S", m_iv)], inc="s_V")
            m_z = cnt["s_V"]

            emit("vector", lambda v: v.tensor_copy(
                _ap(h[:, 0:1], 0, [[NCH * (F + 1), P], [F + 1, NCH], [1, 1]]),
                _ap(h[:, 0:1], F, [[NCH * (F + 1), P], [F + 1, NCH], [1, 1]])),
                inc="s_V")
            m_hcarry = cnt["s_V"]

            def h_aff(c):
                def f(s):
                    s.nop()
                    return s.activation(chp(h, c)[:, 1:F + 1], ch(t1, c),
                                        AF.Identity, scale=Aa[:, c:c + 1],
                                        bias=sa_v[:, c:c + 1])
                return f
            for c in range(NCH):
                emit("scalar", h_aff(c),
                     waits=[W("s_V", m_z), W("s_V", m_hcarry)] if c == 0 else (),
                     inc="s_S" if c == NCH - 1 else None)
            m_h = cnt["s_S"]

            for c in range(NCH):
                emit("vector", lambda v, c=c: v.tensor_tensor(
                    ch(hd, c), chp(h, c)[:, 1:F + 1], chp(h, c)[:, 0:F],
                    ALU.subtract), waits=[W("s_S", m_h)] if c == 0 else ())

            def lerp(dst, tmoff, c):
                def f(v):
                    return v.scalar_tensor_tensor(
                        ch(dst, c), ch(hd, c), vecs[:, tmoff + c:tmoff + c + 1],
                        chp(h, c)[:, 0:F],
                        ALU.mult, ALU.add)
                return f
            for c in range(NCH):
                emit("vector", lerp(xk, 8, c), aux=True,
                     inc="s_V" if c == NCH - 1 else None)
            m_xk = cnt["s_V"]
            for c in range(NCH):
                emit("vector", lerp(xv, 16, c), aux=True,
                     inc="s_V" if c == NCH - 1 else None)
            m_xv = cnt["s_V"]
            for c in range(NCH):
                emit("vector", lerp(xr, 24, c), aux=True,
                     inc="s_V" if c == NCH - 1 else None)
            m_xr = cnt["s_V"]

            for p in range(4):
                bank = 2 + (p % 2)
                wts = []
                if p == 0:
                    wts = [W("s_V", m_xk), Wc("s_w")]
                if p >= 2:
                    wts.append(W("s_S", marks[("kev", ti, p - 2)]))
                emit("tensor", proj_pair(wk, xk, p, bank), waits=wts, inc="s_T")
                mme = cnt["s_T"]
                emit("scalar", lambda s, p=p, bank=bank: s.activation(
                    pr2(a, p), ps[bank][:], AF.Exp),
                    waits=[W("s_T", mme)], inc="s_S")
                marks[("kev", ti, p)] = cnt["s_S"]
            m_kev = marks[("kev", ti, 3)]
            for p in range(4):
                bank = 4 + (p % 2)
                wts = []
                if p == 0:
                    wts = [W("s_V", m_xv)]
                if p >= 2:
                    wts.append(W("s_S", marks[("vev", ti, p - 2)]))
                emit("tensor", proj_pair(wv, xv, p, bank), waits=wts, inc="s_T")
                mme = cnt["s_T"]
                emit("scalar", lambda s, p=p, bank=bank: s.activation(
                    pr2(t2, p), ps[bank][:], AF.Copy),
                    waits=[W("s_T", mme)] + ([W("s_V", m_xr)] if p == 0 else []),
                    inc="s_S")
                marks[("vev", ti, p)] = cnt["s_S"]
            m_vev = marks[("vev", ti, 3)]
            for p in range(4):
                bank = 6 + (p % 2)
                wts = []
                if p == 0:
                    wts = [W("s_V", m_xr), Wc("s_w")]
                if p >= 2:
                    wts.append(W("s_S", marks[("rev", ti, p - 2)]))
                emit("tensor", proj_pair(wr, xr, p, bank), waits=wts, inc="s_T")
                mme = cnt["s_T"]
                emit("scalar", lambda s, p=p, bank=bank: s.activation(
                    pr2(rb, p), ps[bank][:], AF.Exp, scale=-1.0),
                    waits=[W("s_T", mme)], inc="s_S")
                marks[("rev", ti, p)] = cnt["s_S"]
            m_rev = cnt["s_S"]

            emit("vector", lambda v: v.tensor_tensor(cbuf[:], a[:], t2[:], ALU.mult),
                 waits=[W("s_S", m_vev)], inc="s_V")
            m_c = cnt["s_V"]

            emit("vector", lambda v: v.tensor_copy(
                _ap(Nb[:, 0:1], 0, [[NCH * (F + 1), P], [F + 1, NCH], [1, 1]]),
                _ap(Nb[:, 0:1], F, [[NCH * (F + 1), P], [F + 1, NCH], [1, 1]])))
            emit("vector", lambda v: v.tensor_copy(
                _ap(Eb[:, 0:1], 0, [[NCH * (F + 1), P], [F + 1, NCH], [1, 1]]),
                _ap(Eb[:, 0:1], F, [[NCH * (F + 1), P], [F + 1, NCH], [1, 1]])))
            for c in range(NCH):
                emit("vector", lambda v, c=c: v.tensor_tensor_scan(
                    chp(Nb, c)[:, 1:F + 1], lamc[c], ch(cbuf, c),
                    chp(Nb, c)[:, 0:1], ALU.mult, ALU.add), aux=True)
            for c in range(NCH):
                emit("vector", lambda v, c=c: v.tensor_tensor_scan(
                    chp(Eb, c)[:, 1:F + 1], lamc[c], ch(a, c),
                    chp(Eb, c)[:, 0:1], ALU.mult, ALU.max),
                    waits=[W("s_S", m_kev)] if c == 0 else (), aux=True)
            for c in range(NCH):
                emit("vector", lambda v, c=c: v.scalar_tensor_tensor(
                    ch(t1, c), ch(cbuf, c), eu[:, c:c + 1],
                    chp(Nb, c)[:, 0:F], ALU.mult, ALU.add), aux=True)
            for c in range(NCH):
                emit("vector", lambda v, c=c: v.scalar_tensor_tensor(
                    ch(t2, c), ch(a, c), eu[:, c:c + 1],
                    chp(Eb, c)[:, 0:F], ALU.mult, ALU.add), aux=True,
                    inc="s_V" if c == NCH - 1 else None)
            m_den = cnt["s_V"]

            emit("vector", lambda v: v.scalar_tensor_tensor(
                t2[:], rb[:], 1.0, t2[:], ALU.add, ALU.mult),
                waits=[W("s_S", m_rev)], inc="s_V", aux=True)
            m_den2 = cnt["s_V"]

            emit("scalar", lambda s: s.activation(t2[:], t2[:], AF.Ln),
                 waits=[W("s_V", m_den2)])
            def rec_f(s):
                s.nop()
                return s.activation(t2[:], t2[:], AF.Exp, scale=-1.0)
            emit("scalar", rec_f, inc="s_S")
            m_rec = cnt["s_S"]

            emit("vector", lambda v: v.tensor_tensor(xk[:], t1[:], t2[:], ALU.mult),
                 waits=[W("s_S", m_rec)], inc="s_V")
            m_y = cnt["s_V"]

            for p in range(4):
                bank = 2 + (p % 2)
                wts = []
                if p == 0:
                    wts = [W("s_V", m_y)]
                if p >= 2:
                    wts.append(W("s_S", marks[("oev", ti, p - 2)]))
                emit("tensor", proj_pair(wo, xk, p, bank), waits=wts, inc="s_T")
                mme = cnt["s_T"]
                emit("scalar", lambda s, p=p, bank=bank: s.activation(
                    pr2(t2, p), ps[bank][:], AF.Copy),
                    waits=[W("s_T", mme)], inc="s_S")
                marks[("oev", ti, p)] = cnt["s_S"]
            m_oev = cnt["s_S"]
            for c in range(NCH):
                wts = []
                if c == 0:
                    wts = [W("s_S", m_oev)]
                    if ti >= 2:
                        wts.append(Wc("s_ao"))
                emit("vector", lambda v, c=c, xb_=xb_, aab=aab:
                     v.scalar_tensor_tensor(
                         ch(aab, c), ch(t2, c), gav[:, c:c + 1], ch(xb_, c),
                         ALU.mult, ALU.add), waits=wts, aux=True,
                     inc="s_V" if c == NCH - 1 else None)
            m_aa = cnt["s_V"]
            x_rel[ti] = m_aa

            emit("sync", lambda e, aab=aab, ti=ti: e.dma_start(
                out=_ap(out_hbm, ti * F, [[T, P], [P * T, NCH], [1, F]]),
                in_=aab[:]), waits=[W("s_V", m_aa)], inc="s_ao", incn=16)
            st_aa += 16
            aa_st[ti] = st_aa

            emit("scalar", lambda s, aab=aab: s.activation(xbb[:], aab[:], AF.Copy),
                 waits=[W("s_V", m_aa)], inc="s_S")
            emit("scalar", lambda s, aab=aab: s.activation(sqb[:], aab[:], AF.Square),
                 inc="s_S")
            m_casts2 = cnt["s_S"]
            emit("tensor", stat_mms(0, xbb), waits=[W("s_S", m_casts2)], inc="s_T")
            m_sx2 = cnt["s_T"]
            emit("tensor", stat_mms(1, sqb), inc="s_T")
            m_ssq2 = cnt["s_T"]
            emit("scalar", lambda s: s.activation(mu_b[:], ps[0][:, 0:F], AF.Copy),
                 waits=[W("s_T", m_sx2)], inc="s_S")
            m_mu2 = cnt["s_S"]
            emit("vector", var_chain(LN_EPS_A),
                 waits=[W("s_S", m_mu2), W("s_T", m_ssq2)], inc="s_V")
            m_var2 = cnt["s_V"]
            emit("scalar", lambda s: s.activation(va_b[:], va_b[:], AF.Ln),
                 waits=[W("s_V", m_var2)])
            def iv_f(s):
                s.nop()
                return s.activation(iv_b[:], va_b[:], AF.Exp, scale=-0.5)
            emit("scalar", iv_f, inc="s_S")
            m_iv2 = cnt["s_S"]
            emit("vector", lambda v, aab=aab: v.tensor_tensor(
                full3(t1), full3(aab), mu_full, ALU.subtract),
                waits=[W("s_S", m_mu2)])
            emit("vector", lambda v: v.tensor_tensor(
                full3(t1), full3(t1), iv_full, ALU.mult),
                waits=[W("s_S", m_iv2)], inc="s_V")
            m_z2 = cnt["s_V"]

            def h2_aff(c):
                def f(s):
                    s.nop()
                    return s.activation(ch(t2, c), ch(t1, c), AF.Identity,
                                        scale=Bb[:, c:c + 1],
                                        bias=sbv_v[:, c:c + 1])
                return f
            for c in range(NCH):
                emit("scalar", h2_aff(c), waits=[W("s_V", m_z2)] if c == 0 else (),
                     inc="s_S" if c == NCH - 1 else None)
            m_h2 = cnt["s_S"]

            emit("scalar", lambda s: s.activation(xbb[:], t2[:], AF.Copy), inc="s_S")
            emit("scalar", lambda s: s.activation(sqb[:], t2[:], AF.Square),
                 inc="s_S")
            m_casts3 = cnt["s_S"]
            emit("tensor", stat_mms(0, xbb), waits=[W("s_S", m_casts3)], inc="s_T")
            m_sx3 = cnt["s_T"]
            emit("tensor", stat_mms(1, sqb), inc="s_T")
            m_ssq3 = cnt["s_T"]
            marks[("ssq3", ti)] = m_ssq3
            emit("scalar", lambda s: s.activation(mu_b[:], ps[0][:, 0:F], AF.Copy),
                 waits=[W("s_T", m_sx3)])
            emit("scalar", musq_f, inc="s_S")
            m_mu3 = cnt["s_S"]
            emit("vector", var_chain(LN_EPS_2),
                 waits=[W("s_S", m_mu3), W("s_T", m_ssq3)], inc="s_V")
            m_var3 = cnt["s_V"]
            emit("scalar", lambda s: s.activation(va_b[:], va_b[:], AF.Ln),
                 waits=[W("s_V", m_var3)])
            def iv_f(s):
                s.nop()
                return s.activation(iv_b[:], va_b[:], AF.Exp, scale=-0.5)
            emit("scalar", iv_f, inc="s_S")
            m_iv3 = cnt["s_S"]
            emit("vector", lambda v: v.tensor_tensor(
                full3(t1), full3(t2), mu_full, ALU.subtract),
                waits=[W("s_S", m_mu3)])
            emit("vector", lambda v: v.tensor_tensor(
                full3(t1), full3(t1), iv_full, ALU.mult),
                waits=[W("s_S", m_iv3)], inc="s_V")
            marks[("z3", ti)] = cnt["s_V"]
            m_z3 = cnt["s_V"]

            def xn_aff(c, xnb):
                def f(s):
                    s.nop()
                    return s.activation(ch(xnb, c), ch(t1, c), AF.Identity,
                                        scale=vecs[:, 48 + c:49 + c],
                                        bias=vecs[:, 56 + c:57 + c])
                return f
            for c in range(NCH):
                wts = []
                if c == 0:
                    wts = [W("s_V", m_z3)]
                    if ti >= 2:
                        wts.append(Wc("s_no"))
                emit("scalar", xn_aff(c, xnb), waits=wts,
                     inc="s_S" if c == NCH - 1 else None)
            m_xn = cnt["s_S"]

            emit("vector", lambda v, xnb=xnb, ti=ti: v.tensor_reduce(
                maxacc[:, ti:ti + 1], xnb[:], mybir.AxisListType.X, ALU.max,
                apply_absolute_value=True), waits=[W("s_S", m_xn)], inc="s_V")
            marks[("mx", ti)] = cnt["s_V"]

            emit("sync", lambda e, xnb=xnb, ti=ti: e.dma_start(
                out=_ap(xn_hbm, ti * F, [[T, P], [P * T, NCH], [1, F]]),
                in_=xnb[:]), waits=[W("s_S", m_xn)], inc="s_no", incn=16)
            st_xn += 16
            xn_st[ti] = st_xn
            marks[("vend", ti)] = cnt["s_V"]
            marks[("send", ti)] = cnt["s_S"]

            if ti + 2 < NT:
                emit("sync", xload(ti + 2), waits=[W("s_V", x_rel[ti])],
                     inc="s_x", incn=16)

        m_mx_red = marks[("mx", NT - 1)]
        emit("vector", lambda v: v.tensor_reduce(
            mxc[:], maxacc[:], mybir.AxisListType.X, ALU.max),
            waits=[W("s_V", m_mx_red)], inc="s_V", aux=True)
        m_mx_final = cnt["s_V"]
        emit("sync", lambda e: e.dma_start(
            out=mx_hbm, in_=mxc[:], allow_slow_non_contiguous=True),
            waits=[W("s_V", m_mx_final), W("s_S", cnt["s_S"]),
                   W("s_T", cnt["s_T"])], inc="s_x", incn=16)
        m_gx1 = cnt["s_x"]
        emit("sync", lambda e: e.dma_start(out=mxrow[:], in_=mx_hbm),
             waits=[W("s_x", m_gx1)], inc="s_x", incn=16)
        m_gx2 = cnt["s_x"]
        emit("vector", lambda v: v.tensor_reduce(
            gsc[0:1, 0:1], mxrow[:], mybir.AxisListType.X, ALU.max),
            waits=[W("s_x", m_gx2)], inc="s_V", aux=True)
        m_gmax = cnt["s_V"]
        emit("gpsimd", lambda g: g.dma_start(out=cc_in, in_=gsc[:]),
             waits=[W("s_V", m_gmax)], inc="s_G", incn=16)
        m_g3 = cnt["s_G"]
        emit("gpsimd", lambda g: g.collective_compute(
            "AllReduce", ALU.max, replica_groups=[list(range(B))],
            ins=[cc_in], outs=[cc_out]), waits=[W("s_G", m_g3)], inc="s_cc")
        m_cc = cnt["s_cc"]
        emit("gpsimd", lambda g: g.dma_start(out=gsc[:], in_=cc_out),
             waits=[W("s_cc", m_cc)], inc="s_G", incn=16)
        m_g4 = cnt["s_G"]
        emit("sync", lambda e: e.dma_start(
            out=gcol[:], in_=_ap(gsc[0:1, 0:1], 0, [[64, 1], [0, P]]),
            allow_slow_non_contiguous=True),
            waits=[W("s_G", m_g4)], inc="s_x", incn=16)
        m_g5 = cnt["s_x"]
        emit("vector", lambda v: v.reciprocal(sgam[:], gcol[:]),
             waits=[W("s_x", m_g5)], aux=True)
        emit("vector", lambda v: v.tensor_scalar(sgam[:], sgam[:], QB, None,
                                                 ALU.mult), inc="s_V", aux=True)
        m_sg = cnt["s_V"]

        wq_parts = (xk, xv, xr, xbb)

        def wq_part(c):
            return wq_parts[c // 2], (c % 2) * D

        m_z3_last = marks[("z3", NT - 1)]
        m_xn_last = m_xn
        for q in range(4):
            wts = ([W("s_V", m_z3_last), W("s_S", m_xn_last),
                    W("s_T", marks[("ssq3", NT - 1)])] if q == 0
                   else [W("s_S", marks[("wabs", q - 1)])])
            emit("gpsimd", lambda g, q=q: g.dma_start(
                out=t1[:], in_=_ap(ffnw_hbm, 2 * q * P * D,
                                   [[D, P], [P * D, 2], [1, D]])),
                waits=wts, inc="s_G", incn=16)
            m_ld = cnt["s_G"]
            swts = [W("s_G", m_ld)]
            if q == 0:
                swts.append(W("s_V", m_mx_final))
            else:
                swts.append(W("s_T", marks[("bmm", q - 1)]))
            emit("scalar", lambda s: s.activation(sqb[:], t1[:], AF.Abs),
                 waits=swts, inc="s_S")
            m_abs = cnt["s_S"]
            marks[("wabs", q)] = m_abs

            def beta_mms(q):
                def f(t):
                    last = None
                    bank = (2, 3, 6, 7)[q]
                    for half in range(4):
                        last = t.matmul(ps[bank][:, 0:512], ones_t[:],
                                        sqb[:, half * 512:(half + 1) * 512],
                                        start=(half == 0), stop=(half == 3))
                    return last
                return f
            emit("tensor", beta_mms(q), waits=[W("s_S", m_abs)], inc="s_T")
            marks[("bmm", q)] = cnt["s_T"]
        m_bmm_all = cnt["s_T"]
        emit("vector", lambda v: v.tensor_copy(bacc[:], ps[2][:, 0:512]),
             waits=[W("s_T", m_bmm_all)])
        for bank in (3, 6, 7):
            emit("vector", lambda v, bank=bank: v.tensor_tensor(
                bacc[:], bacc[:], ps[bank][:, 0:512], ALU.add),
                inc="s_V" if bank == 7 else None)
        marks[("bev", 3)] = cnt["s_V"]
        def betasum_f(s):
            s.nop()
            return s.activation(t2[:, 0:512], bacc[:], AF.Copy,
                                accum_out=t2[:, 1024:1025])
        emit("scalar", betasum_f, waits=[W("s_V", marks[("bev", 3)])], inc="s_S")
        m_bsum_s = cnt["s_S"]
        emit("vector", lambda v: v.tensor_scalar(
            bcol[:], t2[:, 1024:1025], 1.0 / 1024.0, None, ALU.mult),
            waits=[W("s_S", m_bsum_s)], aux=True)
        emit("vector", lambda v: v.tensor_scalar(
            rbcol[:], bcol[:], 1e-5, None, ALU.add), inc="s_V", aux=True)
        m_g6 = cnt["s_V"]
        emit("vector", lambda v: v.reciprocal(rbcol[:], rbcol[:]),
             inc="s_V", aux=True)
        m_rb = cnt["s_V"]

        for q in range(4):
            wts = ([W("s_V", m_rb)] if q == 0
                   else [W("s_V", marks[("wqq", q - 1)])])
            emit("gpsimd", lambda g, q=q: g.dma_start(
                out=t1[:], in_=_ap(ffnw_hbm, 2 * q * P * D,
                                   [[D, P], [P * D, 2], [1, D]])),
                waits=wts, inc="s_G", incn=16)
            m_ld2 = cnt["s_G"]

            def quant(q):
                def f(v):
                    v.nop()
                    v.tensor_scalar(t1[:], t1[:], rbcol[:, 0:1], None, ALU.mult)
                    v.nop()
                    v.tensor_scalar(t2[:], t1[:], 0.5, None, ALU.is_gt)
                    v.nop()
                    v.tensor_scalar(t1[:], t1[:], -0.5, None, ALU.is_lt)
                    v.nop()
                    part, _ = wq_part(2 * q)
                    return v.tensor_tensor(part[:, 0:2 * D], t2[:], t1[:],
                                           ALU.subtract)
                return f
            emit("vector", quant(q), waits=[W("s_G", m_ld2)], inc="s_V")
            marks[("wqq", q)] = cnt["s_V"]
        m_wq = cnt["s_V"]

        emit("vector", lambda v: v.tensor_scalar(
            mxc[:], bcol[:], 1.0 / QB, None, ALU.mult),
            waits=[W("s_V", m_g6), W("s_V", m_sg)], inc="s_V", aux=True)
        m_b128 = cnt["s_V"]

        def gbs1_f(s):
            s.nop()
            return s.activation(rbcol2[:], mxc[:], AF.Identity,
                                scale=gcol[:, 0:1])
        emit("scalar", gbs1_f, waits=[W("s_V", m_b128)], inc="s_S")

        def gbs2_f(s):
            s.nop()
            return s.activation(gbs[:], gbv[:], AF.Identity,
                                scale=rbcol2[:, 0:1])
        emit("scalar", gbs2_f, inc="s_S")
        m_gbs = cnt["s_S"]

        def xnload(ti, wts):
            def f(e):
                return e.dma_start(
                    out=xnB[ti % 2][:],
                    in_=_ap(xn_hbm, ti * F, [[T, P], [P * T, NCH], [1, F]]))
            return f, wts
        f0, w0 = xnload(0, [Wc("s_no"), W("s_V", m_wq)])
        emit("sync", f0, waits=w0, inc="s_x", incn=16)
        base_px = cnt["s_x"]
        f1, w1 = xnload(1, [Wc("s_no"), W("s_V", m_wq)])
        emit("sync", f1, waits=w1, inc="s_x", incn=16)

        for ti in range(NT):
            xnb = xnB[ti % 2]
            term = aaB[ti % 2]
            qb = sqb if ti % 2 == 0 else sqb2
            wts = [Wc("s_x"), W("s_S", m_gbs)]
            if ti >= 2:
                wts.append(W("s_T", marks[("fmm", ti - 2)]))
            emit("vector", lambda v, xnb=xnb: v.tensor_scalar(
                t1[:], xnb[:], sgam[:, 0:1], 8388608.0, ALU.mult, ALU.add),
                waits=wts, aux=True)
            emit("vector", lambda v, qb=qb: v.tensor_scalar(
                qb[:], t1[:], -8388608.0, CLIP, ALU.add, ALU.min))
            emit("vector", lambda v, qb=qb: v.tensor_scalar(
                qb[:], qb[:], -CLIP, None, ALU.max), inc="s_V")
            m_xq = cnt["s_V"]
            marks[("xq", ti)] = m_xq

            def fmm(e, bank):
                def f(t, qb=qb):
                    last = None
                    for c in range(NCH):
                        part, off = wq_part(c)
                        last = t.matmul(ps[bank][:, 0:F],
                                        part[:, off + e * P:off + (e + 1) * P],
                                        ch(qb, c), start=(c == 0),
                                        stop=(c == NCH - 1))
                    return last
                return f
            for e in range(NCH):
                bank = 2 + (e % 2)
                wts = [W("s_V", m_xq)] if e == 0 else []
                if e < 2 and ti >= 1:
                    wts.append(W("s_S", marks[("fev", ti - 1, 6 + e)]))
                if e >= 2:
                    wts.append(W("s_S", marks[("fev", ti, e - 2)]))
                emit("tensor", fmm(e, bank), waits=wts, inc="s_T")
                mme = cnt["s_T"]
                swts = [W("s_T", mme)]
                if ti == 0 and e == 0:
                    swts.append(Wc("s_ao"))
                if ti >= 2 and e == 0:
                    swts.append(Wc("s_fo"))
                def fev_f(s, e=e, bank=bank, term=term):
                    s.nop()
                    return s.activation(ch(term, e), ps[bank][:, 0:F], AF.Identity,
                                        scale=gbs[:, e:e + 1])
                emit("scalar", fev_f, waits=swts, inc="s_S")
                marks[("fev", ti, e)] = cnt["s_S"]
            marks[("fmm", ti)] = cnt["s_T"]
            m_term = cnt["s_S"]

            emit("gpsimd", lambda g, term=term, ti=ti: g.dma_start(
                out=_ap(out_hbm, ti * F, [[T, P], [P * T, NCH], [1, F]]),
                in_=term[:], accum_op=ALU.add),
                waits=[W("s_S", m_term)], inc="s_fo", incn=16)
            marks[("acc", ti)] = cnt["s_fo"]

            if ti + 2 < NT:
                fN, wN = xnload(ti + 2, [Wc("s_no"),
                                         W("s_V", marks[("xq", ti)])])
                emit("sync", fN, waits=wN, inc="s_x", incn=16)

        emit("gpsimd", lambda g: g.wait_ge(sems["s_fo"], cnt["s_fo"]))

        def replay(eng_name):
            def runner(eng):
                for fn, waits, inc, incn, aux in sched[eng_name]:
                    for (sem, val) in waits:
                        eng.wait_ge(sem, val)
                    if aux:
                        eng.nop()
                    instr = fn(eng)
                    if inc is not None:
                        assert instr is not None, \
                            f"{eng_name} op returned None with inc"
                        instr.then_inc(sems[inc], incn)
            return runner

        block.sync(replay("sync"))
        block.scalar(replay("scalar"))
        block.vector(replay("vector"))
        block.tensor(replay("tensor"))
        block.gpsimd(replay("gpsimd"))

    return nc


def kernel(**inputs):
    x = np.ascontiguousarray(inputs["L0_attn_x"])
    wkT = np.ascontiguousarray(inputs["L0_attn_W_k"].T)
    wvT = np.ascontiguousarray(inputs["L0_attn_W_v"].T)
    wrT = np.ascontiguousarray(inputs["L0_attn_W_r"].T)
    woT = np.ascontiguousarray(inputs["L0_attn_W_o"].T)
    ffnw = np.ascontiguousarray(inputs["L0_ffn_weight"])
    wmod = np.ascontiguousarray(inputs["L0_wmod"])
    crow = inputs["L0_c"]

    def vec8(v):
        return np.ascontiguousarray(np.reshape(np.asarray(v), (NCH, P)).T)

    nc = build_nc()
    in_maps = []
    for b in range(B):
        vecs = np.zeros((P, 64), np.float32)
        vecs[:, 0:8] = vec8(crow[b])
        vecs[:, 8:16] = vec8(inputs["L0_attn_time_mix_k"])
        vecs[:, 16:24] = vec8(inputs["L0_attn_time_mix_v"])
        vecs[:, 24:32] = vec8(inputs["L0_attn_time_mix_r"])
        vecs[:, 32:40] = vec8(inputs["L0_attn_time_decay"])
        vecs[:, 40:48] = vec8(inputs["L0_attn_time_first"])
        vecs[:, 48:56] = vec8(inputs["L0_ffn_ln_gamma"])
        vecs[:, 56:64] = vec8(inputs["L0_ffn_ln_beta"])
        in_maps.append({
            "x": np.ascontiguousarray(x[b].T),
            "wkT": wkT, "wvT": wvT, "wrT": wrT, "woT": woT,
            "ffnw": ffnw, "wmod": wmod, "vecs": vecs,
            "ident": np.eye(P, dtype=np.float32),
        })
    res = run_bass_kernel_spmd(nc, in_maps, core_ids=list(range(B)))
    out = np.stack([np.ascontiguousarray(r["out"].T) for r in res.results])
    return out


# version 14
# speedup vs baseline: 1.0187x; 1.0037x over previous
import contextlib
import numpy as np
import ml_dtypes
import concourse.bass as bass
import concourse.mybir as mybir

from concourse.bass_utils import run_bass_kernel_spmd

F32 = mybir.dt.float32
BF16 = mybir.dt.bfloat16
AF = mybir.ActivationFunctionType
ALU = mybir.AluOpType

P = 128
D = 1024
NCH = D // P
T = 8192
F = 256
NT = T // F
B = 8
LN_EPS_A = 1e-5
LN_EPS_2 = 1e-3
QB = 128.0
CLIP = QB - 1e-5


def _ap(t_ap, offset, dims):
    return bass.AP(t_ap.tensor, offset, dims)


def build_nc():
    nc = bass.Bass("TRN2", target_bir_lowering=False, detect_race_conditions=False)

    x_hbm = nc.dram_tensor("x", [D, T], F32, kind="ExternalInput").ap()
    wk_hbm = nc.dram_tensor("wkT", [D, D], F32, kind="ExternalInput").ap()
    wv_hbm = nc.dram_tensor("wvT", [D, D], F32, kind="ExternalInput").ap()
    wr_hbm = nc.dram_tensor("wrT", [D, D], F32, kind="ExternalInput").ap()
    wo_hbm = nc.dram_tensor("woT", [D, D], F32, kind="ExternalInput").ap()
    ffnw_hbm = nc.dram_tensor("ffnw", [D, D], F32, kind="ExternalInput").ap()
    wmod_hbm = nc.dram_tensor("wmod", [D, 6 * D], F32, kind="ExternalInput").ap()
    vecs_hbm = nc.dram_tensor("vecs", [P, 64], F32, kind="ExternalInput").ap()
    ident_hbm = nc.dram_tensor("ident", [P, P], F32, kind="ExternalInput").ap()
    out_hbm = nc.dram_tensor("out", [D, T], F32, kind="ExternalOutput").ap()

    xn_hbm = nc.dram_tensor("xn_scr", [D, T], BF16).ap()
    mx_hbm = nc.dram_tensor("mx_scr", [1, P], F32).ap()
    cc_in_t = nc.dram_tensor("cc_in", [1, 64], F32)
    cc_out_t = nc.dram_tensor("cc_out", [1, 64], F32)
    cc_in = cc_in_t.ap().opt()
    cc_out = cc_out_t.ap().opt()

    es = contextlib.ExitStack()
    with es:
        sb = lambda name, shape, dt: es.enter_context(nc.sbuf_tensor(name, shape, dt))
        wk = sb("wk", [P, NCH * D], BF16)
        wv = sb("wv", [P, NCH * D], BF16)
        wr = sb("wr", [P, NCH * D], BF16)
        wo = sb("wo", [P, NCH * D], BF16)
        xB = [sb(f"x{i}", [P, NCH * F], F32) for i in range(2)]
        xbb = sb("xbb", [P, NCH * F], BF16)
        sqb = sb("sqb", [P, NCH * F], BF16)
        sqb2 = sb("sqb2", [P, NCH * F], BF16)
        h = sb("h", [P, NCH * (F + 1)], BF16)
        hd = sb("hd", [P, NCH * F], BF16)
        xk = sb("xk", [P, NCH * F], BF16)
        xv = sb("xv", [P, NCH * F], BF16)
        xr = sb("xr", [P, NCH * F], BF16)
        a = sb("a", [P, NCH * F], F32)
        cbuf = sb("cbuf", [P, NCH * F], F32)
        Nb = sb("Nb", [P, NCH * (F + 1)], F32)
        Eb = sb("Eb", [P, NCH * (F + 1)], F32)
        rb = sb("rb", [P, NCH * F], F32)
        t1 = sb("t1", [P, NCH * F], F32)
        t2 = sb("t2", [P, NCH * F], F32)
        aaB = [sb(f"aa{i}", [P, NCH * F], F32) for i in range(2)]
        xnB = [sb(f"xn{i}", [P, NCH * F], BF16) for i in range(2)]
        mu_b = sb("mu_b", [P, F], F32)
        iv_b = sb("iv_b", [P, F], F32)
        va_b = sb("va_b", [P, F], F32)
        ones_t = sb("ones_t", [P, P], BF16)
        identb = sb("identb", [P, P], BF16)
        mxb = sb("mxb", [P, P], BF16)
        bacc = sb("bacc", [P, 512], F32)
        vecs = sb("vecs_sb", [P, 64], F32)
        scol = sb("scol", [P, 8], BF16)
        lam = sb("lam", [P, 8], F32)
        eu = sb("eu", [P, 8], F32)
        modv = sb("modv", [P, 48], F32)
        Aa = sb("Aa", [P, 8], F32)
        Bb = sb("Bb", [P, 8], F32)
        gav = sb("gav", [P, 8], F32)
        gbv = sb("gbv", [P, 8], F32)
        gbs = sb("gbs", [P, 8], F32)
        cpool = sb("cpool", [P, 2], F32)
        mxrow = sb("mxrow", [1, P], F32)
        gsc = sb("gsc", [1, 64], F32)
        gcol = sb("gcol", [P, 1], F32)
        sgam = sb("sgam", [P, 1], F32)
        bcol = sb("bcol", [P, 1], F32)
        rbcol = sb("rbcol", [P, 1], F32)
        rbcol2 = sb("rbcol2", [P, 1], F32)
        bsum = sb("bsum", [1, 2], F32)
        maxacc = sb("maxacc", [P, NT], F32)
        mxc = sb("mxc", [P, 1], F32)
        tiny = sb("tinyv", [P, 8], F32)
        ps = [es.enter_context(nc.psum_tensor(f"ps{i}", [P, 512], F32)) for i in range(8)]

        nc.const_aps.register(F32, -1.0, cpool[:, 0:1])
        nc.const_aps.register(F32, -0.5, cpool[:, 1:2])

        sems = {}
        for name in ["s_x", "s_w", "s_S", "s_V", "s_T", "s_G", "s_ao", "s_no",
                     "s_cc", "s_fo", "s_Gc"]:
            sems[name] = es.enter_context(nc.semaphore(name))
        cnt = {k: 0 for k in sems}
        block = es.enter_context(nc.Block())

        sched = {k: [] for k in ["sync", "scalar", "vector", "tensor", "gpsimd"]}

        def emit(eng, fn, waits=(), inc=None, incn=1, aux=False):
            waits = list(waits)
            if inc is not None and incn == 16:
                waits.insert(0, (sems[inc], cnt[inc]))
            sched[eng].append((fn, tuple(waits), inc, incn, aux))
            if inc is not None:
                cnt[inc] += incn
            return cnt[inc] if inc else None

        def W(name, n):
            return (sems[name], n)

        def Wc(name):
            return (sems[name], cnt[name])

        def ch(buf, c, width=F):
            return buf[:, c * width:(c + 1) * width]

        def chp(buf, c):
            base = c * (F + 1)
            return buf[:, base:base + F + 1]

        marks = {}

        def full3(buf):
            return _ap(buf[:, 0:1], 0, [[NCH * F, P], [F, NCH], [1, F]])

        emit("gpsimd", lambda g: g.memset(cpool[:, 0:1], -1.0))
        emit("gpsimd", lambda g: g.memset(cpool[:, 1:2], -0.5))
        emit("gpsimd", lambda g: g.memset(ones_t[:], 1.0 / 1024.0))
        emit("gpsimd", lambda g: g.memset(h[:], 0.0))
        emit("gpsimd", lambda g: g.memset(Nb[:], 0.0))
        emit("gpsimd", lambda g: g.memset(Eb[:], 0.0))
        emit("gpsimd", lambda g: g.memset(maxacc[:], 0.0))
        emit("gpsimd", lambda g: g.memset(gsc[:], 0.0), inc="s_Gc")
        m_const = cnt["s_Gc"]
        emit("gpsimd", lambda g: g.dma_start(out=vecs[:], in_=vecs_hbm),
             inc="s_w", incn=16)
        m_vecs = cnt["s_w"]

        emit("scalar", lambda s: s.activation(scol[:], vecs[:, 0:8], AF.Silu),
             waits=[Wc("s_w"), W("s_Gc", m_const)], inc="s_S")
        emit("scalar", lambda s: s.activation(lam[:], vecs[:, 32:40], AF.Exp),
             inc="s_S")
        emit("scalar", lambda s: s.activation(eu[:], vecs[:, 40:48], AF.Exp),
             inc="s_S")
        m_scol = cnt["s_S"]

        def wload(dst, src):
            def f(g):
                return g.dma_start(
                    out=dst[:], in_=_ap(src, 0, [[D, P], [P * D, NCH], [1, D]]))
            return f
        emit("gpsimd", lambda g: g.dma_start(out=identb[:], in_=ident_hbm),
             inc="s_w", incn=16)
        emit("gpsimd", wload(wk, wk_hbm), inc="s_w", incn=16)
        emit("gpsimd", wload(wv, wv_hbm), inc="s_w", incn=16)
        emit("gpsimd", wload(wr, wr_hbm), inc="s_w", incn=16)
        emit("gpsimd", wload(wo, wo_hbm), inc="s_w", incn=16)
        m_wall = cnt["s_w"]

        stageA = (xk, xv, xr)
        stageB = (xbb, sqb, xnB[0])

        def wmod_load1(bufp, c, i):
            def f(g):
                return g.dma_start(
                    out=bufp[:, 0:2 * D],
                    in_=_ap(wmod_hbm, c * P * 6 * D + i * 2 * D,
                            [[6 * D, P], [1, 2 * D]]))
            return f

        def mod_mm(stage, c, j):
            def f(t):
                which = (j * P) // (2 * D)
                off = (j * P) % (2 * D)
                stat = stage[which][:, off:off + P]
                return t.matmul(ps[4][:, j:j + 1], stat, scol[:, c:c + 1],
                                start=(c == 0 and j == 0),
                                stop=(c == NCH - 1 and j == 47),
                                skip_group_check=True)
            return f

        for c in range(NCH):
            stg = stageA if c % 2 == 0 else stageB
            wts = []
            if c >= 2:
                wts = [W("s_T", marks[("modmm", c - 2)])]
            emit("gpsimd", wmod_load1(stg[0], c, 0), waits=wts, inc="s_w", incn=16)
            emit("gpsimd", wmod_load1(stg[1], c, 1), inc="s_w", incn=16)
            emit("gpsimd", wmod_load1(stg[2], c, 2), inc="s_w", incn=16)
            m_wmod_c = cnt["s_w"]
            for j in range(48):
                wts = [W("s_w", m_wmod_c)] if j == 0 else ()
                if c == 0 and j == 0:
                    wts = [W("s_w", m_wmod_c), W("s_S", m_scol)]
                emit("tensor", mod_mm(stg, c, j), waits=wts,
                     inc="s_T" if j == 47 else None)
            marks[("modmm", c)] = cnt["s_T"]
        m_modmm = cnt["s_T"]

        emit("scalar", lambda s: s.activation(modv[:], ps[4][:, 0:48], AF.Copy),
             waits=[W("s_T", m_modmm)], inc="s_S")
        m_modv = cnt["s_S"]
        sa_v = modv[:, 0:8]
        ca_v = modv[:, 8:16]
        ga_v = modv[:, 16:24]
        sbv_v = modv[:, 24:32]
        cb_v = modv[:, 32:40]
        gb_v = modv[:, 40:48]

        GW = 256
        emit("scalar", lambda s: s.activation(t1[:, 0:48], modv[:], AF.Copy),
             waits=[W("s_S", m_modv)], inc="s_S")
        m_gcp = cnt["s_S"]

        def gatewide(v):
            g0 = t1[:, 0:GW]
            g1 = t1[:, GW:2 * GW]
            g2 = t1[:, 2 * GW:3 * GW]
            v.tensor_scalar(g1, g0, 0.0, None, ALU.is_ge)
            v.tensor_scalar(g1, g1, 2.0, -1.0, ALU.mult, ALU.add)
            v.tensor_tensor(g2, g0, g1, ALU.mult)
            v.tensor_scalar(g2, g2, 0.25, None, ALU.max)
            return v.tensor_tensor(g2, g1, g2, ALU.mult)
        emit("vector", gatewide, waits=[W("s_S", m_gcp)], inc="s_V")
        m_gw = cnt["s_V"]
        emit("scalar", lambda s: s.activation(
            gav[:], t1[:, 2 * GW + 16:2 * GW + 24], AF.Copy),
            waits=[W("s_V", m_gw)])
        emit("scalar", lambda s: s.activation(
            gbv[:], t1[:, 2 * GW + 40:2 * GW + 48], AF.Copy), inc="s_S")
        m_gates = cnt["s_S"]
        emit("vector", lambda v: v.tensor_scalar(Aa[:], ca_v, 1.0, None, ALU.add),
             waits=[W("s_S", m_gates)], aux=True)
        emit("vector", lambda v: v.tensor_scalar(Bb[:], cb_v, 1.0, None, ALU.add),
             inc="s_V", aux=True)
        m_mod_ready = cnt["s_V"]

        def xload(ti, extra=()):
            def f(e):
                return e.dma_start(
                    out=xB[ti % 2][:],
                    in_=_ap(x_hbm, ti * F, [[T, P], [P * T, NCH], [1, F]]))
            return f
        emit("sync", xload(0), inc="s_x", incn=16)
        emit("sync", xload(1), inc="s_x", incn=16)

        lamc = [_ap(lam[:, 0:1], c, [[8, P], [0, F]]) for c in range(NCH)]
        x_rel = {}
        aa_st = {}
        xn_st = {}
        st_aa = 0
        st_xn = 0

        def stat_mms(bank, src):
            def f(t):
                last = None
                for c in range(NCH):
                    last = t.matmul(ps[bank][:, 0:F], ones_t[:], ch(src, c),
                                    start=(c == 0), stop=(c == NCH - 1))
                return last
            return f

        def var_chain(eps):
            def f(v):
                v.tensor_tensor(va_b[:], ps[1][:, 0:F], va_b[:], ALU.subtract)
                return v.tensor_scalar(va_b[:], va_b[:], eps, None, ALU.add)
            return f

        mu_full = _ap(mu_b[:, 0:1], 0, [[F, P], [0, NCH], [1, F]])
        iv_full = _ap(iv_b[:, 0:1], 0, [[F, P], [0, NCH], [1, F]])
        assert mu_full is not None

        def proj_mms(wsb, src, e, bank):
            def f(t):
                last = None
                for c in range(NCH):
                    last = t.matmul(ps[bank][:, 0:F],
                                    wsb[:, c * D + e * P:c * D + (e + 1) * P],
                                    ch(src, c), start=(c == 0),
                                    stop=(c == NCH - 1))
                return last
            return f

        def proj_pair(wsb, src, p, bank):
            def f(t):
                last = None
                for half in range(2):
                    e = 2 * p + half
                    for c in range(NCH):
                        last = t.matmul(ps[bank][:, half * F:(half + 1) * F],
                                        wsb[:, c * D + e * P:c * D + (e + 1) * P],
                                        ch(src, c), start=(c == 0),
                                        stop=(c == NCH - 1))
                return last
            return f

        def pr2(buf, p):
            return buf[:, 2 * p * F:(2 * p + 2) * F]

        for ti in range(NT):
            xb_ = xB[ti % 2]
            aab = aaB[ti % 2]
            xnb = xnB[ti % 2]

            wts = [Wc("s_x")]
            if ti == 0:
                wts.append(W("s_V", m_mod_ready))
            else:
                wts.append(W("s_T", marks[("ssq3", ti - 1)]))
                wts.append(W("s_V", marks[("vend", ti - 1)]))
                wts.append(W("s_S", marks[("send", ti - 1)]))
            emit("scalar", lambda s, xb_=xb_: s.activation(xbb[:], xb_[:], AF.Copy),
                 waits=wts, inc="s_S")
            emit("scalar", lambda s, xb_=xb_: s.activation(sqb[:], xb_[:], AF.Square),
                 inc="s_S")
            m_casts = cnt["s_S"]

            wts = [W("s_S", m_casts)]
            if ti > 0:
                wts.append(W("s_V", marks[("z3", ti - 1)]))
            emit("tensor", stat_mms(0, xbb), waits=wts, inc="s_T")
            m_sx = cnt["s_T"]
            emit("tensor", stat_mms(1, sqb), inc="s_T")
            m_ssq = cnt["s_T"]

            emit("scalar", lambda s: s.activation(mu_b[:], ps[0][:, 0:F], AF.Copy),
                 waits=[W("s_T", m_sx)])
            def musq_f(s):
                s.nop()
                return s.activation(va_b[:], mu_b[:], AF.Square)
            emit("scalar", musq_f, inc="s_S")
            m_mu = cnt["s_S"]
            emit("vector", var_chain(LN_EPS_A),
                 waits=[W("s_S", m_mu), W("s_T", m_ssq)], inc="s_V")
            m_var = cnt["s_V"]
            emit("scalar", lambda s: s.activation(va_b[:], va_b[:], AF.Ln),
                 waits=[W("s_V", m_var)])
            def iv_f(s):
                s.nop()
                return s.activation(iv_b[:], va_b[:], AF.Exp, scale=-0.5)
            emit("scalar", iv_f, inc="s_S")
            m_iv = cnt["s_S"]

            emit("vector", lambda v, xb_=xb_: v.tensor_tensor(
                full3(t1), full3(xb_), mu_full, ALU.subtract),
                waits=[W("s_S", m_mu)])
            emit("vector", lambda v: v.tensor_tensor(
                full3(t1), full3(t1), iv_full, ALU.mult),
                waits=[W("s_S", m_iv)], inc="s_V")
            m_z = cnt["s_V"]

            emit("vector", lambda v: v.tensor_copy(
                _ap(h[:, 0:1], 0, [[NCH * (F + 1), P], [F + 1, NCH], [1, 1]]),
                _ap(h[:, 0:1], F, [[NCH * (F + 1), P], [F + 1, NCH], [1, 1]])),
                inc="s_V")
            m_hcarry = cnt["s_V"]

            def h_aff(c):
                def f(s):
                    s.nop()
                    return s.activation(chp(h, c)[:, 1:F + 1], ch(t1, c),
                                        AF.Identity, scale=Aa[:, c:c + 1],
                                        bias=sa_v[:, c:c + 1])
                return f
            for c in range(NCH):
                emit("scalar", h_aff(c),
                     waits=[W("s_V", m_z), W("s_V", m_hcarry)] if c == 0 else (),
                     inc="s_S" if c == NCH - 1 else None)
            m_h = cnt["s_S"]

            for c in range(NCH):
                emit("vector", lambda v, c=c: v.tensor_tensor(
                    ch(hd, c), chp(h, c)[:, 1:F + 1], chp(h, c)[:, 0:F],
                    ALU.subtract), waits=[W("s_S", m_h)] if c == 0 else ())

            def lerp(dst, tmoff, c):
                def f(v):
                    return v.scalar_tensor_tensor(
                        ch(dst, c), ch(hd, c), vecs[:, tmoff + c:tmoff + c + 1],
                        chp(h, c)[:, 0:F],
                        ALU.mult, ALU.add)
                return f
            for c in range(NCH):
                emit("vector", lerp(xk, 8, c), aux=(c == 0),
                     inc="s_V" if c == NCH - 1 else None)
            m_xk = cnt["s_V"]
            for c in range(NCH):
                emit("vector", lerp(xv, 16, c), aux=(c == 0),
                     inc="s_V" if c == NCH - 1 else None)
            m_xv = cnt["s_V"]
            for c in range(NCH):
                emit("vector", lerp(xr, 24, c), aux=(c == 0),
                     inc="s_V" if c == NCH - 1 else None)
            m_xr = cnt["s_V"]

            for p in range(4):
                bank = 2 + (p % 2)
                wts = []
                if p == 0:
                    wts = [W("s_V", m_xk), Wc("s_w")]
                if p >= 2:
                    wts.append(W("s_S", marks[("kev", ti, p - 2)]))
                emit("tensor", proj_pair(wk, xk, p, bank), waits=wts, inc="s_T")
                mme = cnt["s_T"]
                emit("scalar", lambda s, p=p, bank=bank: s.activation(
                    pr2(a, p), ps[bank][:], AF.Exp),
                    waits=[W("s_T", mme)], inc="s_S")
                marks[("kev", ti, p)] = cnt["s_S"]
            m_kev = marks[("kev", ti, 3)]
            for p in range(4):
                bank = 4 + (p % 2)
                wts = []
                if p == 0:
                    wts = [W("s_V", m_xv)]
                if p >= 2:
                    wts.append(W("s_S", marks[("vev", ti, p - 2)]))
                emit("tensor", proj_pair(wv, xv, p, bank), waits=wts, inc="s_T")
                mme = cnt["s_T"]
                emit("scalar", lambda s, p=p, bank=bank: s.activation(
                    pr2(t2, p), ps[bank][:], AF.Copy),
                    waits=[W("s_T", mme)] + ([W("s_V", m_xr)] if p == 0 else []),
                    inc="s_S")
                marks[("vev", ti, p)] = cnt["s_S"]
            m_vev = marks[("vev", ti, 3)]
            for p in range(4):
                bank = 6 + (p % 2)
                wts = []
                if p == 0:
                    wts = [W("s_V", m_xr), Wc("s_w")]
                if p >= 2:
                    wts.append(W("s_S", marks[("rev", ti, p - 2)]))
                emit("tensor", proj_pair(wr, xr, p, bank), waits=wts, inc="s_T")
                mme = cnt["s_T"]
                emit("scalar", lambda s, p=p, bank=bank: s.activation(
                    pr2(rb, p), ps[bank][:], AF.Exp, scale=-1.0),
                    waits=[W("s_T", mme)], inc="s_S")
                marks[("rev", ti, p)] = cnt["s_S"]
            m_rev = cnt["s_S"]

            emit("vector", lambda v: v.tensor_tensor(cbuf[:], a[:], t2[:], ALU.mult),
                 waits=[W("s_S", m_vev)], inc="s_V")
            m_c = cnt["s_V"]

            emit("vector", lambda v: v.tensor_copy(
                _ap(Nb[:, 0:1], 0, [[NCH * (F + 1), P], [F + 1, NCH], [1, 1]]),
                _ap(Nb[:, 0:1], F, [[NCH * (F + 1), P], [F + 1, NCH], [1, 1]])))
            emit("vector", lambda v: v.tensor_copy(
                _ap(Eb[:, 0:1], 0, [[NCH * (F + 1), P], [F + 1, NCH], [1, 1]]),
                _ap(Eb[:, 0:1], F, [[NCH * (F + 1), P], [F + 1, NCH], [1, 1]])))
            for c in range(NCH):
                emit("vector", lambda v, c=c: v.tensor_tensor_scan(
                    chp(Nb, c)[:, 1:F + 1], lamc[c], ch(cbuf, c),
                    chp(Nb, c)[:, 0:1], ALU.mult, ALU.add), aux=True)
            for c in range(NCH):
                emit("vector", lambda v, c=c: v.tensor_tensor_scan(
                    chp(Eb, c)[:, 1:F + 1], lamc[c], ch(a, c),
                    chp(Eb, c)[:, 0:1], ALU.mult, ALU.max),
                    waits=[W("s_S", m_kev)] if c == 0 else (), aux=True)
            for c in range(NCH):
                emit("vector", lambda v, c=c: v.scalar_tensor_tensor(
                    ch(t1, c), ch(cbuf, c), eu[:, c:c + 1],
                    chp(Nb, c)[:, 0:F], ALU.mult, ALU.add), aux=(c == 0))
            for c in range(NCH):
                emit("vector", lambda v, c=c: v.scalar_tensor_tensor(
                    ch(t2, c), ch(a, c), eu[:, c:c + 1],
                    chp(Eb, c)[:, 0:F], ALU.mult, ALU.add), aux=(c == 0),
                    inc="s_V" if c == NCH - 1 else None)
            m_den = cnt["s_V"]

            emit("vector", lambda v: v.scalar_tensor_tensor(
                t2[:], rb[:], 1.0, t2[:], ALU.add, ALU.mult),
                waits=[W("s_S", m_rev)], inc="s_V", aux=True)
            m_den2 = cnt["s_V"]

            emit("scalar", lambda s: s.activation(t2[:], t2[:], AF.Ln),
                 waits=[W("s_V", m_den2)])
            def rec_f(s):
                s.nop()
                return s.activation(t2[:], t2[:], AF.Exp, scale=-1.0)
            emit("scalar", rec_f, inc="s_S")
            m_rec = cnt["s_S"]

            emit("vector", lambda v: v.tensor_tensor(xk[:], t1[:], t2[:], ALU.mult),
                 waits=[W("s_S", m_rec)], inc="s_V")
            m_y = cnt["s_V"]

            for p in range(4):
                bank = 2 + (p % 2)
                wts = []
                if p == 0:
                    wts = [W("s_V", m_y)]
                if p >= 2:
                    wts.append(W("s_S", marks[("oev", ti, p - 2)]))
                emit("tensor", proj_pair(wo, xk, p, bank), waits=wts, inc="s_T")
                mme = cnt["s_T"]
                emit("scalar", lambda s, p=p, bank=bank: s.activation(
                    pr2(t2, p), ps[bank][:], AF.Copy),
                    waits=[W("s_T", mme)], inc="s_S")
                marks[("oev", ti, p)] = cnt["s_S"]
            m_oev = cnt["s_S"]
            for c in range(NCH):
                wts = []
                if c == 0:
                    wts = [W("s_S", m_oev)]
                    if ti >= 2:
                        wts.append(Wc("s_ao"))
                emit("vector", lambda v, c=c, xb_=xb_, aab=aab:
                     v.scalar_tensor_tensor(
                         ch(aab, c), ch(t2, c), gav[:, c:c + 1], ch(xb_, c),
                         ALU.mult, ALU.add), waits=wts, aux=(c == 0),
                     inc="s_V" if c == NCH - 1 else None)
            m_aa = cnt["s_V"]
            x_rel[ti] = m_aa

            emit("sync", lambda e, aab=aab, ti=ti: e.dma_start(
                out=_ap(out_hbm, ti * F, [[T, P], [P * T, NCH], [1, F]]),
                in_=aab[:]), waits=[W("s_V", m_aa)], inc="s_ao", incn=16)
            st_aa += 16
            aa_st[ti] = st_aa

            emit("scalar", lambda s, aab=aab: s.activation(xbb[:], aab[:], AF.Copy),
                 waits=[W("s_V", m_aa)], inc="s_S")
            emit("scalar", lambda s, aab=aab: s.activation(sqb[:], aab[:], AF.Square),
                 inc="s_S")
            m_casts2 = cnt["s_S"]
            emit("tensor", stat_mms(0, xbb), waits=[W("s_S", m_casts2)], inc="s_T")
            m_sx2 = cnt["s_T"]
            emit("tensor", stat_mms(1, sqb), inc="s_T")
            m_ssq2 = cnt["s_T"]
            emit("scalar", lambda s: s.activation(mu_b[:], ps[0][:, 0:F], AF.Copy),
                 waits=[W("s_T", m_sx2)], inc="s_S")
            m_mu2 = cnt["s_S"]
            emit("vector", var_chain(LN_EPS_A),
                 waits=[W("s_S", m_mu2), W("s_T", m_ssq2)], inc="s_V")
            m_var2 = cnt["s_V"]
            emit("scalar", lambda s: s.activation(va_b[:], va_b[:], AF.Ln),
                 waits=[W("s_V", m_var2)])
            def iv_f(s):
                s.nop()
                return s.activation(iv_b[:], va_b[:], AF.Exp, scale=-0.5)
            emit("scalar", iv_f, inc="s_S")
            m_iv2 = cnt["s_S"]
            emit("vector", lambda v, aab=aab: v.tensor_tensor(
                full3(t1), full3(aab), mu_full, ALU.subtract),
                waits=[W("s_S", m_mu2)])
            emit("vector", lambda v: v.tensor_tensor(
                full3(t1), full3(t1), iv_full, ALU.mult),
                waits=[W("s_S", m_iv2)], inc="s_V")
            m_z2 = cnt["s_V"]

            def h2_aff(c):
                def f(s):
                    s.nop()
                    return s.activation(ch(t2, c), ch(t1, c), AF.Identity,
                                        scale=Bb[:, c:c + 1],
                                        bias=sbv_v[:, c:c + 1])
                return f
            for c in range(NCH):
                emit("scalar", h2_aff(c), waits=[W("s_V", m_z2)] if c == 0 else (),
                     inc="s_S" if c == NCH - 1 else None)
            m_h2 = cnt["s_S"]

            emit("scalar", lambda s: s.activation(xbb[:], t2[:], AF.Copy), inc="s_S")
            emit("scalar", lambda s: s.activation(sqb[:], t2[:], AF.Square),
                 inc="s_S")
            m_casts3 = cnt["s_S"]
            emit("tensor", stat_mms(0, xbb), waits=[W("s_S", m_casts3)], inc="s_T")
            m_sx3 = cnt["s_T"]
            emit("tensor", stat_mms(1, sqb), inc="s_T")
            m_ssq3 = cnt["s_T"]
            marks[("ssq3", ti)] = m_ssq3
            emit("scalar", lambda s: s.activation(mu_b[:], ps[0][:, 0:F], AF.Copy),
                 waits=[W("s_T", m_sx3)])
            emit("scalar", musq_f, inc="s_S")
            m_mu3 = cnt["s_S"]
            emit("vector", var_chain(LN_EPS_2),
                 waits=[W("s_S", m_mu3), W("s_T", m_ssq3)], inc="s_V")
            m_var3 = cnt["s_V"]
            emit("scalar", lambda s: s.activation(va_b[:], va_b[:], AF.Ln),
                 waits=[W("s_V", m_var3)])
            def iv_f(s):
                s.nop()
                return s.activation(iv_b[:], va_b[:], AF.Exp, scale=-0.5)
            emit("scalar", iv_f, inc="s_S")
            m_iv3 = cnt["s_S"]
            emit("vector", lambda v: v.tensor_tensor(
                full3(t1), full3(t2), mu_full, ALU.subtract),
                waits=[W("s_S", m_mu3)])
            emit("vector", lambda v: v.tensor_tensor(
                full3(t1), full3(t1), iv_full, ALU.mult),
                waits=[W("s_S", m_iv3)], inc="s_V")
            marks[("z3", ti)] = cnt["s_V"]
            m_z3 = cnt["s_V"]

            def xn_aff(c, xnb):
                def f(s):
                    s.nop()
                    return s.activation(ch(xnb, c), ch(t1, c), AF.Identity,
                                        scale=vecs[:, 48 + c:49 + c],
                                        bias=vecs[:, 56 + c:57 + c])
                return f
            for c in range(NCH):
                wts = []
                if c == 0:
                    wts = [W("s_V", m_z3)]
                    if ti >= 2:
                        wts.append(Wc("s_no"))
                emit("scalar", xn_aff(c, xnb), waits=wts,
                     inc="s_S" if c == NCH - 1 else None)
            m_xn = cnt["s_S"]

            emit("vector", lambda v, xnb=xnb, ti=ti: v.tensor_reduce(
                maxacc[:, ti:ti + 1], xnb[:], mybir.AxisListType.X, ALU.max,
                apply_absolute_value=True), waits=[W("s_S", m_xn)], inc="s_V")
            marks[("mx", ti)] = cnt["s_V"]

            emit("sync", lambda e, xnb=xnb, ti=ti: e.dma_start(
                out=_ap(xn_hbm, ti * F, [[T, P], [P * T, NCH], [1, F]]),
                in_=xnb[:]), waits=[W("s_S", m_xn)], inc="s_no", incn=16)
            st_xn += 16
            xn_st[ti] = st_xn
            marks[("vend", ti)] = cnt["s_V"]
            marks[("send", ti)] = cnt["s_S"]

            if ti + 2 < NT:
                emit("sync", xload(ti + 2), waits=[W("s_V", x_rel[ti])],
                     inc="s_x", incn=16)

        m_mx_red = marks[("mx", NT - 1)]
        emit("vector", lambda v: v.tensor_reduce(
            mxc[:], maxacc[:], mybir.AxisListType.X, ALU.max),
            waits=[W("s_V", m_mx_red)], inc="s_V", aux=True)
        m_mx_final = cnt["s_V"]
        emit("sync", lambda e: e.dma_start(
            out=mx_hbm, in_=mxc[:], allow_slow_non_contiguous=True),
            waits=[W("s_V", m_mx_final), W("s_S", cnt["s_S"]),
                   W("s_T", cnt["s_T"])], inc="s_x", incn=16)
        m_gx1 = cnt["s_x"]
        emit("sync", lambda e: e.dma_start(out=mxrow[:], in_=mx_hbm),
             waits=[W("s_x", m_gx1)], inc="s_x", incn=16)
        m_gx2 = cnt["s_x"]
        emit("vector", lambda v: v.tensor_reduce(
            gsc[0:1, 0:1], mxrow[:], mybir.AxisListType.X, ALU.max),
            waits=[W("s_x", m_gx2)], inc="s_V", aux=True)
        m_gmax = cnt["s_V"]
        emit("gpsimd", lambda g: g.dma_start(out=cc_in, in_=gsc[:]),
             waits=[W("s_V", m_gmax)], inc="s_G", incn=16)
        m_g3 = cnt["s_G"]
        emit("gpsimd", lambda g: g.collective_compute(
            "AllReduce", ALU.max, replica_groups=[list(range(B))],
            ins=[cc_in], outs=[cc_out]), waits=[W("s_G", m_g3)], inc="s_cc")
        m_cc = cnt["s_cc"]
        emit("gpsimd", lambda g: g.dma_start(out=gsc[:], in_=cc_out),
             waits=[W("s_cc", m_cc)], inc="s_G", incn=16)
        m_g4 = cnt["s_G"]
        emit("sync", lambda e: e.dma_start(
            out=gcol[:], in_=_ap(gsc[0:1, 0:1], 0, [[64, 1], [0, P]]),
            allow_slow_non_contiguous=True),
            waits=[W("s_G", m_g4)], inc="s_x", incn=16)
        m_g5 = cnt["s_x"]
        emit("vector", lambda v: v.reciprocal(sgam[:], gcol[:]),
             waits=[W("s_x", m_g5)], aux=True)
        emit("vector", lambda v: v.tensor_scalar(sgam[:], sgam[:], QB, None,
                                                 ALU.mult), inc="s_V", aux=True)
        m_sg = cnt["s_V"]

        wq_parts = (xk, xv, xr, xbb)

        def wq_part(c):
            return wq_parts[c // 2], (c % 2) * D

        m_z3_last = marks[("z3", NT - 1)]
        m_xn_last = m_xn
        for q in range(4):
            wts = ([W("s_V", m_z3_last), W("s_S", m_xn_last),
                    W("s_T", marks[("ssq3", NT - 1)])] if q == 0
                   else [W("s_S", marks[("wabs", q - 1)])])
            emit("gpsimd", lambda g, q=q: g.dma_start(
                out=t1[:], in_=_ap(ffnw_hbm, 2 * q * P * D,
                                   [[D, P], [P * D, 2], [1, D]])),
                waits=wts, inc="s_G", incn=16)
            m_ld = cnt["s_G"]
            swts = [W("s_G", m_ld)]
            if q == 0:
                swts.append(W("s_V", m_mx_final))
            else:
                swts.append(W("s_T", marks[("bmm", q - 1)]))
            emit("scalar", lambda s: s.activation(sqb[:], t1[:], AF.Abs),
                 waits=swts, inc="s_S")
            m_abs = cnt["s_S"]
            marks[("wabs", q)] = m_abs

            def beta_mms(q):
                def f(t):
                    last = None
                    bank = (2, 3, 6, 7)[q]
                    for half in range(4):
                        last = t.matmul(ps[bank][:, 0:512], ones_t[:],
                                        sqb[:, half * 512:(half + 1) * 512],
                                        start=(half == 0), stop=(half == 3))
                    return last
                return f
            emit("tensor", beta_mms(q), waits=[W("s_S", m_abs)], inc="s_T")
            marks[("bmm", q)] = cnt["s_T"]
        m_bmm_all = cnt["s_T"]
        emit("vector", lambda v: v.tensor_copy(bacc[:], ps[2][:, 0:512]),
             waits=[W("s_T", m_bmm_all)])
        for bank in (3, 6, 7):
            emit("vector", lambda v, bank=bank: v.tensor_tensor(
                bacc[:], bacc[:], ps[bank][:, 0:512], ALU.add),
                inc="s_V" if bank == 7 else None)
        marks[("bev", 3)] = cnt["s_V"]
        def betasum_f(s):
            s.nop()
            return s.activation(t2[:, 0:512], bacc[:], AF.Copy,
                                accum_out=t2[:, 1024:1025])
        emit("scalar", betasum_f, waits=[W("s_V", marks[("bev", 3)])], inc="s_S")
        m_bsum_s = cnt["s_S"]
        emit("vector", lambda v: v.tensor_scalar(
            bcol[:], t2[:, 1024:1025], 1.0 / 1024.0, None, ALU.mult),
            waits=[W("s_S", m_bsum_s)], aux=True)
        emit("vector", lambda v: v.tensor_scalar(
            rbcol[:], bcol[:], 1e-5, None, ALU.add), inc="s_V", aux=True)
        m_g6 = cnt["s_V"]
        emit("vector", lambda v: v.reciprocal(rbcol[:], rbcol[:]),
             inc="s_V", aux=True)
        m_rb = cnt["s_V"]

        for q in range(4):
            wts = ([W("s_V", m_rb)] if q == 0
                   else [W("s_V", marks[("wqq", q - 1)])])
            emit("gpsimd", lambda g, q=q: g.dma_start(
                out=t1[:], in_=_ap(ffnw_hbm, 2 * q * P * D,
                                   [[D, P], [P * D, 2], [1, D]])),
                waits=wts, inc="s_G", incn=16)
            m_ld2 = cnt["s_G"]

            def quant(q):
                def f(v):
                    v.nop()
                    v.tensor_scalar(t1[:], t1[:], rbcol[:, 0:1], None, ALU.mult)
                    v.nop()
                    v.tensor_scalar(t2[:], t1[:], 0.5, None, ALU.is_gt)
                    v.nop()
                    v.tensor_scalar(t1[:], t1[:], -0.5, None, ALU.is_lt)
                    v.nop()
                    part, _ = wq_part(2 * q)
                    return v.tensor_tensor(part[:, 0:2 * D], t2[:], t1[:],
                                           ALU.subtract)
                return f
            emit("vector", quant(q), waits=[W("s_G", m_ld2)], inc="s_V")
            marks[("wqq", q)] = cnt["s_V"]
        m_wq = cnt["s_V"]

        emit("vector", lambda v: v.tensor_scalar(
            mxc[:], bcol[:], 1.0 / QB, None, ALU.mult),
            waits=[W("s_V", m_g6), W("s_V", m_sg)], inc="s_V", aux=True)
        m_b128 = cnt["s_V"]

        def gbs1_f(s):
            s.nop()
            return s.activation(rbcol2[:], mxc[:], AF.Identity,
                                scale=gcol[:, 0:1])
        emit("scalar", gbs1_f, waits=[W("s_V", m_b128)], inc="s_S")

        def gbs2_f(s):
            s.nop()
            return s.activation(gbs[:], gbv[:], AF.Identity,
                                scale=rbcol2[:, 0:1])
        emit("scalar", gbs2_f, inc="s_S")
        m_gbs = cnt["s_S"]

        def xnload(ti, wts):
            def f(e):
                return e.dma_start(
                    out=xnB[ti % 2][:],
                    in_=_ap(xn_hbm, ti * F, [[T, P], [P * T, NCH], [1, F]]))
            return f, wts
        f0, w0 = xnload(0, [Wc("s_no"), W("s_V", m_wq)])
        emit("sync", f0, waits=w0, inc="s_x", incn=16)
        base_px = cnt["s_x"]
        f1, w1 = xnload(1, [Wc("s_no"), W("s_V", m_wq)])
        emit("sync", f1, waits=w1, inc="s_x", incn=16)

        for ti in range(NT):
            xnb = xnB[ti % 2]
            term = aaB[ti % 2]
            qb = sqb if ti % 2 == 0 else sqb2
            wts = [Wc("s_x"), W("s_S", m_gbs)]
            if ti >= 2:
                wts.append(W("s_T", marks[("fmm", ti - 2)]))
            emit("vector", lambda v, xnb=xnb: v.tensor_scalar(
                t1[:], xnb[:], sgam[:, 0:1], 8388608.0, ALU.mult, ALU.add),
                waits=wts, aux=True)
            emit("vector", lambda v, qb=qb: v.tensor_scalar(
                qb[:], t1[:], -8388608.0, CLIP, ALU.add, ALU.min))
            emit("vector", lambda v, qb=qb: v.tensor_scalar(
                qb[:], qb[:], -CLIP, None, ALU.max), inc="s_V")
            m_xq = cnt["s_V"]
            marks[("xq", ti)] = m_xq

            def fmm(e, bank):
                def f(t, qb=qb):
                    last = None
                    for c in range(NCH):
                        part, off = wq_part(c)
                        last = t.matmul(ps[bank][:, 0:F],
                                        part[:, off + e * P:off + (e + 1) * P],
                                        ch(qb, c), start=(c == 0),
                                        stop=(c == NCH - 1))
                    return last
                return f
            for e in range(NCH):
                bank = 2 + (e % 2)
                wts = [W("s_V", m_xq)] if e == 0 else []
                if e < 2 and ti >= 1:
                    wts.append(W("s_S", marks[("fev", ti - 1, 6 + e)]))
                if e >= 2:
                    wts.append(W("s_S", marks[("fev", ti, e - 2)]))
                emit("tensor", fmm(e, bank), waits=wts, inc="s_T")
                mme = cnt["s_T"]
                swts = [W("s_T", mme)]
                if ti == 0 and e == 0:
                    swts.append(Wc("s_ao"))
                if ti >= 2 and e == 0:
                    swts.append(Wc("s_fo"))
                def fev_f(s, e=e, bank=bank, term=term):
                    s.nop()
                    return s.activation(ch(term, e), ps[bank][:, 0:F], AF.Identity,
                                        scale=gbs[:, e:e + 1])
                emit("scalar", fev_f, waits=swts, inc="s_S")
                marks[("fev", ti, e)] = cnt["s_S"]
            marks[("fmm", ti)] = cnt["s_T"]
            m_term = cnt["s_S"]

            emit("gpsimd", lambda g, term=term, ti=ti: g.dma_start(
                out=_ap(out_hbm, ti * F, [[T, P], [P * T, NCH], [1, F]]),
                in_=term[:], accum_op=ALU.add),
                waits=[W("s_S", m_term)], inc="s_fo", incn=16)
            marks[("acc", ti)] = cnt["s_fo"]

            if ti + 2 < NT:
                fN, wN = xnload(ti + 2, [Wc("s_no"),
                                         W("s_V", marks[("xq", ti)])])
                emit("sync", fN, waits=wN, inc="s_x", incn=16)

        emit("gpsimd", lambda g: g.wait_ge(sems["s_fo"], cnt["s_fo"]))

        def replay(eng_name):
            def runner(eng):
                for fn, waits, inc, incn, aux in sched[eng_name]:
                    for (sem, val) in waits:
                        eng.wait_ge(sem, val)
                    if aux:
                        eng.nop()
                    instr = fn(eng)
                    if inc is not None:
                        assert instr is not None, \
                            f"{eng_name} op returned None with inc"
                        instr.then_inc(sems[inc], incn)
            return runner

        block.sync(replay("sync"))
        block.scalar(replay("scalar"))
        block.vector(replay("vector"))
        block.tensor(replay("tensor"))
        block.gpsimd(replay("gpsimd"))

    return nc


def kernel(**inputs):
    x = np.ascontiguousarray(inputs["L0_attn_x"])
    wkT = np.ascontiguousarray(inputs["L0_attn_W_k"].T)
    wvT = np.ascontiguousarray(inputs["L0_attn_W_v"].T)
    wrT = np.ascontiguousarray(inputs["L0_attn_W_r"].T)
    woT = np.ascontiguousarray(inputs["L0_attn_W_o"].T)
    ffnw = np.ascontiguousarray(inputs["L0_ffn_weight"])
    wmod = np.ascontiguousarray(inputs["L0_wmod"])
    crow = inputs["L0_c"]

    def vec8(v):
        return np.ascontiguousarray(np.reshape(np.asarray(v), (NCH, P)).T)

    nc = build_nc()
    in_maps = []
    for b in range(B):
        vecs = np.zeros((P, 64), np.float32)
        vecs[:, 0:8] = vec8(crow[b])
        vecs[:, 8:16] = vec8(inputs["L0_attn_time_mix_k"])
        vecs[:, 16:24] = vec8(inputs["L0_attn_time_mix_v"])
        vecs[:, 24:32] = vec8(inputs["L0_attn_time_mix_r"])
        vecs[:, 32:40] = vec8(inputs["L0_attn_time_decay"])
        vecs[:, 40:48] = vec8(inputs["L0_attn_time_first"])
        vecs[:, 48:56] = vec8(inputs["L0_ffn_ln_gamma"])
        vecs[:, 56:64] = vec8(inputs["L0_ffn_ln_beta"])
        in_maps.append({
            "x": np.ascontiguousarray(x[b].T),
            "wkT": wkT, "wvT": wvT, "wrT": wrT, "woT": woT,
            "ffnw": ffnw, "wmod": wmod, "vecs": vecs,
            "ident": np.eye(P, dtype=np.float32),
        })
    res = run_bass_kernel_spmd(nc, in_maps, core_ids=list(range(B)))
    out = np.stack([np.ascontiguousarray(r["out"].T) for r in res.results])
    return out


# version 16
# speedup vs baseline: 1.0728x; 1.0531x over previous
import contextlib
import numpy as np
import ml_dtypes
import concourse.bass as bass
import concourse.mybir as mybir

from concourse.bass_utils import run_bass_kernel_spmd

F32 = mybir.dt.float32
BF16 = mybir.dt.bfloat16
AF = mybir.ActivationFunctionType
ALU = mybir.AluOpType

P = 128
D = 1024
NCH = D // P
T = 8192
F = 256
NT = T // F
B = 8
LN_EPS_A = 1e-5
LN_EPS_2 = 1e-3
QB = 128.0
CLIP = QB - 1e-5


def _ap(t_ap, offset, dims):
    return bass.AP(t_ap.tensor, offset, dims)


def build_nc():
    nc = bass.Bass("TRN2", target_bir_lowering=False, detect_race_conditions=False)

    x_hbm = nc.dram_tensor("x", [D, T], F32, kind="ExternalInput").ap()
    wk_hbm = nc.dram_tensor("wkT", [D, D], F32, kind="ExternalInput").ap()
    wv_hbm = nc.dram_tensor("wvT", [D, D], F32, kind="ExternalInput").ap()
    wr_hbm = nc.dram_tensor("wrT", [D, D], F32, kind="ExternalInput").ap()
    wo_hbm = nc.dram_tensor("woT", [D, D], F32, kind="ExternalInput").ap()
    ffnw_hbm = nc.dram_tensor("ffnw", [D, D], F32, kind="ExternalInput").ap()
    wmod_hbm = nc.dram_tensor("wmod", [D, 6 * D], F32, kind="ExternalInput").ap()
    vecs_hbm = nc.dram_tensor("vecs", [P, 64], F32, kind="ExternalInput").ap()
    ident_hbm = nc.dram_tensor("ident", [P, P], F32, kind="ExternalInput").ap()
    out_hbm = nc.dram_tensor("out", [D, T], F32, kind="ExternalOutput").ap()

    xn_hbm = nc.dram_tensor("xn_scr", [D, T], BF16).ap()
    mx_hbm = nc.dram_tensor("mx_scr", [1, P], F32).ap()
    cc_in_t = nc.dram_tensor("cc_in", [1, 64], F32)
    cc_out_t = nc.dram_tensor("cc_out", [1, 64], F32)
    cc_in = cc_in_t.ap().opt()
    cc_out = cc_out_t.ap().opt()

    es = contextlib.ExitStack()
    with es:
        sb = lambda name, shape, dt: es.enter_context(nc.sbuf_tensor(name, shape, dt))
        wk = sb("wk", [P, NCH * D], BF16)
        wv = sb("wv", [P, NCH * D], BF16)
        wr = sb("wr", [P, NCH * D], BF16)
        wo = sb("wo", [P, NCH * D], BF16)
        xB = [sb(f"x{i}", [P, NCH * F], F32) for i in range(2)]
        xbb = sb("xbb", [P, NCH * F], BF16)
        sqb = sb("sqb", [P, NCH * F], BF16)
        sqb2 = sb("sqb2", [P, NCH * F], BF16)
        h = sb("h", [P, NCH * (F + 1)], BF16)
        hd = sb("hd", [P, NCH * F], BF16)
        xk = sb("xk", [P, NCH * F], BF16)
        xv = sb("xv", [P, NCH * F], BF16)
        xr = sb("xr", [P, NCH * F], BF16)
        aP = [sb(f"a{i}", [P, NCH * F], BF16) for i in range(2)]
        cbP = [sb(f"cb{i}", [P, NCH * F], BF16) for i in range(2)]
        rbP = [sb(f"rbx{i}", [P, NCH * F], BF16) for i in range(2)]
        vvP = [sb(f"vv{i}", [P, NCH * F], BF16) for i in range(2)]
        Nb = sb("Nb", [P, NCH * (F + 1)], F32)
        Eb = sb("Eb", [P, NCH * (F + 1)], F32)
        t1 = sb("t1", [P, NCH * F], F32)
        t2 = sb("t2", [P, NCH * F], F32)
        aab = sb("aa0", [P, NCH * F], F32)
        xnB = [sb(f"xn{i}", [P, NCH * F], BF16) for i in range(2)]
        mu_b = sb("mu_b", [P, F], F32)
        iv_b = sb("iv_b", [P, F], F32)
        va_b = sb("va_b", [P, F], F32)
        mu_1 = sb("mu_1", [P, F], F32)
        iv_1 = sb("iv_1", [P, F], F32)
        va_1 = sb("va_1", [P, F], F32)
        ones_t = sb("ones_t", [P, P], BF16)
        identb = sb("identb", [P, P], BF16)
        mxb = sb("mxb", [P, P], BF16)
        bacc = sb("bacc", [P, 512], F32)
        vecs = sb("vecs_sb", [P, 64], F32)
        scol = sb("scol", [P, 8], BF16)
        lam = sb("lam", [P, 8], F32)
        eu = sb("eu", [P, 8], F32)
        modv = sb("modv", [P, 48], F32)
        Aa = sb("Aa", [P, 8], F32)
        Bb = sb("Bb", [P, 8], F32)
        gav = sb("gav", [P, 8], F32)
        gbv = sb("gbv", [P, 8], F32)
        gbs = sb("gbs", [P, 8], F32)
        cpool = sb("cpool", [P, 2], F32)
        mxrow = sb("mxrow", [1, P], F32)
        gsc = sb("gsc", [1, 64], F32)
        gcol = sb("gcol", [P, 1], F32)
        sgam = sb("sgam", [P, 1], F32)
        bcol = sb("bcol", [P, 1], F32)
        rbcol = sb("rbcol", [P, 1], F32)
        rbcol2 = sb("rbcol2", [P, 1], F32)
        bsum = sb("bsum", [1, 2], F32)
        maxacc = sb("maxacc", [P, NT], F32)
        mxc = sb("mxc", [P, 1], F32)
        tiny = sb("tinyv", [P, 8], F32)
        ps = [es.enter_context(nc.psum_tensor(f"ps{i}", [P, 512], F32)) for i in range(8)]

        nc.const_aps.register(F32, -1.0, cpool[:, 0:1])
        nc.const_aps.register(F32, -0.5, cpool[:, 1:2])

        sems = {}
        for name in ["s_x", "s_w", "s_S", "s_V", "s_T", "s_G", "s_ao", "s_no",
                     "s_cc", "s_fo", "s_Gc"]:
            sems[name] = es.enter_context(nc.semaphore(name))
        cnt = {k: 0 for k in sems}
        block = es.enter_context(nc.Block())

        sched = {k: [] for k in ["sync", "scalar", "vector", "tensor", "gpsimd"]}

        def emit(eng, fn, waits=(), inc=None, incn=1, aux=False):
            waits = list(waits)
            if inc is not None and incn == 16:
                waits.insert(0, (sems[inc], cnt[inc]))
            sched[eng].append((fn, tuple(waits), inc, incn, aux))
            if inc is not None:
                cnt[inc] += incn
            return cnt[inc] if inc else None

        def W(name, n):
            return (sems[name], n)

        def Wc(name):
            return (sems[name], cnt[name])

        def ch(buf, c, width=F):
            return buf[:, c * width:(c + 1) * width]

        def chp(buf, c):
            base = c * (F + 1)
            return buf[:, base:base + F + 1]

        marks = {}

        def full3(buf):
            return _ap(buf[:, 0:1], 0, [[NCH * F, P], [F, NCH], [1, F]])

        emit("gpsimd", lambda g: g.memset(cpool[:, 0:1], -1.0))
        emit("gpsimd", lambda g: g.memset(cpool[:, 1:2], -0.5))
        emit("gpsimd", lambda g: g.memset(ones_t[:], 1.0 / 1024.0))
        emit("gpsimd", lambda g: g.memset(h[:], 0.0))
        emit("gpsimd", lambda g: g.memset(Nb[:], 0.0))
        emit("gpsimd", lambda g: g.memset(Eb[:], 0.0))
        emit("gpsimd", lambda g: g.memset(maxacc[:], 0.0))
        emit("gpsimd", lambda g: g.memset(gsc[:], 0.0), inc="s_Gc")
        m_const = cnt["s_Gc"]
        emit("gpsimd", lambda g: g.dma_start(out=vecs[:], in_=vecs_hbm),
             inc="s_w", incn=16)
        m_vecs = cnt["s_w"]

        emit("scalar", lambda s: s.activation(scol[:], vecs[:, 0:8], AF.Silu),
             waits=[Wc("s_w"), W("s_Gc", m_const)], inc="s_S")
        emit("scalar", lambda s: s.activation(lam[:], vecs[:, 32:40], AF.Exp),
             inc="s_S")
        emit("scalar", lambda s: s.activation(eu[:], vecs[:, 40:48], AF.Exp),
             inc="s_S")
        m_scol = cnt["s_S"]

        def wload(dst, src):
            def f(g):
                return g.dma_start(
                    out=dst[:], in_=_ap(src, 0, [[D, P], [P * D, NCH], [1, D]]))
            return f
        emit("gpsimd", lambda g: g.dma_start(out=identb[:], in_=ident_hbm),
             inc="s_w", incn=16)
        emit("gpsimd", wload(wk, wk_hbm), inc="s_w", incn=16)
        emit("gpsimd", wload(wv, wv_hbm), inc="s_w", incn=16)
        emit("gpsimd", wload(wr, wr_hbm), inc="s_w", incn=16)
        emit("gpsimd", wload(wo, wo_hbm), inc="s_w", incn=16)
        m_wall = cnt["s_w"]

        stageA = (xk, xv, xr)
        stageB = (xbb, sqb, xnB[0])

        def wmod_load1(bufp, c, i):
            def f(g):
                return g.dma_start(
                    out=bufp[:, 0:2 * D],
                    in_=_ap(wmod_hbm, c * P * 6 * D + i * 2 * D,
                            [[6 * D, P], [1, 2 * D]]))
            return f

        def mod_mm(stage, c, j):
            def f(t):
                which = (j * P) // (2 * D)
                off = (j * P) % (2 * D)
                stat = stage[which][:, off:off + P]
                return t.matmul(ps[4][:, j:j + 1], stat, scol[:, c:c + 1],
                                start=(c == 0 and j == 0),
                                stop=(c == NCH - 1 and j == 47),
                                skip_group_check=True)
            return f

        for c in range(NCH):
            stg = stageA if c % 2 == 0 else stageB
            wts = []
            if c >= 2:
                wts = [W("s_T", marks[("modmm", c - 2)])]
            emit("gpsimd", wmod_load1(stg[0], c, 0), waits=wts, inc="s_w", incn=16)
            emit("gpsimd", wmod_load1(stg[1], c, 1), inc="s_w", incn=16)
            emit("gpsimd", wmod_load1(stg[2], c, 2), inc="s_w", incn=16)
            m_wmod_c = cnt["s_w"]
            for j in range(48):
                wts = [W("s_w", m_wmod_c)] if j == 0 else ()
                if c == 0 and j == 0:
                    wts = [W("s_w", m_wmod_c), W("s_S", m_scol)]
                emit("tensor", mod_mm(stg, c, j), waits=wts,
                     inc="s_T" if j == 47 else None)
            marks[("modmm", c)] = cnt["s_T"]
        m_modmm = cnt["s_T"]

        emit("scalar", lambda s: s.activation(modv[:], ps[4][:, 0:48], AF.Copy),
             waits=[W("s_T", m_modmm)], inc="s_S")
        m_modv = cnt["s_S"]
        sa_v = modv[:, 0:8]
        ca_v = modv[:, 8:16]
        ga_v = modv[:, 16:24]
        sbv_v = modv[:, 24:32]
        cb_v = modv[:, 32:40]
        gb_v = modv[:, 40:48]

        GW = 256
        emit("scalar", lambda s: s.activation(t1[:, 0:48], modv[:], AF.Copy),
             waits=[W("s_S", m_modv)], inc="s_S")
        m_gcp = cnt["s_S"]

        def gatewide(v):
            g0 = t1[:, 0:GW]
            g1 = t1[:, GW:2 * GW]
            g2 = t1[:, 2 * GW:3 * GW]
            v.tensor_scalar(g1, g0, 0.0, None, ALU.is_ge)
            v.tensor_scalar(g1, g1, 2.0, -1.0, ALU.mult, ALU.add)
            v.tensor_tensor(g2, g0, g1, ALU.mult)
            v.tensor_scalar(g2, g2, 0.25, None, ALU.max)
            return v.tensor_tensor(g2, g1, g2, ALU.mult)
        emit("vector", gatewide, waits=[W("s_S", m_gcp)], inc="s_V")
        m_gw = cnt["s_V"]
        emit("scalar", lambda s: s.activation(
            gav[:], t1[:, 2 * GW + 16:2 * GW + 24], AF.Copy),
            waits=[W("s_V", m_gw)])
        emit("scalar", lambda s: s.activation(
            gbv[:], t1[:, 2 * GW + 40:2 * GW + 48], AF.Copy), inc="s_S")
        m_gates = cnt["s_S"]
        emit("vector", lambda v: v.tensor_scalar(Aa[:], ca_v, 1.0, None, ALU.add),
             waits=[W("s_S", m_gates)], aux=True)
        emit("vector", lambda v: v.tensor_scalar(Bb[:], cb_v, 1.0, None, ALU.add),
             inc="s_V", aux=True)
        m_mod_ready = cnt["s_V"]

        def xload(ti, extra=()):
            def f(e):
                return e.dma_start(
                    out=xB[ti % 2][:],
                    in_=_ap(x_hbm, ti * F, [[T, P], [P * T, NCH], [1, F]]))
            return f
        emit("sync", xload(0), inc="s_x", incn=16)
        emit("sync", xload(1), inc="s_x", incn=16)

        lamc = [_ap(lam[:, 0:1], c, [[8, P], [0, F]]) for c in range(NCH)]
        x_rel = {}
        aa_st = {}
        xn_st = {}
        st_aa = 0
        st_xn = 0

        def stat_mms(bank, src):
            def f(t):
                last = None
                for c in range(NCH):
                    last = t.matmul(ps[bank][:, 0:F], ones_t[:], ch(src, c),
                                    start=(c == 0), stop=(c == NCH - 1))
                return last
            return f

        def var_chain(eps):
            def f(v):
                v.tensor_tensor(va_b[:], ps[1][:, 0:F], va_b[:], ALU.subtract)
                return v.tensor_scalar(va_b[:], va_b[:], eps, None, ALU.add)
            return f

        mu_full = _ap(mu_b[:, 0:1], 0, [[F, P], [0, NCH], [1, F]])
        iv_full = _ap(iv_b[:, 0:1], 0, [[F, P], [0, NCH], [1, F]])
        assert mu_full is not None

        def proj_mms(wsb, src, e, bank):
            def f(t):
                last = None
                for c in range(NCH):
                    last = t.matmul(ps[bank][:, 0:F],
                                    wsb[:, c * D + e * P:c * D + (e + 1) * P],
                                    ch(src, c), start=(c == 0),
                                    stop=(c == NCH - 1))
                return last
            return f

        def proj_pair(wsb, src, p, bank):
            def f(t):
                last = None
                for half in range(2):
                    e = 2 * p + half
                    for c in range(NCH):
                        last = t.matmul(ps[bank][:, half * F:(half + 1) * F],
                                        wsb[:, c * D + e * P:c * D + (e + 1) * P],
                                        ch(src, c), start=(c == 0),
                                        stop=(c == NCH - 1))
                return last
            return f

        def pr2(buf, p):
            return buf[:, 2 * p * F:(2 * p + 2) * F]

        xbb1 = xr
        sqb1 = xv
        z1 = xk
        ybuf = sqb2

        def var_chain2(eps, bank, va_t):
            def f(v):
                v.tensor_tensor(va_t[:], ps[bank][:, 0:F], va_t[:], ALU.subtract)
                return v.tensor_scalar(va_t[:], va_t[:], eps, None, ALU.add)
            return f

        def emit_pre(ti):
            xb_ = xB[ti % 2]
            a_, vv_, rb_ = aP[ti % 2], vvP[ti % 2], rbP[ti % 2]
            wts = [Wc("s_x")]
            if ti == 0:
                wts.append(W("s_V", m_mod_ready))
                wts.append(W("s_T", m_modmm))
            else:
                wts.append(W("s_T", marks[("rmm", ti - 1)]))
            emit("scalar", lambda s, xb_=xb_: s.activation(xbb1[:], xb_[:], AF.Copy),
                 waits=wts, inc="s_S")
            emit("scalar", lambda s, xb_=xb_: s.activation(sqb1[:], xb_[:],
                                                           AF.Square), inc="s_S")
            m_casts = cnt["s_S"]

            wts = [W("s_S", m_casts)]
            if ti > 0:
                wts.append(W("s_V", marks[("var1", ti - 1)]))
            emit("tensor", stat_mms(0, xbb1), waits=wts, inc="s_T")
            m_sx = cnt["s_T"]
            emit("tensor", stat_mms(1, sqb1), inc="s_T")
            m_ssq = cnt["s_T"]

            emit("scalar", lambda s: s.activation(mu_1[:], ps[0][:, 0:F], AF.Copy),
                 waits=[W("s_T", m_sx)])
            def musq1_f(s):
                s.nop()
                return s.activation(va_1[:], mu_1[:], AF.Square)
            emit("scalar", musq1_f, inc="s_S")
            m_mu = cnt["s_S"]
            emit("vector", var_chain2(LN_EPS_A, 1, va_1),
                 waits=[W("s_S", m_mu), W("s_T", m_ssq)], inc="s_V")
            marks[("var1", ti)] = cnt["s_V"]
            emit("scalar", lambda s: s.activation(va_1[:], va_1[:], AF.Ln),
                 waits=[W("s_V", marks[("var1", ti)])])
            def iv1_f(s):
                s.nop()
                return s.activation(iv_1[:], va_1[:], AF.Exp, scale=-0.5)
            emit("scalar", iv1_f, inc="s_S")
            m_iv = cnt["s_S"]

            mu1_full = _ap(mu_1[:, 0:1], 0, [[F, P], [0, NCH], [1, F]])
            iv1_full = _ap(iv_1[:, 0:1], 0, [[F, P], [0, NCH], [1, F]])
            emit("vector", lambda v, xb_=xb_: v.tensor_tensor(
                full3(z1), full3(xb_), mu1_full, ALU.subtract),
                waits=[W("s_S", m_mu)])
            emit("vector", lambda v: v.tensor_tensor(
                full3(z1), full3(z1), iv1_full, ALU.mult),
                waits=[W("s_S", m_iv)], inc="s_V")
            m_z = cnt["s_V"]

            emit("vector", lambda v: v.tensor_copy(
                _ap(h[:, 0:1], 0, [[NCH * (F + 1), P], [F + 1, NCH], [1, 1]]),
                _ap(h[:, 0:1], F, [[NCH * (F + 1), P], [F + 1, NCH], [1, 1]])),
                inc="s_V")
            m_hcarry = cnt["s_V"]

            def h_aff(c):
                def f(s):
                    s.nop()
                    return s.activation(chp(h, c)[:, 1:F + 1], ch(z1, c),
                                        AF.Identity, scale=Aa[:, c:c + 1],
                                        bias=sa_v[:, c:c + 1])
                return f
            for c in range(NCH):
                emit("scalar", h_aff(c),
                     waits=[W("s_V", m_z), W("s_V", m_hcarry)] if c == 0 else (),
                     inc="s_S" if c == NCH - 1 else None)
            m_h = cnt["s_S"]

            for c in range(NCH):
                emit("vector", lambda v, c=c: v.tensor_tensor(
                    ch(hd, c), chp(h, c)[:, 1:F + 1], chp(h, c)[:, 0:F],
                    ALU.subtract), waits=[W("s_S", m_h)] if c == 0 else ())

            def lerp(dst, tmoff, c):
                def f(v):
                    return v.scalar_tensor_tensor(
                        ch(dst, c), ch(hd, c), vecs[:, tmoff + c:tmoff + c + 1],
                        chp(h, c)[:, 0:F], ALU.mult, ALU.add)
                return f
            for c in range(NCH):
                emit("vector", lerp(xk, 8, c), aux=(c == 0),
                     inc="s_V" if c == NCH - 1 else None)
            m_xk = cnt["s_V"]
            for c in range(NCH):
                emit("vector", lerp(xv, 16, c), aux=(c == 0),
                     inc="s_V" if c == NCH - 1 else None)
            m_xv = cnt["s_V"]
            for c in range(NCH):
                emit("vector", lerp(xr, 24, c), aux=(c == 0),
                     inc="s_V" if c == NCH - 1 else None)
            m_xr = cnt["s_V"]
            marks[("lerps", ti)] = m_xr

            kvr = [(wk, xk, a_, AF.Exp, None, m_xk, "kev", "kmm"),
                   (wv, xv, vv_, AF.Copy, None, m_xv, "vev", "vmm"),
                   (wr, xr, rb_, AF.Exp, -1.0, m_xr, "rev", "rmm")]
            g = 0
            for wsb, srcb, dstb, af, sc, m_src, evn, mmn in kvr:
                for p in range(4):
                    bank = 2 + (g % 2)
                    wts = []
                    if p == 0:
                        wts.append(W("s_V", m_src))
                        if evn == "kev":
                            wts.append(Wc("s_w"))
                    if g >= 2:
                        wts.append(marks[("kvrev", ti, g - 2)])
                    emit("tensor", proj_pair(wsb, srcb, p, bank), waits=wts,
                         inc="s_T")
                    mme = cnt["s_T"]
                    def ev_f(s, p=p, bank=bank, dstb=dstb, af=af, sc=sc):
                        if sc is None:
                            return s.activation(pr2(dstb, p), ps[bank][:], af)
                        return s.activation(pr2(dstb, p), ps[bank][:], af,
                                            scale=sc)
                    emit("scalar", ev_f, waits=[W("s_T", mme)], inc="s_S")
                    marks[("kvrev", ti, g)] = W("s_S", cnt["s_S"])
                    g += 1
                marks[(evn, ti)] = cnt["s_S"]
                marks[(mmn, ti)] = cnt["s_T"]

        def emit_post(ti):
            xb_ = xB[ti % 2]
            xnb = xnB[ti % 2]
            a_, vv_, rb_ = aP[ti % 2], vvP[ti % 2], rbP[ti % 2]
            cb_ = cbP[ti % 2]
            m_kev = marks[("kev", ti)]
            m_vev = marks[("vev", ti)]
            m_rev = marks[("rev", ti)]

            emit("vector", lambda v: v.tensor_tensor(cb_[:], a_[:], vv_[:],
                                                     ALU.mult),
                 waits=[W("s_S", m_vev)], inc="s_V")

            emit("vector", lambda v: v.tensor_copy(
                _ap(Nb[:, 0:1], 0, [[NCH * (F + 1), P], [F + 1, NCH], [1, 1]]),
                _ap(Nb[:, 0:1], F, [[NCH * (F + 1), P], [F + 1, NCH], [1, 1]])))
            emit("vector", lambda v: v.tensor_copy(
                _ap(Eb[:, 0:1], 0, [[NCH * (F + 1), P], [F + 1, NCH], [1, 1]]),
                _ap(Eb[:, 0:1], F, [[NCH * (F + 1), P], [F + 1, NCH], [1, 1]])))
            for c in range(NCH):
                emit("vector", lambda v, c=c: v.tensor_tensor_scan(
                    chp(Nb, c)[:, 1:F + 1], lamc[c], ch(cb_, c),
                    chp(Nb, c)[:, 0:1], ALU.mult, ALU.add), aux=True)
            for c in range(NCH):
                emit("vector", lambda v, c=c: v.tensor_tensor_scan(
                    chp(Eb, c)[:, 1:F + 1], lamc[c], ch(a_, c),
                    chp(Eb, c)[:, 0:1], ALU.mult, ALU.max),
                    waits=[W("s_S", m_kev)] if c == 0 else (), aux=True)
            for c in range(NCH):
                emit("vector", lambda v, c=c: v.scalar_tensor_tensor(
                    ch(t1, c), ch(cb_, c), eu[:, c:c + 1],
                    chp(Nb, c)[:, 0:F], ALU.mult, ALU.add), aux=(c == 0))
            for c in range(NCH):
                emit("vector", lambda v, c=c: v.scalar_tensor_tensor(
                    ch(t2, c), ch(a_, c), eu[:, c:c + 1],
                    chp(Eb, c)[:, 0:F], ALU.mult, ALU.add), aux=(c == 0),
                    inc="s_V" if c == NCH - 1 else None)
            m_den = cnt["s_V"]

            emit("vector", lambda v: v.scalar_tensor_tensor(
                t2[:], rb_[:], 1.0, t2[:], ALU.add, ALU.mult),
                waits=[W("s_S", m_rev)], inc="s_V", aux=True)
            m_den2 = cnt["s_V"]

            emit("scalar", lambda s: s.activation(t2[:], t2[:], AF.Ln),
                 waits=[W("s_V", m_den2)])
            def rec_f(s):
                s.nop()
                return s.activation(t2[:], t2[:], AF.Exp, scale=-1.0)
            emit("scalar", rec_f, inc="s_S")
            m_rec = cnt["s_S"]

            wts = [W("s_S", m_rec)]
            if ti >= 1:
                wts.append(W("s_T", marks[("womm", ti - 1)]))
            emit("vector", lambda v: v.tensor_tensor(ybuf[:], t1[:], t2[:],
                                                     ALU.mult),
                 waits=wts, inc="s_V")
            m_y = cnt["s_V"]

            for p in range(4):
                bank = 4 + (p % 2)
                wts = []
                if p == 0:
                    wts = [W("s_V", m_y)]
                if p >= 2:
                    wts.append(W("s_S", marks[("oev", ti, p - 2)]))
                emit("tensor", proj_pair(wo, ybuf, p, bank), waits=wts, inc="s_T")
                mme = cnt["s_T"]
                emit("scalar", lambda s, p=p, bank=bank: s.activation(
                    pr2(t2, p), ps[bank][:], AF.Copy),
                    waits=[W("s_T", mme)], inc="s_S")
                marks[("oev", ti, p)] = cnt["s_S"]
            m_oev = cnt["s_S"]
            marks[("womm", ti)] = cnt["s_T"]
            for c in range(NCH):
                wts = []
                if c == 0:
                    wts = [W("s_S", m_oev)]
                    if ti >= 1:
                        wts.append(Wc("s_ao"))
                emit("vector", lambda v, c=c, xb_=xb_:
                     v.scalar_tensor_tensor(
                         ch(aab, c), ch(t2, c), gav[:, c:c + 1], ch(xb_, c),
                         ALU.mult, ALU.add), waits=wts, aux=(c == 0),
                     inc="s_V" if c == NCH - 1 else None)
            m_aa = cnt["s_V"]
            x_rel[ti] = m_aa

            emit("sync", lambda e, ti=ti: e.dma_start(
                out=_ap(out_hbm, ti * F, [[T, P], [P * T, NCH], [1, F]]),
                in_=aab[:]), waits=[W("s_V", m_aa)], inc="s_ao", incn=16)

            emit("scalar", lambda s: s.activation(xbb[:], aab[:], AF.Copy),
                 waits=[W("s_V", m_aa)], inc="s_S")
            emit("scalar", lambda s: s.activation(sqb[:], aab[:], AF.Square),
                 inc="s_S")
            m_casts2 = cnt["s_S"]
            wts = [W("s_S", m_casts2)]
            if ti >= 1:
                wts.append(W("s_V", marks[("var2", ti - 1)]))
            emit("tensor", stat_mms(6, xbb), waits=wts, inc="s_T")
            m_sx2 = cnt["s_T"]
            emit("tensor", stat_mms(7, sqb), inc="s_T")
            m_ssq2 = cnt["s_T"]
            emit("scalar", lambda s: s.activation(mu_b[:], ps[6][:, 0:F], AF.Copy),
                 waits=[W("s_T", m_sx2)])
            def musq_f(s):
                s.nop()
                return s.activation(va_b[:], mu_b[:], AF.Square)
            emit("scalar", musq_f, inc="s_S")
            m_mu2 = cnt["s_S"]
            emit("vector", var_chain2(LN_EPS_A, 7, va_b),
                 waits=[W("s_S", m_mu2), W("s_T", m_ssq2)], inc="s_V")
            m_var2 = cnt["s_V"]
            emit("scalar", lambda s: s.activation(va_b[:], va_b[:], AF.Ln),
                 waits=[W("s_V", m_var2)])
            def iv_f(s):
                s.nop()
                return s.activation(iv_b[:], va_b[:], AF.Exp, scale=-0.5)
            emit("scalar", iv_f, inc="s_S")
            m_iv2 = cnt["s_S"]
            emit("vector", lambda v: v.tensor_tensor(
                full3(t1), full3(aab), mu_full, ALU.subtract),
                waits=[W("s_S", m_mu2)])
            emit("vector", lambda v: v.tensor_tensor(
                full3(t1), full3(t1), iv_full, ALU.mult),
                waits=[W("s_S", m_iv2)], inc="s_V")
            m_z2 = cnt["s_V"]

            def h2_aff(c):
                def f(s):
                    s.nop()
                    return s.activation(ch(t2, c), ch(t1, c), AF.Identity,
                                        scale=Bb[:, c:c + 1],
                                        bias=sbv_v[:, c:c + 1])
                return f
            for c in range(NCH):
                emit("scalar", h2_aff(c), waits=[W("s_V", m_z2)] if c == 0 else (),
                     inc="s_S" if c == NCH - 1 else None)

            emit("scalar", lambda s: s.activation(xbb[:], t2[:], AF.Copy),
                 inc="s_S")
            emit("scalar", lambda s: s.activation(sqb[:], t2[:], AF.Square),
                 inc="s_S")
            m_casts3 = cnt["s_S"]
            emit("tensor", stat_mms(6, xbb),
                 waits=[W("s_S", m_casts3), W("s_V", m_var2)], inc="s_T")
            m_sx3 = cnt["s_T"]
            emit("tensor", stat_mms(7, sqb), inc="s_T")
            m_ssq3 = cnt["s_T"]
            marks[("ssq3", ti)] = m_ssq3
            emit("scalar", lambda s: s.activation(mu_b[:], ps[6][:, 0:F], AF.Copy),
                 waits=[W("s_T", m_sx3)])
            emit("scalar", musq_f, inc="s_S")
            m_mu3 = cnt["s_S"]
            emit("vector", var_chain2(LN_EPS_2, 7, va_b),
                 waits=[W("s_S", m_mu3), W("s_T", m_ssq3)], inc="s_V")
            marks[("var2", ti)] = cnt["s_V"]
            emit("scalar", lambda s: s.activation(va_b[:], va_b[:], AF.Ln),
                 waits=[W("s_V", marks[("var2", ti)])])
            emit("scalar", iv_f, inc="s_S")
            m_iv3 = cnt["s_S"]
            emit("vector", lambda v: v.tensor_tensor(
                full3(t1), full3(t2), mu_full, ALU.subtract),
                waits=[W("s_S", m_mu3)])
            emit("vector", lambda v: v.tensor_tensor(
                full3(t1), full3(t1), iv_full, ALU.mult),
                waits=[W("s_S", m_iv3)], inc="s_V")
            marks[("z3", ti)] = cnt["s_V"]
            m_z3 = cnt["s_V"]

            def xn_aff(c, xnb):
                def f(s):
                    s.nop()
                    return s.activation(ch(xnb, c), ch(t1, c), AF.Identity,
                                        scale=vecs[:, 48 + c:49 + c],
                                        bias=vecs[:, 56 + c:57 + c])
                return f
            for c in range(NCH):
                wts = []
                if c == 0:
                    wts = [W("s_V", m_z3)]
                    if ti >= 2:
                        wts.append(Wc("s_no"))
                emit("scalar", xn_aff(c, xnb), waits=wts,
                     inc="s_S" if c == NCH - 1 else None)
            m_xn = cnt["s_S"]
            marks[("xnaff", ti)] = m_xn

            emit("vector", lambda v, xnb=xnb, ti=ti: v.tensor_reduce(
                maxacc[:, ti:ti + 1], xnb[:], mybir.AxisListType.X, ALU.max,
                apply_absolute_value=True), waits=[W("s_S", m_xn)], inc="s_V")
            marks[("mx", ti)] = cnt["s_V"]

            emit("sync", lambda e, xnb=xnb, ti=ti: e.dma_start(
                out=_ap(xn_hbm, ti * F, [[T, P], [P * T, NCH], [1, F]]),
                in_=xnb[:]), waits=[W("s_S", m_xn)], inc="s_no", incn=16)

            if ti + 2 < NT:
                emit("sync", xload(ti + 2), waits=[W("s_V", x_rel[ti])],
                     inc="s_x", incn=16)

        for it in range(NT + 1):
            if it < NT:
                emit_pre(it)
            if it >= 1:
                emit_post(it - 1)
        m_xn_last = marks[("xnaff", NT - 1)]

        m_mx_red = marks[("mx", NT - 1)]
        emit("vector", lambda v: v.tensor_reduce(
            mxc[:], maxacc[:], mybir.AxisListType.X, ALU.max),
            waits=[W("s_V", m_mx_red)], inc="s_V", aux=True)
        m_mx_final = cnt["s_V"]
        emit("sync", lambda e: e.dma_start(
            out=mx_hbm, in_=mxc[:], allow_slow_non_contiguous=True),
            waits=[W("s_V", m_mx_final), W("s_S", cnt["s_S"]),
                   W("s_T", cnt["s_T"])], inc="s_x", incn=16)
        m_gx1 = cnt["s_x"]
        emit("sync", lambda e: e.dma_start(out=mxrow[:], in_=mx_hbm),
             waits=[W("s_x", m_gx1)], inc="s_x", incn=16)
        m_gx2 = cnt["s_x"]
        emit("vector", lambda v: v.tensor_reduce(
            gsc[0:1, 0:1], mxrow[:], mybir.AxisListType.X, ALU.max),
            waits=[W("s_x", m_gx2)], inc="s_V", aux=True)
        m_gmax = cnt["s_V"]
        emit("gpsimd", lambda g: g.dma_start(out=cc_in, in_=gsc[:]),
             waits=[W("s_V", m_gmax)], inc="s_G", incn=16)
        m_g3 = cnt["s_G"]
        emit("gpsimd", lambda g: g.collective_compute(
            "AllReduce", ALU.max, replica_groups=[list(range(B))],
            ins=[cc_in], outs=[cc_out]), waits=[W("s_G", m_g3)], inc="s_cc")
        m_cc = cnt["s_cc"]
        emit("gpsimd", lambda g: g.dma_start(out=gsc[:], in_=cc_out),
             waits=[W("s_cc", m_cc)], inc="s_G", incn=16)
        m_g4 = cnt["s_G"]
        emit("sync", lambda e: e.dma_start(
            out=gcol[:], in_=_ap(gsc[0:1, 0:1], 0, [[64, 1], [0, P]]),
            allow_slow_non_contiguous=True),
            waits=[W("s_G", m_g4)], inc="s_x", incn=16)
        m_g5 = cnt["s_x"]
        emit("vector", lambda v: v.reciprocal(sgam[:], gcol[:]),
             waits=[W("s_x", m_g5)], aux=True)
        emit("vector", lambda v: v.tensor_scalar(sgam[:], sgam[:], QB, None,
                                                 ALU.mult), inc="s_V", aux=True)
        m_sg = cnt["s_V"]

        wq_parts = (xk, xv, xr, xbb)

        def wq_part(c):
            return wq_parts[c // 2], (c % 2) * D

        m_z3_last = marks[("z3", NT - 1)]
        for q in range(4):
            wts = ([W("s_V", m_z3_last), W("s_S", m_xn_last),
                    W("s_T", marks[("ssq3", NT - 1)])] if q == 0
                   else [W("s_S", marks[("wabs", q - 1)])])
            emit("gpsimd", lambda g, q=q: g.dma_start(
                out=t1[:], in_=_ap(ffnw_hbm, 2 * q * P * D,
                                   [[D, P], [P * D, 2], [1, D]])),
                waits=wts, inc="s_G", incn=16)
            m_ld = cnt["s_G"]
            swts = [W("s_G", m_ld)]
            if q == 0:
                swts.append(W("s_V", m_mx_final))
            else:
                swts.append(W("s_T", marks[("bmm", q - 1)]))
            emit("scalar", lambda s: s.activation(sqb[:], t1[:], AF.Abs),
                 waits=swts, inc="s_S")
            m_abs = cnt["s_S"]
            marks[("wabs", q)] = m_abs

            def beta_mms(q):
                def f(t):
                    last = None
                    bank = (2, 3, 6, 7)[q]
                    for half in range(4):
                        last = t.matmul(ps[bank][:, 0:512], ones_t[:],
                                        sqb[:, half * 512:(half + 1) * 512],
                                        start=(half == 0), stop=(half == 3))
                    return last
                return f
            emit("tensor", beta_mms(q), waits=[W("s_S", m_abs)], inc="s_T")
            marks[("bmm", q)] = cnt["s_T"]
        m_bmm_all = cnt["s_T"]
        emit("vector", lambda v: v.tensor_copy(bacc[:], ps[2][:, 0:512]),
             waits=[W("s_T", m_bmm_all)])
        for bank in (3, 6, 7):
            emit("vector", lambda v, bank=bank: v.tensor_tensor(
                bacc[:], bacc[:], ps[bank][:, 0:512], ALU.add),
                inc="s_V" if bank == 7 else None)
        marks[("bev", 3)] = cnt["s_V"]
        def betasum_f(s):
            s.nop()
            return s.activation(t2[:, 0:512], bacc[:], AF.Copy,
                                accum_out=t2[:, 1024:1025])
        emit("scalar", betasum_f, waits=[W("s_V", marks[("bev", 3)])], inc="s_S")
        m_bsum_s = cnt["s_S"]
        emit("vector", lambda v: v.tensor_scalar(
            bcol[:], t2[:, 1024:1025], 1.0 / 1024.0, None, ALU.mult),
            waits=[W("s_S", m_bsum_s)], aux=True)
        emit("vector", lambda v: v.tensor_scalar(
            rbcol[:], bcol[:], 1e-5, None, ALU.add), inc="s_V", aux=True)
        m_g6 = cnt["s_V"]
        emit("vector", lambda v: v.reciprocal(rbcol[:], rbcol[:]),
             inc="s_V", aux=True)
        m_rb = cnt["s_V"]

        for q in range(4):
            wts = ([W("s_V", m_rb)] if q == 0
                   else [W("s_V", marks[("wqq", q - 1)])])
            emit("gpsimd", lambda g, q=q: g.dma_start(
                out=t1[:], in_=_ap(ffnw_hbm, 2 * q * P * D,
                                   [[D, P], [P * D, 2], [1, D]])),
                waits=wts, inc="s_G", incn=16)
            m_ld2 = cnt["s_G"]

            def quant(q):
                def f(v):
                    v.nop()
                    v.tensor_scalar(t1[:], t1[:], rbcol[:, 0:1], None, ALU.mult)
                    v.nop()
                    v.tensor_scalar(t2[:], t1[:], 0.5, None, ALU.is_gt)
                    v.nop()
                    v.tensor_scalar(t1[:], t1[:], -0.5, None, ALU.is_lt)
                    v.nop()
                    part, _ = wq_part(2 * q)
                    return v.tensor_tensor(part[:, 0:2 * D], t2[:], t1[:],
                                           ALU.subtract)
                return f
            emit("vector", quant(q), waits=[W("s_G", m_ld2)], inc="s_V")
            marks[("wqq", q)] = cnt["s_V"]
        m_wq = cnt["s_V"]

        emit("vector", lambda v: v.tensor_scalar(
            mxc[:], bcol[:], 1.0 / QB, None, ALU.mult),
            waits=[W("s_V", m_g6), W("s_V", m_sg)], inc="s_V", aux=True)
        m_b128 = cnt["s_V"]

        def gbs1_f(s):
            s.nop()
            return s.activation(rbcol2[:], mxc[:], AF.Identity,
                                scale=gcol[:, 0:1])
        emit("scalar", gbs1_f, waits=[W("s_V", m_b128)], inc="s_S")

        def gbs2_f(s):
            s.nop()
            return s.activation(gbs[:], gbv[:], AF.Identity,
                                scale=rbcol2[:, 0:1])
        emit("scalar", gbs2_f, inc="s_S")
        m_gbs = cnt["s_S"]

        def xnload(ti, wts):
            def f(e):
                return e.dma_start(
                    out=xnB[ti % 2][:],
                    in_=_ap(xn_hbm, ti * F, [[T, P], [P * T, NCH], [1, F]]))
            return f, wts
        f0, w0 = xnload(0, [Wc("s_no"), W("s_V", m_wq)])
        emit("sync", f0, waits=w0, inc="s_x", incn=16)
        base_px = cnt["s_x"]
        f1, w1 = xnload(1, [Wc("s_no"), W("s_V", m_wq)])
        emit("sync", f1, waits=w1, inc="s_x", incn=16)

        for ti in range(NT):
            xnb = xnB[ti % 2]
            term = aab if ti % 2 == 0 else t2
            qb = sqb if ti % 2 == 0 else sqb2
            wts = [Wc("s_x"), W("s_S", m_gbs)]
            if ti >= 2:
                wts.append(W("s_T", marks[("fmm", ti - 2)]))
            emit("vector", lambda v, xnb=xnb: v.tensor_scalar(
                t1[:], xnb[:], sgam[:, 0:1], 8388608.0, ALU.mult, ALU.add),
                waits=wts, aux=True)
            emit("vector", lambda v, qb=qb: v.tensor_scalar(
                qb[:], t1[:], -8388608.0, CLIP, ALU.add, ALU.min))
            emit("vector", lambda v, qb=qb: v.tensor_scalar(
                qb[:], qb[:], -CLIP, None, ALU.max), inc="s_V")
            m_xq = cnt["s_V"]
            marks[("xq", ti)] = m_xq

            def fmm(e, bank):
                def f(t, qb=qb):
                    last = None
                    for c in range(NCH):
                        part, off = wq_part(c)
                        last = t.matmul(ps[bank][:, 0:F],
                                        part[:, off + e * P:off + (e + 1) * P],
                                        ch(qb, c), start=(c == 0),
                                        stop=(c == NCH - 1))
                    return last
                return f
            for e in range(NCH):
                bank = 2 + (e % 2)
                wts = [W("s_V", m_xq)] if e == 0 else []
                if e < 2 and ti >= 1:
                    wts.append(W("s_S", marks[("fev", ti - 1, 6 + e)]))
                if e >= 2:
                    wts.append(W("s_S", marks[("fev", ti, e - 2)]))
                emit("tensor", fmm(e, bank), waits=wts, inc="s_T")
                mme = cnt["s_T"]
                swts = [W("s_T", mme)]
                if ti == 0 and e == 0:
                    swts.append(Wc("s_ao"))
                if ti >= 2 and e == 0:
                    swts.append(Wc("s_fo"))
                def fev_f(s, e=e, bank=bank, term=term):
                    s.nop()
                    return s.activation(ch(term, e), ps[bank][:, 0:F], AF.Identity,
                                        scale=gbs[:, e:e + 1])
                emit("scalar", fev_f, waits=swts, inc="s_S")
                marks[("fev", ti, e)] = cnt["s_S"]
            marks[("fmm", ti)] = cnt["s_T"]
            m_term = cnt["s_S"]

            emit("gpsimd", lambda g, term=term, ti=ti: g.dma_start(
                out=_ap(out_hbm, ti * F, [[T, P], [P * T, NCH], [1, F]]),
                in_=term[:], accum_op=ALU.add),
                waits=[W("s_S", m_term)], inc="s_fo", incn=16)
            marks[("acc", ti)] = cnt["s_fo"]

            if ti + 2 < NT:
                fN, wN = xnload(ti + 2, [Wc("s_no"),
                                         W("s_V", marks[("xq", ti)])])
                emit("sync", fN, waits=wN, inc="s_x", incn=16)

        emit("gpsimd", lambda g: g.wait_ge(sems["s_fo"], cnt["s_fo"]))

        def replay(eng_name):
            def runner(eng):
                for fn, waits, inc, incn, aux in sched[eng_name]:
                    for (sem, val) in waits:
                        eng.wait_ge(sem, val)
                    if aux:
                        eng.nop()
                    instr = fn(eng)
                    if inc is not None:
                        assert instr is not None, \
                            f"{eng_name} op returned None with inc"
                        instr.then_inc(sems[inc], incn)
            return runner

        block.sync(replay("sync"))
        block.scalar(replay("scalar"))
        block.vector(replay("vector"))
        block.tensor(replay("tensor"))
        block.gpsimd(replay("gpsimd"))

    return nc


def kernel(**inputs):
    x = np.ascontiguousarray(inputs["L0_attn_x"])
    wkT = np.ascontiguousarray(inputs["L0_attn_W_k"].T)
    wvT = np.ascontiguousarray(inputs["L0_attn_W_v"].T)
    wrT = np.ascontiguousarray(inputs["L0_attn_W_r"].T)
    woT = np.ascontiguousarray(inputs["L0_attn_W_o"].T)
    ffnw = np.ascontiguousarray(inputs["L0_ffn_weight"])
    wmod = np.ascontiguousarray(inputs["L0_wmod"])
    crow = inputs["L0_c"]

    def vec8(v):
        return np.ascontiguousarray(np.reshape(np.asarray(v), (NCH, P)).T)

    nc = build_nc()
    in_maps = []
    for b in range(B):
        vecs = np.zeros((P, 64), np.float32)
        vecs[:, 0:8] = vec8(crow[b])
        vecs[:, 8:16] = vec8(inputs["L0_attn_time_mix_k"])
        vecs[:, 16:24] = vec8(inputs["L0_attn_time_mix_v"])
        vecs[:, 24:32] = vec8(inputs["L0_attn_time_mix_r"])
        vecs[:, 32:40] = vec8(inputs["L0_attn_time_decay"])
        vecs[:, 40:48] = vec8(inputs["L0_attn_time_first"])
        vecs[:, 48:56] = vec8(inputs["L0_ffn_ln_gamma"])
        vecs[:, 56:64] = vec8(inputs["L0_ffn_ln_beta"])
        in_maps.append({
            "x": np.ascontiguousarray(x[b].T),
            "wkT": wkT, "wvT": wvT, "wrT": wrT, "woT": woT,
            "ffnw": ffnw, "wmod": wmod, "vecs": vecs,
            "ident": np.eye(P, dtype=np.float32),
        })
    res = run_bass_kernel_spmd(nc, in_maps, core_ids=list(range(B)))
    out = np.stack([np.ascontiguousarray(r["out"].T) for r in res.results])
    return out


# version 17
# speedup vs baseline: 1.0801x; 1.0068x over previous
import contextlib
import numpy as np
import ml_dtypes
import concourse.bass as bass
import concourse.mybir as mybir

from concourse.bass_utils import run_bass_kernel_spmd

F32 = mybir.dt.float32
BF16 = mybir.dt.bfloat16
AF = mybir.ActivationFunctionType
ALU = mybir.AluOpType

P = 128
D = 1024
NCH = D // P
T = 8192
F = 256
NT = T // F
B = 8
LN_EPS_A = 1e-5
LN_EPS_2 = 1e-3
QB = 128.0
CLIP = QB - 1e-5


def _ap(t_ap, offset, dims):
    return bass.AP(t_ap.tensor, offset, dims)


def build_nc():
    nc = bass.Bass("TRN2", target_bir_lowering=False, detect_race_conditions=False)

    x_hbm = nc.dram_tensor("x", [D, T], F32, kind="ExternalInput").ap()
    wk_hbm = nc.dram_tensor("wkT", [D, D], F32, kind="ExternalInput").ap()
    wv_hbm = nc.dram_tensor("wvT", [D, D], F32, kind="ExternalInput").ap()
    wr_hbm = nc.dram_tensor("wrT", [D, D], F32, kind="ExternalInput").ap()
    wo_hbm = nc.dram_tensor("woT", [D, D], F32, kind="ExternalInput").ap()
    ffnw_hbm = nc.dram_tensor("ffnw", [D, D], F32, kind="ExternalInput").ap()
    wmod_hbm = nc.dram_tensor("wmod", [D, 6 * D], F32, kind="ExternalInput").ap()
    vecs_hbm = nc.dram_tensor("vecs", [P, 64], F32, kind="ExternalInput").ap()
    ident_hbm = nc.dram_tensor("ident", [P, P], F32, kind="ExternalInput").ap()
    out_hbm = nc.dram_tensor("out", [D, T], F32, kind="ExternalOutput").ap()

    xn_hbm = nc.dram_tensor("xn_scr", [D, T], BF16).ap()
    mx_hbm = nc.dram_tensor("mx_scr", [1, P], F32).ap()
    cc_in_t = nc.dram_tensor("cc_in", [1, 64], F32)
    cc_out_t = nc.dram_tensor("cc_out", [1, 64], F32)
    cc_in = cc_in_t.ap().opt()
    cc_out = cc_out_t.ap().opt()

    es = contextlib.ExitStack()
    with es:
        sb = lambda name, shape, dt: es.enter_context(nc.sbuf_tensor(name, shape, dt))
        wk = sb("wk", [P, NCH * D], BF16)
        wv = sb("wv", [P, NCH * D], BF16)
        wr = sb("wr", [P, NCH * D], BF16)
        wo = sb("wo", [P, NCH * D], BF16)
        xB = [sb(f"x{i}", [P, NCH * F], F32) for i in range(2)]
        xbb = sb("xbb", [P, NCH * F], BF16)
        sqb = sb("sqb", [P, NCH * F], BF16)
        sqb2 = sb("sqb2", [P, NCH * F], BF16)
        h = sb("h", [P, NCH * (F + 1)], BF16)
        hd = sb("hd", [P, NCH * F], BF16)
        xk = sb("xk", [P, NCH * F], BF16)
        xv = sb("xv", [P, NCH * F], BF16)
        xr = sb("xr", [P, NCH * F], BF16)
        aP = [sb(f"a{i}", [P, NCH * F], BF16) for i in range(2)]
        cbP = [sb(f"cb{i}", [P, NCH * F], BF16) for i in range(2)]
        rbP = [sb(f"rbx{i}", [P, NCH * F], BF16) for i in range(2)]
        vvP = [sb(f"vv{i}", [P, NCH * F], BF16) for i in range(2)]
        Nb = sb("Nb", [P, NCH * (F + 1)], F32)
        Eb = sb("Eb", [P, NCH * (F + 1)], F32)
        t1 = sb("t1", [P, NCH * F], F32)
        t2 = sb("t2", [P, NCH * F], F32)
        aab = sb("aa0", [P, NCH * F], F32)
        xnB = [sb(f"xn{i}", [P, NCH * F], BF16) for i in range(2)]
        mu_b = sb("mu_b", [P, F], F32)
        iv_b = sb("iv_b", [P, F], F32)
        va_b = sb("va_b", [P, F], F32)
        mu_1 = sb("mu_1", [P, F], F32)
        iv_1 = sb("iv_1", [P, F], F32)
        va_1 = sb("va_1", [P, F], F32)
        ones_t = sb("ones_t", [P, P], BF16)
        identb = sb("identb", [P, P], BF16)
        mxb = sb("mxb", [P, P], BF16)
        bacc = sb("bacc", [P, 512], F32)
        vecs = sb("vecs_sb", [P, 64], F32)
        scol = sb("scol", [P, 8], BF16)
        lam = sb("lam", [P, 8], F32)
        eu = sb("eu", [P, 8], F32)
        modv = sb("modv", [P, 48], F32)
        Aa = sb("Aa", [P, 8], F32)
        Bb = sb("Bb", [P, 8], F32)
        gav = sb("gav", [P, 8], F32)
        gbv = sb("gbv", [P, 8], F32)
        gbs = sb("gbs", [P, 8], F32)
        cpool = sb("cpool", [P, 2], F32)
        mxrow = sb("mxrow", [1, P], F32)
        gsc = sb("gsc", [1, 64], F32)
        gcol = sb("gcol", [P, 1], F32)
        sgam = sb("sgam", [P, 1], F32)
        bcol = sb("bcol", [P, 1], F32)
        rbcol = sb("rbcol", [P, 1], F32)
        rbcol2 = sb("rbcol2", [P, 1], F32)
        bsum = sb("bsum", [1, 2], F32)
        maxacc = sb("maxacc", [P, NT], F32)
        mxc = sb("mxc", [P, 1], F32)
        tiny = sb("tinyv", [P, 8], F32)
        ps = [es.enter_context(nc.psum_tensor(f"ps{i}", [P, 512], F32)) for i in range(8)]

        nc.const_aps.register(F32, -1.0, cpool[:, 0:1])
        nc.const_aps.register(F32, -0.5, cpool[:, 1:2])

        sems = {}
        for name in ["s_x", "s_w", "s_S", "s_V", "s_T", "s_G", "s_ao", "s_no",
                     "s_cc", "s_fo", "s_Gc"]:
            sems[name] = es.enter_context(nc.semaphore(name))
        cnt = {k: 0 for k in sems}
        block = es.enter_context(nc.Block())

        sched = {k: [] for k in ["sync", "scalar", "vector", "tensor", "gpsimd"]}

        def emit(eng, fn, waits=(), inc=None, incn=1, aux=False):
            waits = list(waits)
            if inc is not None and incn == 16:
                waits.insert(0, (sems[inc], cnt[inc]))
            sched[eng].append((fn, tuple(waits), inc, incn, aux))
            if inc is not None:
                cnt[inc] += incn
            return cnt[inc] if inc else None

        def W(name, n):
            return (sems[name], n)

        def Wc(name):
            return (sems[name], cnt[name])

        def ch(buf, c, width=F):
            return buf[:, c * width:(c + 1) * width]

        def chp(buf, c):
            base = c * (F + 1)
            return buf[:, base:base + F + 1]

        marks = {}

        def full3(buf):
            return _ap(buf[:, 0:1], 0, [[NCH * F, P], [F, NCH], [1, F]])

        emit("gpsimd", lambda g: g.memset(cpool[:, 0:1], -1.0))
        emit("gpsimd", lambda g: g.memset(cpool[:, 1:2], -0.5))
        emit("gpsimd", lambda g: g.memset(ones_t[:], 1.0 / 1024.0))
        emit("gpsimd", lambda g: g.memset(h[:], 0.0))
        emit("gpsimd", lambda g: g.memset(Nb[:], 0.0))
        emit("gpsimd", lambda g: g.memset(Eb[:], 0.0))
        emit("gpsimd", lambda g: g.memset(maxacc[:], 0.0))
        emit("gpsimd", lambda g: g.memset(gsc[:], 0.0), inc="s_Gc")
        m_const = cnt["s_Gc"]
        emit("gpsimd", lambda g: g.dma_start(out=vecs[:], in_=vecs_hbm),
             inc="s_w", incn=16)
        m_vecs = cnt["s_w"]

        emit("scalar", lambda s: s.activation(scol[:], vecs[:, 0:8], AF.Silu),
             waits=[Wc("s_w"), W("s_Gc", m_const)], inc="s_S")
        emit("scalar", lambda s: s.activation(lam[:], vecs[:, 32:40], AF.Exp),
             inc="s_S")
        emit("scalar", lambda s: s.activation(eu[:], vecs[:, 40:48], AF.Exp),
             inc="s_S")
        m_scol = cnt["s_S"]

        def wload(dst, src):
            def f(g):
                return g.dma_start(
                    out=dst[:], in_=_ap(src, 0, [[D, P], [P * D, NCH], [1, D]]))
            return f
        emit("gpsimd", lambda g: g.dma_start(out=identb[:], in_=ident_hbm),
             inc="s_w", incn=16)
        emit("gpsimd", wload(wk, wk_hbm), inc="s_w", incn=16)
        emit("gpsimd", wload(wv, wv_hbm), inc="s_w", incn=16)
        emit("gpsimd", wload(wr, wr_hbm), inc="s_w", incn=16)
        emit("gpsimd", wload(wo, wo_hbm), inc="s_w", incn=16)
        m_wall = cnt["s_w"]

        stageA = (xk, xv, xr)
        stageB = (xbb, sqb, xnB[0])

        def wmod_load1(bufp, c, i):
            def f(g):
                return g.dma_start(
                    out=bufp[:, 0:2 * D],
                    in_=_ap(wmod_hbm, c * P * 6 * D + i * 2 * D,
                            [[6 * D, P], [1, 2 * D]]))
            return f

        def mod_mm(stage, c, j):
            def f(t):
                which = (j * P) // (2 * D)
                off = (j * P) % (2 * D)
                stat = stage[which][:, off:off + P]
                return t.matmul(ps[4][:, j:j + 1], stat, scol[:, c:c + 1],
                                start=(c == 0 and j == 0),
                                stop=(c == NCH - 1 and j == 47),
                                skip_group_check=True)
            return f

        for c in range(NCH):
            stg = stageA if c % 2 == 0 else stageB
            wts = []
            if c >= 2:
                wts = [W("s_T", marks[("modmm", c - 2)])]
            emit("gpsimd", wmod_load1(stg[0], c, 0), waits=wts, inc="s_w", incn=16)
            emit("gpsimd", wmod_load1(stg[1], c, 1), inc="s_w", incn=16)
            emit("gpsimd", wmod_load1(stg[2], c, 2), inc="s_w", incn=16)
            m_wmod_c = cnt["s_w"]
            for j in range(48):
                wts = [W("s_w", m_wmod_c)] if j == 0 else ()
                if c == 0 and j == 0:
                    wts = [W("s_w", m_wmod_c), W("s_S", m_scol)]
                emit("tensor", mod_mm(stg, c, j), waits=wts,
                     inc="s_T" if j == 47 else None)
            marks[("modmm", c)] = cnt["s_T"]
        m_modmm = cnt["s_T"]

        emit("scalar", lambda s: s.activation(modv[:], ps[4][:, 0:48], AF.Copy),
             waits=[W("s_T", m_modmm)], inc="s_S")
        m_modv = cnt["s_S"]
        sa_v = modv[:, 0:8]
        ca_v = modv[:, 8:16]
        ga_v = modv[:, 16:24]
        sbv_v = modv[:, 24:32]
        cb_v = modv[:, 32:40]
        gb_v = modv[:, 40:48]

        GW = 256
        emit("scalar", lambda s: s.activation(t1[:, 0:48], modv[:], AF.Copy),
             waits=[W("s_S", m_modv)], inc="s_S")
        m_gcp = cnt["s_S"]

        def gatewide(v):
            g0 = t1[:, 0:GW]
            g1 = t1[:, GW:2 * GW]
            g2 = t1[:, 2 * GW:3 * GW]
            v.tensor_scalar(g1, g0, 0.0, None, ALU.is_ge)
            v.tensor_scalar(g1, g1, 2.0, -1.0, ALU.mult, ALU.add)
            v.tensor_tensor(g2, g0, g1, ALU.mult)
            v.tensor_scalar(g2, g2, 0.25, None, ALU.max)
            return v.tensor_tensor(g2, g1, g2, ALU.mult)
        emit("vector", gatewide, waits=[W("s_S", m_gcp)], inc="s_V")
        m_gw = cnt["s_V"]
        emit("scalar", lambda s: s.activation(
            gav[:], t1[:, 2 * GW + 16:2 * GW + 24], AF.Copy),
            waits=[W("s_V", m_gw)])
        emit("scalar", lambda s: s.activation(
            gbv[:], t1[:, 2 * GW + 40:2 * GW + 48], AF.Copy), inc="s_S")
        m_gates = cnt["s_S"]
        emit("vector", lambda v: v.tensor_scalar(Aa[:], ca_v, 1.0, None, ALU.add),
             waits=[W("s_S", m_gates)], aux=True)
        emit("vector", lambda v: v.tensor_scalar(Bb[:], cb_v, 1.0, None, ALU.add),
             inc="s_V", aux=True)
        m_mod_ready = cnt["s_V"]

        def xload(ti, extra=()):
            def f(e):
                return e.dma_start(
                    out=xB[ti % 2][:],
                    in_=_ap(x_hbm, ti * F, [[T, P], [P * T, NCH], [1, F]]))
            return f
        emit("sync", xload(0), inc="s_x", incn=16)
        emit("sync", xload(1), inc="s_x", incn=16)

        lamc = [_ap(lam[:, 0:1], c, [[8, P], [0, F]]) for c in range(NCH)]
        x_rel = {}
        aa_st = {}
        xn_st = {}
        st_aa = 0
        st_xn = 0

        def stat_mms(bank, src):
            def f(t):
                last = None
                for c in range(NCH):
                    last = t.matmul(ps[bank][:, 0:F], ones_t[:], ch(src, c),
                                    start=(c == 0), stop=(c == NCH - 1))
                return last
            return f

        def var_chain(eps):
            def f(v):
                v.tensor_tensor(va_b[:], ps[1][:, 0:F], va_b[:], ALU.subtract)
                return v.tensor_scalar(va_b[:], va_b[:], eps, None, ALU.add)
            return f

        mu_full = _ap(mu_b[:, 0:1], 0, [[F, P], [0, NCH], [1, F]])
        iv_full = _ap(iv_b[:, 0:1], 0, [[F, P], [0, NCH], [1, F]])
        assert mu_full is not None

        def proj_mms(wsb, src, e, bank):
            def f(t):
                last = None
                for c in range(NCH):
                    last = t.matmul(ps[bank][:, 0:F],
                                    wsb[:, c * D + e * P:c * D + (e + 1) * P],
                                    ch(src, c), start=(c == 0),
                                    stop=(c == NCH - 1))
                return last
            return f

        def proj_pair(wsb, src, p, bank):
            def f(t):
                last = None
                for half in range(2):
                    e = 2 * p + half
                    for c in range(NCH):
                        last = t.matmul(ps[bank][:, half * F:(half + 1) * F],
                                        wsb[:, c * D + e * P:c * D + (e + 1) * P],
                                        ch(src, c), start=(c == 0),
                                        stop=(c == NCH - 1))
                return last
            return f

        def pr2(buf, p):
            return buf[:, 2 * p * F:(2 * p + 2) * F]

        xbb1 = xr
        sqb1 = xv
        z1 = xk
        ybuf = sqb2

        def var_chain2(eps, bank, va_t):
            def f(v):
                v.tensor_tensor(va_t[:], ps[bank][:, 0:F], va_t[:], ALU.subtract)
                return v.tensor_scalar(va_t[:], va_t[:], eps, None, ALU.add)
            return f

        def emit_pre(ti):
            xb_ = xB[ti % 2]
            a_, vv_, rb_ = aP[ti % 2], vvP[ti % 2], rbP[ti % 2]
            wts = [W("s_x", 16 * (ti + 1))]
            if ti == 0:
                wts.append(W("s_V", m_mod_ready))
                wts.append(W("s_T", m_modmm))
            else:
                wts.append(W("s_T", marks[("rmm", ti - 1)]))
            emit("scalar", lambda s, xb_=xb_: s.activation(xbb1[:], xb_[:], AF.Copy),
                 waits=wts, inc="s_S")
            emit("scalar", lambda s, xb_=xb_: s.activation(sqb1[:], xb_[:],
                                                           AF.Square), inc="s_S")
            m_casts = cnt["s_S"]

            wts = [W("s_S", m_casts)]
            if ti > 0:
                wts.append(W("s_V", marks[("z1v", ti - 1)]))
            emit("tensor", stat_mms(0, xbb1), waits=wts, inc="s_T")
            m_sx = cnt["s_T"]
            emit("tensor", stat_mms(1, sqb1), inc="s_T")
            m_ssq = cnt["s_T"]

            emit("scalar", lambda s: s.activation(mu_1[:], ps[0][:, 0:F], AF.Copy),
                 waits=[W("s_T", m_sx)])
            def musq1_f(s):
                s.nop()
                return s.activation(va_1[:], mu_1[:], AF.Square)
            emit("scalar", musq1_f, inc="s_S")
            m_mu = cnt["s_S"]
            emit("vector", var_chain2(LN_EPS_A, 1, va_1),
                 waits=[W("s_S", m_mu), W("s_T", m_ssq)], inc="s_V")
            marks[("var1", ti)] = cnt["s_V"]
            emit("scalar", lambda s: s.activation(va_1[:], va_1[:], AF.Ln),
                 waits=[W("s_V", marks[("var1", ti)])])
            def iv1_f(s):
                s.nop()
                return s.activation(iv_1[:], va_1[:], AF.Exp, scale=-0.5)
            emit("scalar", iv1_f, inc="s_S")
            m_iv = cnt["s_S"]

            mu1_full = _ap(mu_1[:, 0:1], 0, [[F, P], [0, NCH], [1, F]])
            iv1_full = _ap(iv_1[:, 0:1], 0, [[F, P], [0, NCH], [1, F]])
            emit("vector", lambda v, xb_=xb_: v.tensor_tensor(
                full3(z1), full3(xb_), mu1_full, ALU.subtract),
                waits=[W("s_S", m_mu)])
            emit("vector", lambda v: v.tensor_tensor(
                full3(z1), full3(z1), iv1_full, ALU.mult),
                waits=[W("s_S", m_iv)], inc="s_V")
            m_z = cnt["s_V"]
            marks[("z1v", ti)] = m_z

            emit("vector", lambda v: v.tensor_copy(
                _ap(h[:, 0:1], 0, [[NCH * (F + 1), P], [F + 1, NCH], [1, 1]]),
                _ap(h[:, 0:1], F, [[NCH * (F + 1), P], [F + 1, NCH], [1, 1]])),
                inc="s_V")
            m_hcarry = cnt["s_V"]

            def h_aff(c):
                def f(s):
                    s.nop()
                    return s.activation(chp(h, c)[:, 1:F + 1], ch(z1, c),
                                        AF.Identity, scale=Aa[:, c:c + 1],
                                        bias=sa_v[:, c:c + 1])
                return f
            for c in range(NCH):
                emit("scalar", h_aff(c),
                     waits=[W("s_V", m_z), W("s_V", m_hcarry)] if c == 0 else (),
                     inc="s_S" if c == NCH - 1 else None)
            m_h = cnt["s_S"]

            for c in range(NCH):
                emit("vector", lambda v, c=c: v.tensor_tensor(
                    ch(hd, c), chp(h, c)[:, 1:F + 1], chp(h, c)[:, 0:F],
                    ALU.subtract), waits=[W("s_S", m_h)] if c == 0 else ())

            def lerp(dst, tmoff, c):
                def f(v):
                    return v.scalar_tensor_tensor(
                        ch(dst, c), ch(hd, c), vecs[:, tmoff + c:tmoff + c + 1],
                        chp(h, c)[:, 0:F], ALU.mult, ALU.add)
                return f
            for c in range(NCH):
                emit("vector", lerp(xk, 8, c), aux=(c == 0),
                     inc="s_V" if c == NCH - 1 else None)
            m_xk = cnt["s_V"]
            for c in range(NCH):
                emit("vector", lerp(xv, 16, c), aux=(c == 0),
                     inc="s_V" if c == NCH - 1 else None)
            m_xv = cnt["s_V"]
            for c in range(NCH):
                emit("vector", lerp(xr, 24, c), aux=(c == 0),
                     inc="s_V" if c == NCH - 1 else None)
            m_xr = cnt["s_V"]
            marks[("lerps", ti)] = m_xr

            kvr = [(wk, xk, a_, AF.Exp, None, m_xk, "kev", "kmm"),
                   (wv, xv, vv_, AF.Copy, None, m_xv, "vev", "vmm"),
                   (wr, xr, rb_, AF.Exp, -1.0, m_xr, "rev", "rmm")]
            g = 0
            for wsb, srcb, dstb, af, sc, m_src, evn, mmn in kvr:
                for p in range(4):
                    bank = 2 + (g % 2)
                    wts = []
                    if p == 0:
                        wts.append(W("s_V", m_src))
                        if evn == "kev":
                            wts.append(Wc("s_w"))
                            if ti >= 2:
                                wts.append(W("s_V", marks[("den2", ti - 2)]))
                    if g >= 2:
                        wts.append(marks[("kvrev", ti, g - 2)])
                    emit("tensor", proj_pair(wsb, srcb, p, bank), waits=wts,
                         inc="s_T")
                    mme = cnt["s_T"]
                    def ev_f(s, p=p, bank=bank, dstb=dstb, af=af, sc=sc):
                        if sc is None:
                            return s.activation(pr2(dstb, p), ps[bank][:], af)
                        return s.activation(pr2(dstb, p), ps[bank][:], af,
                                            scale=sc)
                    emit("scalar", ev_f, waits=[W("s_T", mme)], inc="s_S")
                    marks[("kvrev", ti, g)] = W("s_S", cnt["s_S"])
                    g += 1
                marks[(evn, ti)] = cnt["s_S"]
                marks[(mmn, ti)] = cnt["s_T"]

        def emit_post(ti):
            xb_ = xB[ti % 2]
            xnb = xnB[ti % 2]
            a_, vv_, rb_ = aP[ti % 2], vvP[ti % 2], rbP[ti % 2]
            cb_ = cbP[ti % 2]
            m_kev = marks[("kev", ti)]
            m_vev = marks[("vev", ti)]
            m_rev = marks[("rev", ti)]

            emit("vector", lambda v: v.tensor_tensor(cb_[:], a_[:], vv_[:],
                                                     ALU.mult),
                 waits=[W("s_S", m_vev)], inc="s_V")

            emit("vector", lambda v: v.tensor_copy(
                _ap(Nb[:, 0:1], 0, [[NCH * (F + 1), P], [F + 1, NCH], [1, 1]]),
                _ap(Nb[:, 0:1], F, [[NCH * (F + 1), P], [F + 1, NCH], [1, 1]])))
            emit("vector", lambda v: v.tensor_copy(
                _ap(Eb[:, 0:1], 0, [[NCH * (F + 1), P], [F + 1, NCH], [1, 1]]),
                _ap(Eb[:, 0:1], F, [[NCH * (F + 1), P], [F + 1, NCH], [1, 1]])))
            for c in range(NCH):
                emit("vector", lambda v, c=c: v.tensor_tensor_scan(
                    chp(Nb, c)[:, 1:F + 1], lamc[c], ch(cb_, c),
                    chp(Nb, c)[:, 0:1], ALU.mult, ALU.add), aux=True)
            for c in range(NCH):
                emit("vector", lambda v, c=c: v.tensor_tensor_scan(
                    chp(Eb, c)[:, 1:F + 1], lamc[c], ch(a_, c),
                    chp(Eb, c)[:, 0:1], ALU.mult, ALU.max),
                    waits=[W("s_S", m_kev)] if c == 0 else (), aux=True)
            for c in range(NCH):
                emit("vector", lambda v, c=c: v.scalar_tensor_tensor(
                    ch(t1, c), ch(cb_, c), eu[:, c:c + 1],
                    chp(Nb, c)[:, 0:F], ALU.mult, ALU.add), aux=(c == 0))
            for c in range(NCH):
                emit("vector", lambda v, c=c: v.scalar_tensor_tensor(
                    ch(t2, c), ch(a_, c), eu[:, c:c + 1],
                    chp(Eb, c)[:, 0:F], ALU.mult, ALU.add), aux=(c == 0),
                    inc="s_V" if c == NCH - 1 else None)
            m_den = cnt["s_V"]

            emit("vector", lambda v: v.scalar_tensor_tensor(
                t2[:], rb_[:], 1.0, t2[:], ALU.add, ALU.mult),
                waits=[W("s_S", m_rev)], inc="s_V", aux=True)
            m_den2 = cnt["s_V"]
            marks[("den2", ti)] = m_den2

            emit("scalar", lambda s: s.activation(t2[:], t2[:], AF.Ln),
                 waits=[W("s_V", m_den2)])
            def rec_f(s):
                s.nop()
                return s.activation(t2[:], t2[:], AF.Exp, scale=-1.0)
            emit("scalar", rec_f, inc="s_S")
            m_rec = cnt["s_S"]

            wts = [W("s_S", m_rec)]
            if ti >= 1:
                wts.append(W("s_T", marks[("womm", ti - 1)]))
            emit("vector", lambda v: v.tensor_tensor(ybuf[:], t1[:], t2[:],
                                                     ALU.mult),
                 waits=wts, inc="s_V")
            m_y = cnt["s_V"]

            for p in range(4):
                bank = 4 + (p % 2)
                wts = []
                if p == 0:
                    wts = [W("s_V", m_y)]
                if p >= 2:
                    wts.append(W("s_S", marks[("oev", ti, p - 2)]))
                emit("tensor", proj_pair(wo, ybuf, p, bank), waits=wts, inc="s_T")
                mme = cnt["s_T"]
                emit("scalar", lambda s, p=p, bank=bank: s.activation(
                    pr2(t2, p), ps[bank][:], AF.Copy),
                    waits=[W("s_T", mme)], inc="s_S")
                marks[("oev", ti, p)] = cnt["s_S"]
            m_oev = cnt["s_S"]
            marks[("womm", ti)] = cnt["s_T"]
            for c in range(NCH):
                wts = []
                if c == 0:
                    wts = [W("s_S", m_oev)]
                    if ti >= 1:
                        wts.append(Wc("s_ao"))
                emit("vector", lambda v, c=c, xb_=xb_:
                     v.scalar_tensor_tensor(
                         ch(aab, c), ch(t2, c), gav[:, c:c + 1], ch(xb_, c),
                         ALU.mult, ALU.add), waits=wts, aux=(c == 0),
                     inc="s_V" if c == NCH - 1 else None)
            m_aa = cnt["s_V"]
            x_rel[ti] = m_aa

            emit("sync", lambda e, ti=ti: e.dma_start(
                out=_ap(out_hbm, ti * F, [[T, P], [P * T, NCH], [1, F]]),
                in_=aab[:]), waits=[W("s_V", m_aa)], inc="s_ao", incn=16)

            emit("scalar", lambda s: s.activation(xbb[:], aab[:], AF.Copy),
                 waits=[W("s_V", m_aa)], inc="s_S")
            emit("scalar", lambda s: s.activation(sqb[:], aab[:], AF.Square),
                 inc="s_S")
            m_casts2 = cnt["s_S"]
            wts = [W("s_S", m_casts2)]
            if ti >= 1:
                wts.append(W("s_V", marks[("z3", ti - 1)]))
            emit("tensor", stat_mms(6, xbb), waits=wts, inc="s_T")
            m_sx2 = cnt["s_T"]
            emit("tensor", stat_mms(7, sqb), inc="s_T")
            m_ssq2 = cnt["s_T"]
            emit("scalar", lambda s: s.activation(mu_b[:], ps[6][:, 0:F], AF.Copy),
                 waits=[W("s_T", m_sx2)])
            def musq_f(s):
                s.nop()
                return s.activation(va_b[:], mu_b[:], AF.Square)
            emit("scalar", musq_f, inc="s_S")
            m_mu2 = cnt["s_S"]
            emit("vector", var_chain2(LN_EPS_A, 7, va_b),
                 waits=[W("s_S", m_mu2), W("s_T", m_ssq2)], inc="s_V")
            m_var2 = cnt["s_V"]
            emit("scalar", lambda s: s.activation(va_b[:], va_b[:], AF.Ln),
                 waits=[W("s_V", m_var2)])
            def iv_f(s):
                s.nop()
                return s.activation(iv_b[:], va_b[:], AF.Exp, scale=-0.5)
            emit("scalar", iv_f, inc="s_S")
            m_iv2 = cnt["s_S"]
            emit("vector", lambda v: v.tensor_tensor(
                full3(t1), full3(aab), mu_full, ALU.subtract),
                waits=[W("s_S", m_mu2)])
            emit("vector", lambda v: v.tensor_tensor(
                full3(t1), full3(t1), iv_full, ALU.mult),
                waits=[W("s_S", m_iv2)], inc="s_V")
            m_z2 = cnt["s_V"]

            def h2_aff(c):
                def f(s):
                    s.nop()
                    return s.activation(ch(t2, c), ch(t1, c), AF.Identity,
                                        scale=Bb[:, c:c + 1],
                                        bias=sbv_v[:, c:c + 1])
                return f
            for c in range(NCH):
                emit("scalar", h2_aff(c), waits=[W("s_V", m_z2)] if c == 0 else (),
                     inc="s_S" if c == NCH - 1 else None)

            emit("scalar", lambda s: s.activation(xbb[:], t2[:], AF.Copy),
                 inc="s_S")
            emit("scalar", lambda s: s.activation(sqb[:], t2[:], AF.Square),
                 inc="s_S")
            m_casts3 = cnt["s_S"]
            emit("tensor", stat_mms(6, xbb),
                 waits=[W("s_S", m_casts3), W("s_V", m_var2)], inc="s_T")
            m_sx3 = cnt["s_T"]
            emit("tensor", stat_mms(7, sqb), inc="s_T")
            m_ssq3 = cnt["s_T"]
            marks[("ssq3", ti)] = m_ssq3
            emit("scalar", lambda s: s.activation(mu_b[:], ps[6][:, 0:F], AF.Copy),
                 waits=[W("s_T", m_sx3)])
            emit("scalar", musq_f, inc="s_S")
            m_mu3 = cnt["s_S"]
            emit("vector", var_chain2(LN_EPS_2, 7, va_b),
                 waits=[W("s_S", m_mu3), W("s_T", m_ssq3)], inc="s_V")
            marks[("var2", ti)] = cnt["s_V"]
            emit("scalar", lambda s: s.activation(va_b[:], va_b[:], AF.Ln),
                 waits=[W("s_V", marks[("var2", ti)])])
            emit("scalar", iv_f, inc="s_S")
            m_iv3 = cnt["s_S"]
            emit("vector", lambda v: v.tensor_tensor(
                full3(t1), full3(t2), mu_full, ALU.subtract),
                waits=[W("s_S", m_mu3)])
            emit("vector", lambda v: v.tensor_tensor(
                full3(t1), full3(t1), iv_full, ALU.mult),
                waits=[W("s_S", m_iv3)], inc="s_V")
            marks[("z3", ti)] = cnt["s_V"]
            m_z3 = cnt["s_V"]

            def xn_aff(c, xnb):
                def f(s):
                    s.nop()
                    return s.activation(ch(xnb, c), ch(t1, c), AF.Identity,
                                        scale=vecs[:, 48 + c:49 + c],
                                        bias=vecs[:, 56 + c:57 + c])
                return f
            for c in range(NCH):
                wts = []
                if c == 0:
                    wts = [W("s_V", m_z3)]
                    if ti >= 2:
                        wts.append(Wc("s_no"))
                emit("scalar", xn_aff(c, xnb), waits=wts,
                     inc="s_S" if c == NCH - 1 else None)
            m_xn = cnt["s_S"]
            marks[("xnaff", ti)] = m_xn

            emit("vector", lambda v, xnb=xnb, ti=ti: v.tensor_reduce(
                maxacc[:, ti:ti + 1], xnb[:], mybir.AxisListType.X, ALU.max,
                apply_absolute_value=True), waits=[W("s_S", m_xn)], inc="s_V")
            marks[("mx", ti)] = cnt["s_V"]

            emit("sync", lambda e, xnb=xnb, ti=ti: e.dma_start(
                out=_ap(xn_hbm, ti * F, [[T, P], [P * T, NCH], [1, F]]),
                in_=xnb[:]), waits=[W("s_S", m_xn)], inc="s_no", incn=16)

            if ti + 2 < NT:
                emit("sync", xload(ti + 2), waits=[W("s_V", x_rel[ti])],
                     inc="s_x", incn=16)

        for it in range(NT + 2):
            if it >= 2:
                emit_post(it - 2)
            if it < NT:
                emit_pre(it)
        m_xn_last = marks[("xnaff", NT - 1)]

        m_mx_red = marks[("mx", NT - 1)]
        emit("vector", lambda v: v.tensor_reduce(
            mxc[:], maxacc[:], mybir.AxisListType.X, ALU.max),
            waits=[W("s_V", m_mx_red)], inc="s_V", aux=True)
        m_mx_final = cnt["s_V"]
        emit("sync", lambda e: e.dma_start(
            out=mx_hbm, in_=mxc[:], allow_slow_non_contiguous=True),
            waits=[W("s_V", m_mx_final), W("s_S", cnt["s_S"]),
                   W("s_T", cnt["s_T"])], inc="s_x", incn=16)
        m_gx1 = cnt["s_x"]
        emit("sync", lambda e: e.dma_start(out=mxrow[:], in_=mx_hbm),
             waits=[W("s_x", m_gx1)], inc="s_x", incn=16)
        m_gx2 = cnt["s_x"]
        emit("vector", lambda v: v.tensor_reduce(
            gsc[0:1, 0:1], mxrow[:], mybir.AxisListType.X, ALU.max),
            waits=[W("s_x", m_gx2)], inc="s_V", aux=True)
        m_gmax = cnt["s_V"]
        emit("gpsimd", lambda g: g.dma_start(out=cc_in, in_=gsc[:]),
             waits=[W("s_V", m_gmax)], inc="s_G", incn=16)
        m_g3 = cnt["s_G"]
        emit("gpsimd", lambda g: g.collective_compute(
            "AllReduce", ALU.max, replica_groups=[list(range(B))],
            ins=[cc_in], outs=[cc_out]), waits=[W("s_G", m_g3)], inc="s_cc")
        m_cc = cnt["s_cc"]
        emit("gpsimd", lambda g: g.dma_start(out=gsc[:], in_=cc_out),
             waits=[W("s_cc", m_cc)], inc="s_G", incn=16)
        m_g4 = cnt["s_G"]
        emit("sync", lambda e: e.dma_start(
            out=gcol[:], in_=_ap(gsc[0:1, 0:1], 0, [[64, 1], [0, P]]),
            allow_slow_non_contiguous=True),
            waits=[W("s_G", m_g4)], inc="s_x", incn=16)
        m_g5 = cnt["s_x"]
        emit("vector", lambda v: v.reciprocal(sgam[:], gcol[:]),
             waits=[W("s_x", m_g5)], aux=True)
        emit("vector", lambda v: v.tensor_scalar(sgam[:], sgam[:], QB, None,
                                                 ALU.mult), inc="s_V", aux=True)
        m_sg = cnt["s_V"]

        wq_parts = (xk, xv, xr, xbb)

        def wq_part(c):
            return wq_parts[c // 2], (c % 2) * D

        m_z3_last = marks[("z3", NT - 1)]
        for q in range(4):
            wts = ([W("s_V", m_z3_last), W("s_S", m_xn_last),
                    W("s_T", marks[("ssq3", NT - 1)])] if q == 0
                   else [W("s_S", marks[("wabs", q - 1)])])
            emit("gpsimd", lambda g, q=q: g.dma_start(
                out=t1[:], in_=_ap(ffnw_hbm, 2 * q * P * D,
                                   [[D, P], [P * D, 2], [1, D]])),
                waits=wts, inc="s_G", incn=16)
            m_ld = cnt["s_G"]
            swts = [W("s_G", m_ld)]
            if q == 0:
                swts.append(W("s_V", m_mx_final))
            else:
                swts.append(W("s_T", marks[("bmm", q - 1)]))
            emit("scalar", lambda s: s.activation(sqb[:], t1[:], AF.Abs),
                 waits=swts, inc="s_S")
            m_abs = cnt["s_S"]
            marks[("wabs", q)] = m_abs

            def beta_mms(q):
                def f(t):
                    last = None
                    bank = (2, 3, 6, 7)[q]
                    for half in range(4):
                        last = t.matmul(ps[bank][:, 0:512], ones_t[:],
                                        sqb[:, half * 512:(half + 1) * 512],
                                        start=(half == 0), stop=(half == 3))
                    return last
                return f
            emit("tensor", beta_mms(q), waits=[W("s_S", m_abs)], inc="s_T")
            marks[("bmm", q)] = cnt["s_T"]
        m_bmm_all = cnt["s_T"]
        emit("vector", lambda v: v.tensor_copy(bacc[:], ps[2][:, 0:512]),
             waits=[W("s_T", m_bmm_all)])
        for bank in (3, 6, 7):
            emit("vector", lambda v, bank=bank: v.tensor_tensor(
                bacc[:], bacc[:], ps[bank][:, 0:512], ALU.add),
                inc="s_V" if bank == 7 else None)
        marks[("bev", 3)] = cnt["s_V"]
        def betasum_f(s):
            s.nop()
            return s.activation(t2[:, 0:512], bacc[:], AF.Copy,
                                accum_out=t2[:, 1024:1025])
        emit("scalar", betasum_f, waits=[W("s_V", marks[("bev", 3)])], inc="s_S")
        m_bsum_s = cnt["s_S"]
        emit("vector", lambda v: v.tensor_scalar(
            bcol[:], t2[:, 1024:1025], 1.0 / 1024.0, None, ALU.mult),
            waits=[W("s_S", m_bsum_s)], aux=True)
        emit("vector", lambda v: v.tensor_scalar(
            rbcol[:], bcol[:], 1e-5, None, ALU.add), inc="s_V", aux=True)
        m_g6 = cnt["s_V"]
        emit("vector", lambda v: v.reciprocal(rbcol[:], rbcol[:]),
             inc="s_V", aux=True)
        m_rb = cnt["s_V"]

        for q in range(4):
            wts = ([W("s_V", m_rb)] if q == 0
                   else [W("s_V", marks[("wqq", q - 1)])])
            emit("gpsimd", lambda g, q=q: g.dma_start(
                out=t1[:], in_=_ap(ffnw_hbm, 2 * q * P * D,
                                   [[D, P], [P * D, 2], [1, D]])),
                waits=wts, inc="s_G", incn=16)
            m_ld2 = cnt["s_G"]

            def quant(q):
                def f(v):
                    v.nop()
                    v.tensor_scalar(t1[:], t1[:], rbcol[:, 0:1], None, ALU.mult)
                    v.nop()
                    v.tensor_scalar(t2[:], t1[:], 0.5, None, ALU.is_gt)
                    v.nop()
                    v.tensor_scalar(t1[:], t1[:], -0.5, None, ALU.is_lt)
                    v.nop()
                    part, _ = wq_part(2 * q)
                    return v.tensor_tensor(part[:, 0:2 * D], t2[:], t1[:],
                                           ALU.subtract)
                return f
            emit("vector", quant(q), waits=[W("s_G", m_ld2)], inc="s_V")
            marks[("wqq", q)] = cnt["s_V"]
        m_wq = cnt["s_V"]

        emit("vector", lambda v: v.tensor_scalar(
            mxc[:], bcol[:], 1.0 / QB, None, ALU.mult),
            waits=[W("s_V", m_g6), W("s_V", m_sg)], inc="s_V", aux=True)
        m_b128 = cnt["s_V"]

        def gbs1_f(s):
            s.nop()
            return s.activation(rbcol2[:], mxc[:], AF.Identity,
                                scale=gcol[:, 0:1])
        emit("scalar", gbs1_f, waits=[W("s_V", m_b128)], inc="s_S")

        def gbs2_f(s):
            s.nop()
            return s.activation(gbs[:], gbv[:], AF.Identity,
                                scale=rbcol2[:, 0:1])
        emit("scalar", gbs2_f, inc="s_S")
        m_gbs = cnt["s_S"]

        def xnload(ti, wts):
            def f(e):
                return e.dma_start(
                    out=xnB[ti % 2][:],
                    in_=_ap(xn_hbm, ti * F, [[T, P], [P * T, NCH], [1, F]]))
            return f, wts
        f0, w0 = xnload(0, [Wc("s_no"), W("s_V", m_wq)])
        emit("sync", f0, waits=w0, inc="s_x", incn=16)
        base_px = cnt["s_x"]
        f1, w1 = xnload(1, [Wc("s_no"), W("s_V", m_wq)])
        emit("sync", f1, waits=w1, inc="s_x", incn=16)

        for ti in range(NT):
            xnb = xnB[ti % 2]
            term = aab if ti % 2 == 0 else t2
            qb = sqb if ti % 2 == 0 else sqb2
            wts = [Wc("s_x"), W("s_S", m_gbs)]
            if ti >= 2:
                wts.append(W("s_T", marks[("fmm", ti - 2)]))
            emit("vector", lambda v, xnb=xnb: v.tensor_scalar(
                t1[:], xnb[:], sgam[:, 0:1], 8388608.0, ALU.mult, ALU.add),
                waits=wts, aux=True)
            emit("vector", lambda v, qb=qb: v.tensor_scalar(
                qb[:], t1[:], -8388608.0, CLIP, ALU.add, ALU.min))
            emit("vector", lambda v, qb=qb: v.tensor_scalar(
                qb[:], qb[:], -CLIP, None, ALU.max), inc="s_V")
            m_xq = cnt["s_V"]
            marks[("xq", ti)] = m_xq

            def fmm(e, bank):
                def f(t, qb=qb):
                    last = None
                    for c in range(NCH):
                        part, off = wq_part(c)
                        last = t.matmul(ps[bank][:, 0:F],
                                        part[:, off + e * P:off + (e + 1) * P],
                                        ch(qb, c), start=(c == 0),
                                        stop=(c == NCH - 1))
                    return last
                return f
            for e in range(NCH):
                bank = 2 + (e % 2)
                wts = [W("s_V", m_xq)] if e == 0 else []
                if e < 2 and ti >= 1:
                    wts.append(W("s_S", marks[("fev", ti - 1, 6 + e)]))
                if e >= 2:
                    wts.append(W("s_S", marks[("fev", ti, e - 2)]))
                emit("tensor", fmm(e, bank), waits=wts, inc="s_T")
                mme = cnt["s_T"]
                swts = [W("s_T", mme)]
                if ti == 0 and e == 0:
                    swts.append(Wc("s_ao"))
                if ti >= 2 and e == 0:
                    swts.append(Wc("s_fo"))
                def fev_f(s, e=e, bank=bank, term=term):
                    s.nop()
                    return s.activation(ch(term, e), ps[bank][:, 0:F], AF.Identity,
                                        scale=gbs[:, e:e + 1])
                emit("scalar", fev_f, waits=swts, inc="s_S")
                marks[("fev", ti, e)] = cnt["s_S"]
            marks[("fmm", ti)] = cnt["s_T"]
            m_term = cnt["s_S"]

            emit("gpsimd", lambda g, term=term, ti=ti: g.dma_start(
                out=_ap(out_hbm, ti * F, [[T, P], [P * T, NCH], [1, F]]),
                in_=term[:], accum_op=ALU.add),
                waits=[W("s_S", m_term)], inc="s_fo", incn=16)
            marks[("acc", ti)] = cnt["s_fo"]

            if ti + 2 < NT:
                fN, wN = xnload(ti + 2, [Wc("s_no"),
                                         W("s_V", marks[("xq", ti)])])
                emit("sync", fN, waits=wN, inc="s_x", incn=16)

        emit("gpsimd", lambda g: g.wait_ge(sems["s_fo"], cnt["s_fo"]))

        def replay(eng_name):
            def runner(eng):
                for fn, waits, inc, incn, aux in sched[eng_name]:
                    for (sem, val) in waits:
                        eng.wait_ge(sem, val)
                    if aux:
                        eng.nop()
                    instr = fn(eng)
                    if inc is not None:
                        assert instr is not None, \
                            f"{eng_name} op returned None with inc"
                        instr.then_inc(sems[inc], incn)
            return runner

        block.sync(replay("sync"))
        block.scalar(replay("scalar"))
        block.vector(replay("vector"))
        block.tensor(replay("tensor"))
        block.gpsimd(replay("gpsimd"))

    return nc


def kernel(**inputs):
    x = np.ascontiguousarray(inputs["L0_attn_x"])
    wkT = np.ascontiguousarray(inputs["L0_attn_W_k"].T)
    wvT = np.ascontiguousarray(inputs["L0_attn_W_v"].T)
    wrT = np.ascontiguousarray(inputs["L0_attn_W_r"].T)
    woT = np.ascontiguousarray(inputs["L0_attn_W_o"].T)
    ffnw = np.ascontiguousarray(inputs["L0_ffn_weight"])
    wmod = np.ascontiguousarray(inputs["L0_wmod"])
    crow = inputs["L0_c"]

    def vec8(v):
        return np.ascontiguousarray(np.reshape(np.asarray(v), (NCH, P)).T)

    nc = build_nc()
    in_maps = []
    for b in range(B):
        vecs = np.zeros((P, 64), np.float32)
        vecs[:, 0:8] = vec8(crow[b])
        vecs[:, 8:16] = vec8(inputs["L0_attn_time_mix_k"])
        vecs[:, 16:24] = vec8(inputs["L0_attn_time_mix_v"])
        vecs[:, 24:32] = vec8(inputs["L0_attn_time_mix_r"])
        vecs[:, 32:40] = vec8(inputs["L0_attn_time_decay"])
        vecs[:, 40:48] = vec8(inputs["L0_attn_time_first"])
        vecs[:, 48:56] = vec8(inputs["L0_ffn_ln_gamma"])
        vecs[:, 56:64] = vec8(inputs["L0_ffn_ln_beta"])
        in_maps.append({
            "x": np.ascontiguousarray(x[b].T),
            "wkT": wkT, "wvT": wvT, "wrT": wrT, "woT": woT,
            "ffnw": ffnw, "wmod": wmod, "vecs": vecs,
            "ident": np.eye(P, dtype=np.float32),
        })
    res = run_bass_kernel_spmd(nc, in_maps, core_ids=list(range(B)))
    out = np.stack([np.ascontiguousarray(r["out"].T) for r in res.results])
    return out


# version 19
# speedup vs baseline: 1.1841x; 1.0963x over previous
import contextlib
import numpy as np
import ml_dtypes
import concourse.bass as bass
import concourse.mybir as mybir

from concourse.bass_utils import run_bass_kernel_spmd

F32 = mybir.dt.float32
BF16 = mybir.dt.bfloat16
AF = mybir.ActivationFunctionType
ALU = mybir.AluOpType

P = 128
D = 1024
NCH = D // P
T = 8192
F = 256
NT = T // F
B = 8
LN_EPS_A = 1e-5
LN_EPS_2 = 1e-3
QB = 128.0
CLIP = QB - 1e-5


def _ap(t_ap, offset, dims):
    return bass.AP(t_ap.tensor, offset, dims)


def build_nc():
    nc = bass.Bass("TRN2", target_bir_lowering=False, detect_race_conditions=False)

    x_hbm = nc.dram_tensor("x", [D, T], F32, kind="ExternalInput").ap()
    wk_hbm = nc.dram_tensor("wkT", [D, D], F32, kind="ExternalInput").ap()
    wv_hbm = nc.dram_tensor("wvT", [D, D], F32, kind="ExternalInput").ap()
    wr_hbm = nc.dram_tensor("wrT", [D, D], F32, kind="ExternalInput").ap()
    wo_hbm = nc.dram_tensor("woT", [D, D], F32, kind="ExternalInput").ap()
    ffnw_hbm = nc.dram_tensor("ffnw", [D, D], F32, kind="ExternalInput").ap()
    wmod_hbm = nc.dram_tensor("wmod", [D, 6 * D], F32, kind="ExternalInput").ap()
    vecs_hbm = nc.dram_tensor("vecs", [P, 64], F32, kind="ExternalInput").ap()
    ident_hbm = nc.dram_tensor("ident", [P, P], F32, kind="ExternalInput").ap()
    out_hbm = nc.dram_tensor("out", [D, T], F32, kind="ExternalOutput").ap()

    xn_hbm = nc.dram_tensor("xn_scr", [D, T], BF16).ap()
    mx_hbm = nc.dram_tensor("mx_scr", [1, P], F32).ap()
    cc_in_t = nc.dram_tensor("cc_in", [1, 64], F32)
    cc_out_t = nc.dram_tensor("cc_out", [1, 64], F32)
    cc_in = cc_in_t.ap().opt()
    cc_out = cc_out_t.ap().opt()

    es = contextlib.ExitStack()
    with es:
        sb = lambda name, shape, dt: es.enter_context(nc.sbuf_tensor(name, shape, dt))
        wk = sb("wk", [P, NCH * D], BF16)
        wv = sb("wv", [P, NCH * D], BF16)
        wr = sb("wr", [P, NCH * D], BF16)
        wo = sb("wo", [P, NCH * D], BF16)
        xB = [sb(f"x{i}", [P, NCH * F], F32) for i in range(2)]
        xbb = sb("xbb", [P, NCH * F], BF16)
        sqb = sb("sqb", [P, NCH * F], BF16)
        sqb2 = sb("sqb2", [P, NCH * F], BF16)
        h = sb("h", [P, NCH * (F + 1)], BF16)
        hd = sb("hd", [P, NCH * F], BF16)
        xc = sb("xc", [P, NCH * F], BF16)
        xk = sb("xk", [P, NCH * F], BF16)
        xv = sb("xv", [P, NCH * F], BF16)
        xr = sb("xr", [P, NCH * F], BF16)
        aP = [sb(f"a{i}", [P, NCH * F], BF16) for i in range(2)]
        cbP = [sb(f"cb{i}", [P, NCH * F], BF16) for i in range(2)]
        rbP = [sb(f"rbx{i}", [P, NCH * F], BF16) for i in range(2)]
        vvP = [sb(f"vv{i}", [P, NCH * F], BF16) for i in range(2)]
        Nb = sb("Nb", [P, NCH * (F + 1)], F32)
        Eb = sb("Eb", [P, NCH * (F + 1)], F32)
        t1 = sb("t1", [P, NCH * F], F32)
        t2 = sb("t2", [P, NCH * F], F32)
        aab = sb("aa0", [P, NCH * F], F32)
        xnB = [sb(f"xn{i}", [P, NCH * F], BF16) for i in range(2)]
        mu_b = sb("mu_b", [P, F], F32)
        iv_b = sb("iv_b", [P, F], F32)
        va_b = sb("va_b", [P, F], F32)
        mu_1 = sb("mu_1", [P, F], F32)
        iv_1 = sb("iv_1", [P, F], F32)
        va_1 = sb("va_1", [P, F], F32)
        ones_t = sb("ones_t", [P, P], BF16)
        identb = sb("identb", [P, P], BF16)
        mxb = sb("mxb", [P, P], BF16)
        bacc = sb("bacc", [P, 512], F32)
        vecs = sb("vecs_sb", [P, 64], F32)
        scol = sb("scol", [P, 8], BF16)
        lam = sb("lam", [P, 8], F32)
        eu = sb("eu", [P, 8], F32)
        modv = sb("modv", [P, 48], F32)
        Aa = sb("Aa", [P, 8], F32)
        Bb = sb("Bb", [P, 8], F32)
        gav = sb("gav", [P, 8], F32)
        gbv = sb("gbv", [P, 8], F32)
        gbs = sb("gbs", [P, 8], F32)
        cpool = sb("cpool", [P, 2], F32)
        mxrow = sb("mxrow", [1, P], F32)
        gsc = sb("gsc", [1, 64], F32)
        gcol = sb("gcol", [P, 1], F32)
        sgam = sb("sgam", [P, 1], F32)
        bcol = sb("bcol", [P, 1], F32)
        rbcol = sb("rbcol", [P, 1], F32)
        rbcol2 = sb("rbcol2", [P, 1], F32)
        bsum = sb("bsum", [1, 2], F32)
        maxacc = sb("maxacc", [P, NT], F32)
        mxc = sb("mxc", [P, 1], F32)
        tiny = sb("tinyv", [P, 8], F32)
        ps = [es.enter_context(nc.psum_tensor(f"ps{i}", [P, 512], F32)) for i in range(8)]

        nc.const_aps.register(F32, -1.0, cpool[:, 0:1])
        nc.const_aps.register(F32, -0.5, cpool[:, 1:2])

        sems = {}
        for name in ["s_x", "s_w", "s_S", "s_V", "s_T", "s_G", "s_ao", "s_no",
                     "s_cc", "s_fo", "s_Gc"]:
            sems[name] = es.enter_context(nc.semaphore(name))
        cnt = {k: 0 for k in sems}
        block = es.enter_context(nc.Block())

        sched = {k: [] for k in ["sync", "scalar", "vector", "tensor", "gpsimd"]}

        def emit(eng, fn, waits=(), inc=None, incn=1, aux=False):
            waits = list(waits)
            if inc is not None and incn == 16:
                waits.insert(0, (sems[inc], cnt[inc]))
            sched[eng].append((fn, tuple(waits), inc, incn, aux))
            if inc is not None:
                cnt[inc] += incn
            return cnt[inc] if inc else None

        def W(name, n):
            return (sems[name], n)

        def Wc(name):
            return (sems[name], cnt[name])

        def ch(buf, c, width=F):
            return buf[:, c * width:(c + 1) * width]

        def chp(buf, c):
            base = c * (F + 1)
            return buf[:, base:base + F + 1]

        marks = {}

        def full3(buf):
            return _ap(buf[:, 0:1], 0, [[NCH * F, P], [F, NCH], [1, F]])

        emit("gpsimd", lambda g: g.memset(cpool[:, 0:1], -1.0))
        emit("gpsimd", lambda g: g.memset(cpool[:, 1:2], -0.5))
        emit("gpsimd", lambda g: g.memset(ones_t[:], 1.0 / 1024.0))
        emit("gpsimd", lambda g: g.memset(h[:], 0.0))
        emit("gpsimd", lambda g: g.memset(Nb[:], 0.0))
        emit("gpsimd", lambda g: g.memset(Eb[:], 0.0))
        emit("gpsimd", lambda g: g.memset(maxacc[:], 0.0))
        emit("gpsimd", lambda g: g.memset(gsc[:], 0.0), inc="s_Gc")
        m_const = cnt["s_Gc"]
        emit("gpsimd", lambda g: g.dma_start(out=vecs[:], in_=vecs_hbm),
             inc="s_w", incn=16)
        m_vecs = cnt["s_w"]

        emit("scalar", lambda s: s.activation(scol[:], vecs[:, 0:8], AF.Silu),
             waits=[Wc("s_w"), W("s_Gc", m_const)], inc="s_S")
        emit("scalar", lambda s: s.activation(lam[:], vecs[:, 32:40], AF.Exp),
             inc="s_S")
        emit("scalar", lambda s: s.activation(eu[:], vecs[:, 40:48], AF.Exp),
             inc="s_S")
        m_scol = cnt["s_S"]

        def wload(dst, src):
            def f(g):
                return g.dma_start(
                    out=dst[:], in_=_ap(src, 0, [[D, P], [P * D, NCH], [1, D]]))
            return f
        emit("gpsimd", lambda g: g.dma_start(out=identb[:], in_=ident_hbm),
             inc="s_w", incn=16)
        emit("gpsimd", wload(wk, wk_hbm), inc="s_w", incn=16)
        emit("gpsimd", wload(wv, wv_hbm), inc="s_w", incn=16)
        emit("gpsimd", wload(wr, wr_hbm), inc="s_w", incn=16)
        emit("gpsimd", wload(wo, wo_hbm), inc="s_w", incn=16)
        m_wall = cnt["s_w"]

        stageA = (xk, xv, xr)
        stageB = (xbb, sqb, xnB[0])

        def wmod_load1(bufp, c, i):
            def f(g):
                return g.dma_start(
                    out=bufp[:, 0:2 * D],
                    in_=_ap(wmod_hbm, c * P * 6 * D + i * 2 * D,
                            [[6 * D, P], [1, 2 * D]]))
            return f

        def mod_mm(stage, c, j):
            def f(t):
                which = (j * P) // (2 * D)
                off = (j * P) % (2 * D)
                stat = stage[which][:, off:off + P]
                return t.matmul(ps[4][:, j:j + 1], stat, scol[:, c:c + 1],
                                start=(c == 0 and j == 0),
                                stop=(c == NCH - 1 and j == 47),
                                skip_group_check=True)
            return f

        for c in range(NCH):
            stg = stageA if c % 2 == 0 else stageB
            wts = []
            if c >= 2:
                wts = [W("s_T", marks[("modmm", c - 2)])]
            emit("gpsimd", wmod_load1(stg[0], c, 0), waits=wts, inc="s_w", incn=16)
            emit("gpsimd", wmod_load1(stg[1], c, 1), inc="s_w", incn=16)
            emit("gpsimd", wmod_load1(stg[2], c, 2), inc="s_w", incn=16)
            m_wmod_c = cnt["s_w"]
            for j in range(48):
                wts = [W("s_w", m_wmod_c)] if j == 0 else ()
                if c == 0 and j == 0:
                    wts = [W("s_w", m_wmod_c), W("s_S", m_scol)]
                emit("tensor", mod_mm(stg, c, j), waits=wts,
                     inc="s_T" if j == 47 else None)
            marks[("modmm", c)] = cnt["s_T"]
        m_modmm = cnt["s_T"]

        emit("scalar", lambda s: s.activation(modv[:], ps[4][:, 0:48], AF.Copy),
             waits=[W("s_T", m_modmm)], inc="s_S")
        m_modv = cnt["s_S"]
        sa_v = modv[:, 0:8]
        ca_v = modv[:, 8:16]
        ga_v = modv[:, 16:24]
        sbv_v = modv[:, 24:32]
        cb_v = modv[:, 32:40]
        gb_v = modv[:, 40:48]

        GW = 256
        emit("scalar", lambda s: s.activation(t1[:, 0:48], modv[:], AF.Copy),
             waits=[W("s_S", m_modv)], inc="s_S")
        m_gcp = cnt["s_S"]

        def gatewide(v):
            g0 = t1[:, 0:GW]
            g1 = t1[:, GW:2 * GW]
            g2 = t1[:, 2 * GW:3 * GW]
            v.tensor_scalar(g1, g0, 0.0, None, ALU.is_ge)
            v.tensor_scalar(g1, g1, 2.0, -1.0, ALU.mult, ALU.add)
            v.tensor_tensor(g2, g0, g1, ALU.mult)
            v.tensor_scalar(g2, g2, 0.25, None, ALU.max)
            return v.tensor_tensor(g2, g1, g2, ALU.mult)
        emit("vector", gatewide, waits=[W("s_S", m_gcp)], inc="s_V")
        m_gw = cnt["s_V"]
        emit("scalar", lambda s: s.activation(
            gav[:], t1[:, 2 * GW + 16:2 * GW + 24], AF.Copy),
            waits=[W("s_V", m_gw)])
        emit("scalar", lambda s: s.activation(
            gbv[:], t1[:, 2 * GW + 40:2 * GW + 48], AF.Copy), inc="s_S")
        m_gates = cnt["s_S"]
        emit("vector", lambda v: v.tensor_scalar(Aa[:], ca_v, 1.0, None, ALU.add),
             waits=[W("s_S", m_gates)], aux=True)
        emit("vector", lambda v: v.tensor_scalar(Bb[:], cb_v, 1.0, None, ALU.add),
             inc="s_V", aux=True)
        m_mod_ready = cnt["s_V"]

        def xload(ti, extra=()):
            def f(e):
                return e.dma_start(
                    out=xB[ti % 2][:],
                    in_=_ap(x_hbm, ti * F, [[T, P], [P * T, NCH], [1, F]]))
            return f
        emit("sync", xload(0), inc="s_x", incn=16)
        emit("sync", xload(1), inc="s_x", incn=16)

        lamc = [_ap(lam[:, 0:1], c, [[8, P], [0, F]]) for c in range(NCH)]
        x_rel = {}
        aa_st = {}
        xn_st = {}
        st_aa = 0
        st_xn = 0

        def stat_mms(bank, src):
            def f(t):
                last = None
                for c in range(NCH):
                    last = t.matmul(ps[bank][:, 0:F], ones_t[:], ch(src, c),
                                    start=(c == 0), stop=(c == NCH - 1))
                return last
            return f

        def var_chain(eps):
            def f(v):
                v.tensor_tensor(va_b[:], ps[1][:, 0:F], va_b[:], ALU.subtract)
                return v.tensor_scalar(va_b[:], va_b[:], eps, None, ALU.add)
            return f

        mu_full = _ap(mu_b[:, 0:1], 0, [[F, P], [0, NCH], [1, F]])
        iv_full = _ap(iv_b[:, 0:1], 0, [[F, P], [0, NCH], [1, F]])
        assert mu_full is not None

        def proj_mms(wsb, src, e, bank):
            def f(t):
                last = None
                for c in range(NCH):
                    last = t.matmul(ps[bank][:, 0:F],
                                    wsb[:, c * D + e * P:c * D + (e + 1) * P],
                                    ch(src, c), start=(c == 0),
                                    stop=(c == NCH - 1))
                return last
            return f

        def proj_pair(wsb, src, p, bank):
            def f(t):
                last = None
                for half in range(2):
                    e = 2 * p + half
                    for c in range(NCH):
                        last = t.matmul(ps[bank][:, half * F:(half + 1) * F],
                                        wsb[:, c * D + e * P:c * D + (e + 1) * P],
                                        ch(src, c), start=(c == 0),
                                        stop=(c == NCH - 1))
                return last
            return f

        def pr2(buf, p):
            return buf[:, 2 * p * F:(2 * p + 2) * F]

        sqb1 = xv
        z1 = xk
        ybuf = sqb2

        def var_chain2(eps, bank, va_t):
            def f(v):
                v.tensor_tensor(va_t[:], ps[bank][:, 0:F], va_t[:], ALU.subtract)
                return v.tensor_scalar(va_t[:], va_t[:], eps, None, ALU.add)
            return f

        def emit_pre(ti):
            a_, vv_, rb_ = aP[ti % 2], vvP[ti % 2], rbP[ti % 2]
            wts = []
            if ti >= 1:
                wts.append(W("s_T", marks[("ssq1", ti - 1)]))
                wts.append(W("s_V", marks[("z1v", ti - 1)]))
            emit("gpsimd", lambda g, ti=ti: g.dma_start(
                out=xc[:], in_=_ap(x_hbm, ti * F, [[T, P], [P * T, NCH], [1, F]])),
                waits=wts, inc="s_G", incn=16)
            m_xc = cnt["s_G"]

            wts = [W("s_G", m_xc)]
            if ti == 0:
                wts.append(W("s_V", m_mod_ready))
                wts.append(W("s_T", m_modmm))
            else:
                wts.append(W("s_T", marks[("rmm", ti - 1)]))
            emit("scalar", lambda s: s.activation(sqb1[:], xc[:], AF.Square),
                 waits=wts, inc="s_S")
            m_casts = cnt["s_S"]

            wts = [W("s_S", m_casts)]
            if ti > 0:
                wts.append(W("s_V", marks[("z1v", ti - 1)]))
            emit("tensor", stat_mms(0, xc), waits=wts, inc="s_T")
            m_sx = cnt["s_T"]
            emit("tensor", stat_mms(1, sqb1), inc="s_T")
            m_ssq = cnt["s_T"]
            marks[("ssq1", ti)] = m_ssq

            emit("scalar", lambda s: s.activation(mu_1[:], ps[0][:, 0:F], AF.Copy),
                 waits=[W("s_T", m_sx)])
            def musq1_f(s):
                s.nop()
                return s.activation(va_1[:], mu_1[:], AF.Square)
            emit("scalar", musq1_f, inc="s_S")
            m_mu = cnt["s_S"]
            emit("vector", var_chain2(LN_EPS_A, 1, va_1),
                 waits=[W("s_S", m_mu), W("s_T", m_ssq)], inc="s_V")
            marks[("var1", ti)] = cnt["s_V"]
            emit("scalar", lambda s: s.activation(va_1[:], va_1[:], AF.Ln),
                 waits=[W("s_V", marks[("var1", ti)])])
            def iv1_f(s):
                s.nop()
                return s.activation(iv_1[:], va_1[:], AF.Exp, scale=-0.5)
            emit("scalar", iv1_f, inc="s_S")
            m_iv = cnt["s_S"]

            mu1_full = _ap(mu_1[:, 0:1], 0, [[F, P], [0, NCH], [1, F]])
            iv1_full = _ap(iv_1[:, 0:1], 0, [[F, P], [0, NCH], [1, F]])
            emit("vector", lambda v: v.tensor_tensor(
                full3(z1), full3(xc), mu1_full, ALU.subtract),
                waits=[W("s_S", m_mu)])
            emit("vector", lambda v: v.tensor_tensor(
                full3(z1), full3(z1), iv1_full, ALU.mult),
                waits=[W("s_S", m_iv)], inc="s_V")
            m_z = cnt["s_V"]
            marks[("z1v", ti)] = m_z

            emit("vector", lambda v: v.tensor_copy(
                _ap(h[:, 0:1], 0, [[NCH * (F + 1), P], [F + 1, NCH], [1, 1]]),
                _ap(h[:, 0:1], F, [[NCH * (F + 1), P], [F + 1, NCH], [1, 1]])),
                inc="s_V")
            m_hcarry = cnt["s_V"]

            def h_aff(c):
                def f(s):
                    s.nop()
                    return s.activation(chp(h, c)[:, 1:F + 1], ch(z1, c),
                                        AF.Identity, scale=Aa[:, c:c + 1],
                                        bias=sa_v[:, c:c + 1])
                return f
            for c in range(NCH):
                emit("scalar", h_aff(c),
                     waits=[W("s_V", m_z), W("s_V", m_hcarry)] if c == 0 else (),
                     inc="s_S" if c == NCH - 1 else None)
            m_h = cnt["s_S"]

            for c in range(NCH):
                emit("vector", lambda v, c=c: v.tensor_tensor(
                    ch(hd, c), chp(h, c)[:, 1:F + 1], chp(h, c)[:, 0:F],
                    ALU.subtract), waits=[W("s_S", m_h)] if c == 0 else ())

            def lerp(dst, tmoff, c):
                def f(v):
                    return v.scalar_tensor_tensor(
                        ch(dst, c), ch(hd, c), vecs[:, tmoff + c:tmoff + c + 1],
                        chp(h, c)[:, 0:F], ALU.mult, ALU.add)
                return f
            for c in range(NCH):
                emit("vector", lerp(xk, 8, c), aux=(c == 0),
                     inc="s_V" if c == NCH - 1 else None)
            m_xk = cnt["s_V"]
            for c in range(NCH):
                emit("vector", lerp(xv, 16, c), aux=(c == 0),
                     inc="s_V" if c == NCH - 1 else None)
            m_xv = cnt["s_V"]
            for c in range(NCH):
                emit("vector", lerp(xr, 24, c), aux=(c == 0),
                     inc="s_V" if c == NCH - 1 else None)
            m_xr = cnt["s_V"]
            marks[("lerps", ti)] = m_xr

            kvr = [(wk, xk, a_, AF.Exp, None, m_xk, "kev", "kmm"),
                   (wv, xv, vv_, AF.Copy, None, m_xv, "vev", "vmm"),
                   (wr, xr, rb_, AF.Exp, -1.0, m_xr, "rev", "rmm")]
            g = 0
            for wsb, srcb, dstb, af, sc, m_src, evn, mmn in kvr:
                for p in range(4):
                    bank = 2 + (g % 2)
                    wts = []
                    if p == 0:
                        wts.append(W("s_V", m_src))
                        if evn == "kev":
                            wts.append(Wc("s_w"))
                            if ti >= 2:
                                wts.append(W("s_V", marks[("den2", ti - 2)]))
                    if g >= 2:
                        wts.append(marks[("kvrev", ti, g - 2)])
                    emit("tensor", proj_pair(wsb, srcb, p, bank), waits=wts,
                         inc="s_T")
                    mme = cnt["s_T"]
                    def ev_f(s, p=p, bank=bank, dstb=dstb, af=af, sc=sc):
                        if sc is None:
                            return s.activation(pr2(dstb, p), ps[bank][:], af)
                        return s.activation(pr2(dstb, p), ps[bank][:], af,
                                            scale=sc)
                    emit("scalar", ev_f, waits=[W("s_T", mme)], inc="s_S")
                    marks[("kvrev", ti, g)] = W("s_S", cnt["s_S"])
                    g += 1
                marks[(evn, ti)] = cnt["s_S"]
                marks[(mmn, ti)] = cnt["s_T"]

        def emit_postA(ti):
            xnb = xnB[ti % 2]
            a_, vv_, rb_ = aP[ti % 2], vvP[ti % 2], rbP[ti % 2]
            cb_ = cbP[ti % 2]
            m_kev = marks[("kev", ti)]
            m_vev = marks[("vev", ti)]
            m_rev = marks[("rev", ti)]

            emit("vector", lambda v: v.tensor_tensor(cb_[:], a_[:], vv_[:],
                                                     ALU.mult),
                 waits=[W("s_S", m_vev)], inc="s_V")

            emit("vector", lambda v: v.tensor_copy(
                _ap(Nb[:, 0:1], 0, [[NCH * (F + 1), P], [F + 1, NCH], [1, 1]]),
                _ap(Nb[:, 0:1], F, [[NCH * (F + 1), P], [F + 1, NCH], [1, 1]])))
            emit("vector", lambda v: v.tensor_copy(
                _ap(Eb[:, 0:1], 0, [[NCH * (F + 1), P], [F + 1, NCH], [1, 1]]),
                _ap(Eb[:, 0:1], F, [[NCH * (F + 1), P], [F + 1, NCH], [1, 1]])))
            for c in range(NCH):
                emit("vector", lambda v, c=c: v.tensor_tensor_scan(
                    chp(Nb, c)[:, 1:F + 1], lamc[c], ch(cb_, c),
                    chp(Nb, c)[:, 0:1], ALU.mult, ALU.add), aux=True)
            for c in range(NCH):
                emit("vector", lambda v, c=c: v.tensor_tensor_scan(
                    chp(Eb, c)[:, 1:F + 1], lamc[c], ch(a_, c),
                    chp(Eb, c)[:, 0:1], ALU.mult, ALU.max),
                    waits=[W("s_S", m_kev)] if c == 0 else (), aux=True)
            for c in range(NCH):
                emit("vector", lambda v, c=c: v.scalar_tensor_tensor(
                    ch(cb_, c), ch(cb_, c), eu[:, c:c + 1],
                    chp(Nb, c)[:, 0:F], ALU.mult, ALU.add), aux=(c == 0))
            for c in range(NCH):
                wts = []
                if c == 0 and ti >= 2:
                    wts = [W("s_no", 16 * (ti - 1))]
                emit("vector", lambda v, c=c, xnb=xnb: v.scalar_tensor_tensor(
                    ch(xnb, c), ch(a_, c), eu[:, c:c + 1],
                    chp(Eb, c)[:, 0:F], ALU.mult, ALU.add), waits=wts,
                    aux=(c == 0), inc="s_V" if c == NCH - 1 else None)
            m_den = cnt["s_V"]

            emit("vector", lambda v, xnb=xnb: v.scalar_tensor_tensor(
                xnb[:], rb_[:], 1.0, xnb[:], ALU.add, ALU.mult),
                waits=[W("s_S", m_rev)], inc="s_V", aux=True)
            m_den2 = cnt["s_V"]
            marks[("den2", ti)] = m_den2

        def emit_postB(ti):
            xb_ = xB[ti % 2]
            xnb = xnB[ti % 2]
            cb_ = cbP[ti % 2]
            emit("scalar", lambda s, xnb=xnb: s.activation(xnb[:], xnb[:], AF.Ln),
                 waits=[W("s_V", marks[("den2", ti)])])
            def rec_f(s, xnb=xnb):
                s.nop()
                return s.activation(t1[:], xnb[:], AF.Exp, scale=-1.0)
            emit("scalar", rec_f, inc="s_S")
            m_rec = cnt["s_S"]

            wts = [W("s_S", m_rec)]
            if ti >= 1:
                wts.append(W("s_T", marks[("womm", ti - 1)]))
            emit("vector", lambda v: v.tensor_tensor(ybuf[:], cb_[:], t1[:],
                                                     ALU.mult),
                 waits=wts, inc="s_V")
            m_y = cnt["s_V"]

            for p in range(4):
                bank = 4 + (p % 2)
                wts = []
                if p == 0:
                    wts = [W("s_V", m_y)]
                if p >= 2:
                    wts.append(W("s_S", marks[("oev", ti, p - 2)]))
                emit("tensor", proj_pair(wo, ybuf, p, bank), waits=wts, inc="s_T")
                mme = cnt["s_T"]
                emit("scalar", lambda s, p=p, bank=bank: s.activation(
                    pr2(t2, p), ps[bank][:], AF.Copy),
                    waits=[W("s_T", mme)], inc="s_S")
                marks[("oev", ti, p)] = cnt["s_S"]
            m_oev = cnt["s_S"]
            marks[("womm", ti)] = cnt["s_T"]
            for c in range(NCH):
                wts = []
                if c == 0:
                    wts = [W("s_S", m_oev), W("s_x", 16 * (ti + 1))]
                    if ti >= 1:
                        wts.append(Wc("s_ao"))
                emit("vector", lambda v, c=c, xb_=xb_:
                     v.scalar_tensor_tensor(
                         ch(aab, c), ch(t2, c), gav[:, c:c + 1], ch(xb_, c),
                         ALU.mult, ALU.add), waits=wts, aux=(c == 0),
                     inc="s_V" if c == NCH - 1 else None)
            m_aa = cnt["s_V"]
            x_rel[ti] = m_aa

            emit("sync", lambda e, ti=ti: e.dma_start(
                out=_ap(out_hbm, ti * F, [[T, P], [P * T, NCH], [1, F]]),
                in_=aab[:]), waits=[W("s_V", m_aa)], inc="s_ao", incn=16)

            emit("scalar", lambda s: s.activation(xbb[:], aab[:], AF.Copy),
                 waits=[W("s_V", m_aa)], inc="s_S")
            emit("scalar", lambda s: s.activation(sqb[:], aab[:], AF.Square),
                 inc="s_S")
            m_casts2 = cnt["s_S"]
            wts = [W("s_S", m_casts2)]
            if ti >= 1:
                wts.append(W("s_V", marks[("z3", ti - 1)]))
            emit("tensor", stat_mms(6, xbb), waits=wts, inc="s_T")
            m_sx2 = cnt["s_T"]
            emit("tensor", stat_mms(7, sqb), inc="s_T")
            m_ssq2 = cnt["s_T"]
            emit("scalar", lambda s: s.activation(mu_b[:], ps[6][:, 0:F], AF.Copy),
                 waits=[W("s_T", m_sx2)])
            def musq_f(s):
                s.nop()
                return s.activation(va_b[:], mu_b[:], AF.Square)
            emit("scalar", musq_f, inc="s_S")
            m_mu2 = cnt["s_S"]
            emit("vector", var_chain2(LN_EPS_A, 7, va_b),
                 waits=[W("s_S", m_mu2), W("s_T", m_ssq2)], inc="s_V")
            m_var2 = cnt["s_V"]
            emit("scalar", lambda s: s.activation(va_b[:], va_b[:], AF.Ln),
                 waits=[W("s_V", m_var2)])
            def iv_f(s):
                s.nop()
                return s.activation(iv_b[:], va_b[:], AF.Exp, scale=-0.5)
            emit("scalar", iv_f, inc="s_S")
            m_iv2 = cnt["s_S"]
            emit("vector", lambda v: v.tensor_tensor(
                full3(t1), full3(aab), mu_full, ALU.subtract),
                waits=[W("s_S", m_mu2)])
            emit("vector", lambda v: v.tensor_tensor(
                full3(t1), full3(t1), iv_full, ALU.mult),
                waits=[W("s_S", m_iv2)], inc="s_V")
            m_z2 = cnt["s_V"]

            def h2_aff(c):
                def f(s):
                    s.nop()
                    return s.activation(ch(t2, c), ch(t1, c), AF.Identity,
                                        scale=Bb[:, c:c + 1],
                                        bias=sbv_v[:, c:c + 1])
                return f
            for c in range(NCH):
                emit("scalar", h2_aff(c), waits=[W("s_V", m_z2)] if c == 0 else (),
                     inc="s_S" if c == NCH - 1 else None)

            emit("scalar", lambda s: s.activation(xbb[:], t2[:], AF.Copy),
                 inc="s_S")
            emit("scalar", lambda s: s.activation(sqb[:], t2[:], AF.Square),
                 inc="s_S")
            m_casts3 = cnt["s_S"]
            emit("tensor", stat_mms(6, xbb),
                 waits=[W("s_S", m_casts3), W("s_V", m_var2)], inc="s_T")
            m_sx3 = cnt["s_T"]
            emit("tensor", stat_mms(7, sqb), inc="s_T")
            m_ssq3 = cnt["s_T"]
            marks[("ssq3", ti)] = m_ssq3
            emit("scalar", lambda s: s.activation(mu_b[:], ps[6][:, 0:F], AF.Copy),
                 waits=[W("s_T", m_sx3)])
            emit("scalar", musq_f, inc="s_S")
            m_mu3 = cnt["s_S"]
            emit("vector", var_chain2(LN_EPS_2, 7, va_b),
                 waits=[W("s_S", m_mu3), W("s_T", m_ssq3)], inc="s_V")
            marks[("var2", ti)] = cnt["s_V"]
            emit("scalar", lambda s: s.activation(va_b[:], va_b[:], AF.Ln),
                 waits=[W("s_V", marks[("var2", ti)])])
            emit("scalar", iv_f, inc="s_S")
            m_iv3 = cnt["s_S"]
            emit("vector", lambda v: v.tensor_tensor(
                full3(t1), full3(t2), mu_full, ALU.subtract),
                waits=[W("s_S", m_mu3)])
            emit("vector", lambda v: v.tensor_tensor(
                full3(t1), full3(t1), iv_full, ALU.mult),
                waits=[W("s_S", m_iv3)], inc="s_V")
            marks[("z3", ti)] = cnt["s_V"]
            m_z3 = cnt["s_V"]

            def xn_aff(c, xnb):
                def f(s):
                    s.nop()
                    return s.activation(ch(xnb, c), ch(t1, c), AF.Identity,
                                        scale=vecs[:, 48 + c:49 + c],
                                        bias=vecs[:, 56 + c:57 + c])
                return f
            for c in range(NCH):
                wts = []
                if c == 0:
                    wts = [W("s_V", m_z3)]
                    if ti >= 2:
                        wts.append(Wc("s_no"))
                emit("scalar", xn_aff(c, xnb), waits=wts,
                     inc="s_S" if c == NCH - 1 else None)
            m_xn = cnt["s_S"]
            marks[("xnaff", ti)] = m_xn

            emit("vector", lambda v, xnb=xnb, ti=ti: v.tensor_reduce(
                maxacc[:, ti:ti + 1], xnb[:], mybir.AxisListType.X, ALU.max,
                apply_absolute_value=True), waits=[W("s_S", m_xn)], inc="s_V")
            marks[("mx", ti)] = cnt["s_V"]

            emit("sync", lambda e, xnb=xnb, ti=ti: e.dma_start(
                out=_ap(xn_hbm, ti * F, [[T, P], [P * T, NCH], [1, F]]),
                in_=xnb[:]), waits=[W("s_S", m_xn)], inc="s_no", incn=16)

            if ti + 2 < NT:
                emit("sync", xload(ti + 2), waits=[W("s_V", x_rel[ti])],
                     inc="s_x", incn=16)

        for it in range(NT + 3):
            if it >= 2 and it - 2 < NT:
                emit_postA(it - 2)
            if it >= 3:
                emit_postB(it - 3)
            if it < NT:
                emit_pre(it)
        m_xn_last = marks[("xnaff", NT - 1)]

        m_mx_red = marks[("mx", NT - 1)]
        emit("vector", lambda v: v.tensor_reduce(
            mxc[:], maxacc[:], mybir.AxisListType.X, ALU.max),
            waits=[W("s_V", m_mx_red)], inc="s_V", aux=True)
        m_mx_final = cnt["s_V"]
        emit("sync", lambda e: e.dma_start(
            out=mx_hbm, in_=mxc[:], allow_slow_non_contiguous=True),
            waits=[W("s_V", m_mx_final), W("s_S", cnt["s_S"]),
                   W("s_T", cnt["s_T"])], inc="s_x", incn=16)
        m_gx1 = cnt["s_x"]
        emit("sync", lambda e: e.dma_start(out=mxrow[:], in_=mx_hbm),
             waits=[W("s_x", m_gx1)], inc="s_x", incn=16)
        m_gx2 = cnt["s_x"]
        emit("vector", lambda v: v.tensor_reduce(
            gsc[0:1, 0:1], mxrow[:], mybir.AxisListType.X, ALU.max),
            waits=[W("s_x", m_gx2)], inc="s_V", aux=True)
        m_gmax = cnt["s_V"]
        emit("gpsimd", lambda g: g.dma_start(out=cc_in, in_=gsc[:]),
             waits=[W("s_V", m_gmax)], inc="s_G", incn=16)
        m_g3 = cnt["s_G"]
        emit("gpsimd", lambda g: g.collective_compute(
            "AllReduce", ALU.max, replica_groups=[list(range(B))],
            ins=[cc_in], outs=[cc_out]), waits=[W("s_G", m_g3)], inc="s_cc")
        m_cc = cnt["s_cc"]
        emit("gpsimd", lambda g: g.dma_start(out=gsc[:], in_=cc_out),
             waits=[W("s_cc", m_cc)], inc="s_G", incn=16)
        m_g4 = cnt["s_G"]
        emit("sync", lambda e: e.dma_start(
            out=gcol[:], in_=_ap(gsc[0:1, 0:1], 0, [[64, 1], [0, P]]),
            allow_slow_non_contiguous=True),
            waits=[W("s_G", m_g4)], inc="s_x", incn=16)
        m_g5 = cnt["s_x"]
        emit("vector", lambda v: v.reciprocal(sgam[:], gcol[:]),
             waits=[W("s_x", m_g5)], aux=True)
        emit("vector", lambda v: v.tensor_scalar(sgam[:], sgam[:], QB, None,
                                                 ALU.mult), inc="s_V", aux=True)
        m_sg = cnt["s_V"]

        wq_parts = (xk, xv, xr, xbb)

        def wq_part(c):
            return wq_parts[c // 2], (c % 2) * D

        m_z3_last = marks[("z3", NT - 1)]
        for q in range(4):
            wts = ([W("s_V", m_z3_last), W("s_S", m_xn_last),
                    W("s_T", marks[("ssq3", NT - 1)])] if q == 0
                   else [W("s_S", marks[("wabs", q - 1)])])
            emit("gpsimd", lambda g, q=q: g.dma_start(
                out=t1[:], in_=_ap(ffnw_hbm, 2 * q * P * D,
                                   [[D, P], [P * D, 2], [1, D]])),
                waits=wts, inc="s_G", incn=16)
            m_ld = cnt["s_G"]
            swts = [W("s_G", m_ld)]
            if q == 0:
                swts.append(W("s_V", m_mx_final))
            else:
                swts.append(W("s_T", marks[("bmm", q - 1)]))
            emit("scalar", lambda s: s.activation(sqb[:], t1[:], AF.Abs),
                 waits=swts, inc="s_S")
            m_abs = cnt["s_S"]
            marks[("wabs", q)] = m_abs

            def beta_mms(q):
                def f(t):
                    last = None
                    bank = (2, 3, 6, 7)[q]
                    for half in range(4):
                        last = t.matmul(ps[bank][:, 0:512], ones_t[:],
                                        sqb[:, half * 512:(half + 1) * 512],
                                        start=(half == 0), stop=(half == 3))
                    return last
                return f
            emit("tensor", beta_mms(q), waits=[W("s_S", m_abs)], inc="s_T")
            marks[("bmm", q)] = cnt["s_T"]
        m_bmm_all = cnt["s_T"]
        emit("vector", lambda v: v.tensor_copy(bacc[:], ps[2][:, 0:512]),
             waits=[W("s_T", m_bmm_all)])
        for bank in (3, 6, 7):
            emit("vector", lambda v, bank=bank: v.tensor_tensor(
                bacc[:], bacc[:], ps[bank][:, 0:512], ALU.add),
                inc="s_V" if bank == 7 else None)
        marks[("bev", 3)] = cnt["s_V"]
        def betasum_f(s):
            s.nop()
            return s.activation(t2[:, 0:512], bacc[:], AF.Copy,
                                accum_out=t2[:, 1024:1025])
        emit("scalar", betasum_f, waits=[W("s_V", marks[("bev", 3)])], inc="s_S")
        m_bsum_s = cnt["s_S"]
        emit("vector", lambda v: v.tensor_scalar(
            bcol[:], t2[:, 1024:1025], 1.0 / 1024.0, None, ALU.mult),
            waits=[W("s_S", m_bsum_s)], aux=True)
        emit("vector", lambda v: v.tensor_scalar(
            rbcol[:], bcol[:], 1e-5, None, ALU.add), inc="s_V", aux=True)
        m_g6 = cnt["s_V"]
        emit("vector", lambda v: v.reciprocal(rbcol[:], rbcol[:]),
             inc="s_V", aux=True)
        m_rb = cnt["s_V"]

        for q in range(4):
            wts = ([W("s_V", m_rb)] if q == 0
                   else [W("s_V", marks[("wqq", q - 1)])])
            emit("gpsimd", lambda g, q=q: g.dma_start(
                out=t1[:], in_=_ap(ffnw_hbm, 2 * q * P * D,
                                   [[D, P], [P * D, 2], [1, D]])),
                waits=wts, inc="s_G", incn=16)
            m_ld2 = cnt["s_G"]

            def quant(q):
                def f(v):
                    v.nop()
                    v.tensor_scalar(t1[:], t1[:], rbcol[:, 0:1], None, ALU.mult)
                    v.nop()
                    v.tensor_scalar(t2[:], t1[:], 0.5, None, ALU.is_gt)
                    v.nop()
                    v.tensor_scalar(t1[:], t1[:], -0.5, None, ALU.is_lt)
                    v.nop()
                    part, _ = wq_part(2 * q)
                    return v.tensor_tensor(part[:, 0:2 * D], t2[:], t1[:],
                                           ALU.subtract)
                return f
            emit("vector", quant(q), waits=[W("s_G", m_ld2)], inc="s_V")
            marks[("wqq", q)] = cnt["s_V"]
        m_wq = cnt["s_V"]

        emit("vector", lambda v: v.tensor_scalar(
            mxc[:], bcol[:], 1.0 / QB, None, ALU.mult),
            waits=[W("s_V", m_g6), W("s_V", m_sg)], inc="s_V", aux=True)
        m_b128 = cnt["s_V"]

        def gbs1_f(s):
            s.nop()
            return s.activation(rbcol2[:], mxc[:], AF.Identity,
                                scale=gcol[:, 0:1])
        emit("scalar", gbs1_f, waits=[W("s_V", m_b128)], inc="s_S")

        def gbs2_f(s):
            s.nop()
            return s.activation(gbs[:], gbv[:], AF.Identity,
                                scale=rbcol2[:, 0:1])
        emit("scalar", gbs2_f, inc="s_S")
        m_gbs = cnt["s_S"]

        def xnload(ti, wts):
            def f(e):
                return e.dma_start(
                    out=xnB[ti % 2][:],
                    in_=_ap(xn_hbm, ti * F, [[T, P], [P * T, NCH], [1, F]]))
            return f, wts
        f0, w0 = xnload(0, [Wc("s_no"), W("s_V", m_wq)])
        emit("sync", f0, waits=w0, inc="s_x", incn=16)
        base_px = cnt["s_x"]
        f1, w1 = xnload(1, [Wc("s_no"), W("s_V", m_wq)])
        emit("sync", f1, waits=w1, inc="s_x", incn=16)

        for ti in range(NT):
            xnb = xnB[ti % 2]
            term = aab if ti % 2 == 0 else t2
            qb = sqb if ti % 2 == 0 else sqb2
            wts = [Wc("s_x"), W("s_S", m_gbs)]
            if ti >= 2:
                wts.append(W("s_T", marks[("fmm", ti - 2)]))
            emit("vector", lambda v, xnb=xnb: v.tensor_scalar(
                t1[:], xnb[:], sgam[:, 0:1], 8388608.0, ALU.mult, ALU.add),
                waits=wts, aux=True)
            emit("vector", lambda v, qb=qb: v.tensor_scalar(
                qb[:], t1[:], -8388608.0, CLIP, ALU.add, ALU.min))
            emit("vector", lambda v, qb=qb: v.tensor_scalar(
                qb[:], qb[:], -CLIP, None, ALU.max), inc="s_V")
            m_xq = cnt["s_V"]
            marks[("xq", ti)] = m_xq

            def fmm(e, bank):
                def f(t, qb=qb):
                    last = None
                    for c in range(NCH):
                        part, off = wq_part(c)
                        last = t.matmul(ps[bank][:, 0:F],
                                        part[:, off + e * P:off + (e + 1) * P],
                                        ch(qb, c), start=(c == 0),
                                        stop=(c == NCH - 1))
                    return last
                return f
            for e in range(NCH):
                bank = 2 + (e % 2)
                wts = [W("s_V", m_xq)] if e == 0 else []
                if e < 2 and ti >= 1:
                    wts.append(W("s_S", marks[("fev", ti - 1, 6 + e)]))
                if e >= 2:
                    wts.append(W("s_S", marks[("fev", ti, e - 2)]))
                emit("tensor", fmm(e, bank), waits=wts, inc="s_T")
                mme = cnt["s_T"]
                swts = [W("s_T", mme)]
                if ti == 0 and e == 0:
                    swts.append(Wc("s_ao"))
                if ti >= 2 and e == 0:
                    swts.append(Wc("s_fo"))
                def fev_f(s, e=e, bank=bank, term=term):
                    s.nop()
                    return s.activation(ch(term, e), ps[bank][:, 0:F], AF.Identity,
                                        scale=gbs[:, e:e + 1])
                emit("scalar", fev_f, waits=swts, inc="s_S")
                marks[("fev", ti, e)] = cnt["s_S"]
            marks[("fmm", ti)] = cnt["s_T"]
            m_term = cnt["s_S"]

            emit("gpsimd", lambda g, term=term, ti=ti: g.dma_start(
                out=_ap(out_hbm, ti * F, [[T, P], [P * T, NCH], [1, F]]),
                in_=term[:], accum_op=ALU.add),
                waits=[W("s_S", m_term)], inc="s_fo", incn=16)
            marks[("acc", ti)] = cnt["s_fo"]

            if ti + 2 < NT:
                fN, wN = xnload(ti + 2, [Wc("s_no"),
                                         W("s_V", marks[("xq", ti)])])
                emit("sync", fN, waits=wN, inc="s_x", incn=16)

        emit("gpsimd", lambda g: g.wait_ge(sems["s_fo"], cnt["s_fo"]))

        def replay(eng_name):
            def runner(eng):
                for fn, waits, inc, incn, aux in sched[eng_name]:
                    for wt in waits:
                        if wt[0] == "fut":
                            eng.wait_ge(sems[wt[1]], marks[wt[2]])
                        else:
                            sem, val = wt
                            eng.wait_ge(sem, val)
                    if aux:
                        eng.nop()
                    instr = fn(eng)
                    if inc is not None:
                        assert instr is not None, \
                            f"{eng_name} op returned None with inc"
                        instr.then_inc(sems[inc], incn)
            return runner

        block.sync(replay("sync"))
        block.scalar(replay("scalar"))
        block.vector(replay("vector"))
        block.tensor(replay("tensor"))
        block.gpsimd(replay("gpsimd"))

    return nc


def kernel(**inputs):
    x = np.ascontiguousarray(inputs["L0_attn_x"])
    wkT = np.ascontiguousarray(inputs["L0_attn_W_k"].T)
    wvT = np.ascontiguousarray(inputs["L0_attn_W_v"].T)
    wrT = np.ascontiguousarray(inputs["L0_attn_W_r"].T)
    woT = np.ascontiguousarray(inputs["L0_attn_W_o"].T)
    ffnw = np.ascontiguousarray(inputs["L0_ffn_weight"])
    wmod = np.ascontiguousarray(inputs["L0_wmod"])
    crow = inputs["L0_c"]

    def vec8(v):
        return np.ascontiguousarray(np.reshape(np.asarray(v), (NCH, P)).T)

    nc = build_nc()
    in_maps = []
    for b in range(B):
        vecs = np.zeros((P, 64), np.float32)
        vecs[:, 0:8] = vec8(crow[b])
        vecs[:, 8:16] = vec8(inputs["L0_attn_time_mix_k"])
        vecs[:, 16:24] = vec8(inputs["L0_attn_time_mix_v"])
        vecs[:, 24:32] = vec8(inputs["L0_attn_time_mix_r"])
        vecs[:, 32:40] = vec8(inputs["L0_attn_time_decay"])
        vecs[:, 40:48] = vec8(inputs["L0_attn_time_first"])
        vecs[:, 48:56] = vec8(inputs["L0_ffn_ln_gamma"])
        vecs[:, 56:64] = vec8(inputs["L0_ffn_ln_beta"])
        in_maps.append({
            "x": np.ascontiguousarray(x[b].T),
            "wkT": wkT, "wvT": wvT, "wrT": wrT, "woT": woT,
            "ffnw": ffnw, "wmod": wmod, "vecs": vecs,
            "ident": np.eye(P, dtype=np.float32),
        })
    res = run_bass_kernel_spmd(nc, in_maps, core_ids=list(range(B)))
    out = np.stack([np.ascontiguousarray(r["out"].T) for r in res.results])
    return out
